# version 04
# baseline (speedup 1.0000x reference)
import numpy as np
import ml_dtypes
import concourse.bass as bass
import concourse.mybir as mybir
from concourse.bass_utils import run_bass_kernel_spmd

F32 = mybir.dt.float32
BF16 = mybir.dt.bfloat16
AF = mybir.ActivationFunctionType
ALU = mybir.AluOpType
AX = mybir.AxisListType

NCORES = 8
T = 2048
HALO = 256
TH = T + 2 * HALO
D = 1024
DC = 8
H = 16
DH = 64
NG = T // 256
S = 8192
DFF = 4096
NQ = 4
NTT = 4
TT = 512
E = 8
MDFF = 2048
LN_EPS = 1e-5

SBUF_BYTES = 229376 - 64
SBUF_BASE = 16512

OFF_Y1 = SBUF_BASE
OFF_VEC = SBUF_BASE + 65536
PHASE = SBUF_BASE + 69632

OFF_SRCB = PHASE
OFF_VTOK = OFF_SRCB + 40960
OFF_BMASK = OFF_VTOK + 41600
OFF_EM0 = OFF_BMASK + 2048
OFF_EM1 = OFF_EM0 + 3072
OFF_ESB = OFF_EM1 + 3072
OFF_PVS = OFF_ESB + 9216
OFF_RDEN = OFF_PVS + 2048
OFF_PVN = OFF_RDEN + 1024
OFF_W1QA = SBUF_BYTES - 16384

OFF_ROWA = PHASE
OFF_ROWB = OFF_ROWA + 8192
OFF_ROWT = OFF_ROWB + 8192
OFF_UBF = PHASE + 24576
OFF_HQ = OFF_UBF + 32768
OFF_W1QB = OFF_HQ + 32768
OFF_W2S = OFF_W1QB + 16384
OFF_SRC2 = OFF_UBF

OFF_X1BF = OFF_UBF
OFF_ABF = OFF_HQ
OFF_W1E = OFF_HQ + 32768
OFF_CSB = PHASE
OFF_CB = PHASE + 4096
OFF_SEL = PHASE + 12288
OFF_SG = PHASE + 14336
OFF_W2E = PHASE + 16384
OFF_ROW2A = OFF_W1E
OFF_ROW2B = OFF_W1E + 8192
OFF_ROW2T = OFF_W1E + 16384
OFF_LG = OFF_W1E + 24576
OFF_OSTG = OFF_X1BF

VEC = {}
_vo = OFF_VEC
def _vecslot(name, nbytes, align=32):
    global _vo
    _vo = (_vo + align - 1) // align * align
    VEC[name] = _vo
    _vo += nbytes
_vecslot("b1T", 32 * 4)
_vecslot("b2T", 8 * 4)
_vecslot("g1T", 8 * 4)
_vecslot("b1lT", 8 * 4)
_vecslot("g2T", 8 * 4)
_vecslot("b2lT", 8 * 4)
_vecslot("bg", 4)
_vecslot("negM", 4)
_vecslot("mrow", 4)
_vecslot("wg", 64 * 4)
_vecslot("wghi", 64 * 2)
_vecslot("wglo", 64 * 2)
_vecslot("ident", 128 * 2)
_vecslot("ones_bf", 2)
_vecslot("onesr_bf", 64 * 2)
_vecslot("lt", 16 * 8 * 4)
_vecslot("ct", 16 * 8 * 4)
_vecslot("srt", 8 * 4)
_vecslot("gs1", 4)
_vecslot("gs2", 4)
_vecslot("gs3", 8 * 4)
_vecslot("gs4", 8 * 4)
_vecslot("onesf", 128 * 4)
_vecslot("identf", 128 * 4)
_vecslot("eps1", 4)
_vecslot("cz", 4)
_vecslot("co", 4)
assert _vo <= PHASE, _vo
assert PHASE + 24576 + 32768 + 32768 + 16384 + 4096 <= OFF_W1QA


class Ctr:
    def __init__(self, nc, name):
        self.h = nc.alloc_semaphore(name)
        self.n = 0

    def inc(self, instr, k=1):
        instr.then_inc(self.h, k)
        self.n += k
        return self.n


class Builder:
    def __init__(self, stage="full"):
        self.stage = stage
        nc = self.nc = bass.Bass("TRN2", target_bir_lowering=False)
        assert nc.SBUF_PARTITION_SIZE_BYTES >= SBUF_BYTES
        assert nc.sbuf_base <= SBUF_BASE, nc.sbuf_base

        def dram(name, shape, kind="ExternalInput", dt=F32):
            return nc.dram_tensor(name, shape, dt, kind=kind).ap()

        self.d_srcT = dram("srcT", [D, TH])
        self.d_w1 = dram("w1", [D, DFF])
        self.d_w2 = dram("w2", [DFF, D])
        self.d_b1T = dram("b1T", [128, 32])
        self.d_vec6 = dram("vec6", [5, 128, 8])
        self.d_bg = dram("bg", [8, 1])
        self.d_wgT = dram("wgT", [D, 8])
        self.d_mw1 = dram("mw1", [E, D, MDFF])
        self.d_mw2 = dram("mw2", [E, D, D])
        self.d_bmask = dram("bmask", [4, 128, 256], dt=BF16)
        self.d_em0 = dram("em0", [6, 128, 256], dt=BF16)
        self.d_em1 = dram("em1", [6, 128, 256], dt=BF16)
        self.d_ident = dram("identm", [128, 128], dt=BF16)
        self.d_sel8 = dram("sel8", [8, 1024], dt=BF16)
        self.d_out = dram("outT", [D, T], kind="ExternalOutput")

        sb = lambda name, shape, dt, off: nc.alloc_sbuf_tensor_at(name, shape, dt, offset=off).ap()
        self.Y1 = sb("y1", [128, DC, T], F32, OFF_Y1)
        self.SRCB = sb("srcb", [128, DC, TH], BF16, OFF_SRCB)
        self.VTOK = sb("vtok", [128, 20, H, 65], BF16, OFF_VTOK)
        self.BMASK = sb("bmask_s", [128, 4, 256], BF16, OFF_BMASK)
        self.EM0 = sb("em0_s", [128, 6, 256], BF16, OFF_EM0)
        self.EM1 = sb("em1_s", [128, 6, 256], BF16, OFF_EM1)
        self.ESB = sb("esb", [128, 3, 1536], BF16, OFF_ESB)
        self.PVS = sb("pvs", [65, 2, 256], F32, OFF_PVS)
        self.RDEN = sb("rden", [128, 2, 256], BF16, OFF_RDEN)
        self.PVN = sb("pvn", [64, 2, 256], BF16, OFF_PVN)
        self.W1QA = sb("w1qa", [128, DC, 1024], BF16, OFF_W1QA)
        self.W1QB = sb("w1qb", [128, DC, 1024], BF16, OFF_W1QB)
        self.W2S = sb("w2s", [128, 2, DC, 128], BF16, OFF_W2S)
        self.ROWA = sb("rowa", [1, T], F32, OFF_ROWA)
        self.ROWB = sb("rowb", [1, T], F32, OFF_ROWB)
        self.ROWT = sb("rowt", [1, T], F32, OFF_ROWT)
        self.UBF = sb("ubf", [128, DC, T], BF16, OFF_UBF)
        self.HQ = sb("hq", [128, DC, T], BF16, OFF_HQ)
        self.SRC2 = sb("src2", [128, DC, T], F32, OFF_SRC2)
        self.X1BF = sb("x1bf", [128, DC, T], BF16, OFF_X1BF)
        self.LOBF = sb("lobf", [128, DC, T], BF16, OFF_ABF)
        self.ABF = sb("abf", [128, DC, T], BF16, OFF_ABF)
        self.W1EB = sb("w1e", [128, DC, MDFF], BF16, OFF_W1E)
        self.CSB = sb("csb", [8, T], BF16, OFF_CSB)
        self.SEL8 = sb("sel8_s", [8, 1024], BF16, OFF_SEL)
        self.CBC = sb("cb", [128, T], BF16, OFF_CB)
        self.SGB = sb("sgb", [128, 2, TT], BF16, OFF_SG)
        self.W2E = sb("w2e", [128, 2, DC, 128], BF16, OFF_W2E)
        self.ROW2A = sb("row2a", [1, T], F32, OFF_ROW2A)
        self.ROW2B = sb("row2b", [1, T], F32, OFF_ROW2B)
        self.ROW2T = sb("row2t", [1, T], F32, OFF_ROW2T)
        self.LGS = sb("lg", [8, T], F32, OFF_LG)
        self.OSTG = sb("ostg", [128, 2, T], F32, OFF_OSTG)

        self.B1T = sb("b1t_s", [128, 32], F32, VEC["b1T"])
        self.B2T = sb("b2t_s", [128, 8], F32, VEC["b2T"])
        self.G1T = sb("g1t_s", [128, 8], F32, VEC["g1T"])
        self.B1LT = sb("b1lt_s", [128, 8], F32, VEC["b1lT"])
        self.G2T = sb("g2t_s", [128, 8], F32, VEC["g2T"])
        self.B2LT = sb("b2lt_s", [128, 8], F32, VEC["b2lT"])
        self.BG = sb("bg_s", [8, 1], F32, VEC["bg"])
        self.NEGM = sb("negm_s", [128, 1], F32, VEC["negM"])
        self.MROW = sb("mrow_s", [128, 1], F32, VEC["mrow"])
        self.WG = sb("wg_s", [128, DC, 8], F32, VEC["wg"])
        self.WGHI = sb("wghi_s", [128, DC, 8], BF16, VEC["wghi"])
        self.WGLO = sb("wglo_s", [128, DC, 8], BF16, VEC["wglo"])
        self.IDENT = sb("ident_s", [128, 128], BF16, VEC["ident"])
        self.ONES = sb("ones_s", [128, 1], BF16, VEC["ones_bf"])
        self.ONESR = sb("onesr_s", [128, 64], BF16, VEC["onesr_bf"])
        self.LT = sb("lt_s", [128, 16, 8], F32, VEC["lt"])
        self.CT = sb("ct_s", [128, 16, 8], F32, VEC["ct"])
        self.SRT = sb("srt_s", [128, 8], F32, VEC["srt"])
        self.GS1 = sb("gs1_s", [128, 1], F32, VEC["gs1"])
        self.GS2 = sb("gs2_s", [128, 1], F32, VEC["gs2"])
        self.GS3 = sb("gs3_s", [128, 8], F32, VEC["gs3"])
        self.GS4 = sb("gs4_s", [128, 8], F32, VEC["gs4"])
        self.ONESF = sb("onesf_s", [1, 128], F32, VEC["onesf"])
        self.IDENTF = sb("identf_s", [128, 128], F32, VEC["identf"])
        self.STB = sb("stb_s", [128, 2, 512], BF16, OFF_W2S)
        self.STB2 = sb("stb2_s", [128, 2, 512], BF16, OFF_W2S + 2048)
        self.EPS1 = sb("eps1_s", [1, 1], F32, VEC["eps1"])
        self.CZ = sb("cz_s", [128, 1], F32, VEC["cz"])
        self.CO = sb("co_s", [128, 1], F32, VEC["co"])
        nc.const_aps.register(F32, 0.0, self.CZ)
        nc.const_aps.register(F32, 1.0, self.CO)

        self.PS = nc.alloc_psum_tensor("ps", [128, 8, 512], F32).ap()
        self.PSB = self.PS.bitcast(BF16)

        self.sT = Ctr(nc, "c_st")
        self.sV = Ctr(nc, "c_sv")
        self.sS = Ctr(nc, "c_ss")
        self.sP = Ctr(nc, "c_sp")
        self.sO = Ctr(nc, "c_so")
        self.cY1 = Ctr(nc, "c_y1")
        self.cCST = Ctr(nc, "c_cst")
        self.cSRCB = Ctr(nc, "c_srcb")
        self.cW1A = Ctr(nc, "c_w1a")
        self.cW1B = Ctr(nc, "c_w1b")
        self.cW2S = [Ctr(nc, "c_w2s0"), Ctr(nc, "c_w2s1")]
        self.cSRC2 = Ctr(nc, "c_src2")
        self.cW1E = Ctr(nc, "c_w1e")
        self.cSEL = Ctr(nc, "c_sel")
        self.cOUT = [Ctr(nc, "c_out0"), Ctr(nc, "c_out1")]
        self.cW2E = [Ctr(nc, "c_w2e0"), Ctr(nc, "c_w2e1")]

        self.build()

    def barrier(self):
        self.nc.all_engine_barrier()

    def build(self):
        nc = self.nc
        self.prologue()
        self.attn()
        if self.stage == "attn":
            self.emit_out(self.Y1)
            return
        self.lna_ffn()
        if self.stage == "ffn":
            self.emit_out(self.Y1)
            return
        self.lnb_lnc_gating()
        self.moe()
        self.lnd_out()

    def emit_out(self, plane):
        nc, sy = self.nc, self.nc.sync
        self.barrier()
        outr = self.d_out.rearrange("(c p) n -> p c n", p=128)
        for dc in range(DC):
            self.sO.inc(sy.dma_start(out=outr[:, dc, :], in_=plane[:, dc, :]), 16)
        sy.wait_ge(self.sO.h, self.sO.n)

    def prologue(self):
        nc = self.nc
        te, ve, se, sy, ge = nc.tensor, nc.vector, nc.scalar, nc.sync, nc.gpsimd
        src_r = self.d_srcT.rearrange("(c p) n -> p c n", p=128)
        self.cY1.inc(sy.dma_start(out=self.Y1, in_=src_r[:, :, HALO:HALO + T]), 16)
        self.cCST.inc(sy.dma_start(out=self.BMASK, in_=self.d_bmask.rearrange("c p n -> p c n")), 16)
        self.cCST.inc(sy.dma_start(out=self.EM0, in_=self.d_em0.rearrange("c p n -> p c n")), 16)
        self.cCST.inc(sy.dma_start(out=self.EM1, in_=self.d_em1.rearrange("c p n -> p c n")), 16)
        self.cCST.inc(sy.dma_start(out=self.IDENT, in_=self.d_ident), 16)
        self.cCST.inc(sy.dma_start(out=self.B1T, in_=self.d_b1T), 16)
        for i, t in enumerate([self.B2T, self.G1T, self.B1LT, self.G2T, self.B2LT]):
            self.cCST.inc(sy.dma_start(out=t, in_=self.d_vec6[i]), 16)
        self.cCST.inc(sy.dma_start(out=self.BG, in_=self.d_bg), 16)
        self.cCST.inc(sy.dma_start(out=self.WG, in_=self.d_wgT.rearrange("(c p) e -> p c e", p=128)), 16)
        self.cSRCB.inc(ge.dma_start(out=self.SRCB, in_=src_r), 16)
        self.w1_r = self.d_w1.rearrange("(c p) (q n) -> q p c n", p=128, n=1024)
        self.cW1A.inc(ge.dma_start(out=self.W1QA, in_=self.w1_r[0]), 16)

        ve.memset(self.ONES, 1.0)
        ve.memset(self.ONESR[64:65, :], 1.0)
        ve.memset(self.ONESF, 1.0)
        ve.memset(self.EPS1, float(LN_EPS))
        ve.memzero(self.CZ)
        ve.memset(self.CO, 1.0)
        ve.wait_ge(self.cCST.h, self.cCST.n)
        ve.tensor_copy(self.IDENTF, self.IDENT)
        self.sV.inc(ve.memset(self.VTOK[:, :, :, 64:65], 1.0))
        self.v_ones = self.sV.n

    def attn(self):
        nc = self.nc
        te, ve, se, ge = nc.tensor, nc.vector, nc.scalar, nc.gpsimd
        PS, SRCB, ESB, VTOK, PVS, RDEN, Y1 = (self.PS, self.SRCB, self.ESB, self.VTOK,
                                              self.PVS, self.RDEN, self.Y1)
        sT, sV, sS, sP = self.sT, self.sV, self.sS, self.sP

        se.preload_act_table(AF.Exp)

        te.wait_ge(self.cSRCB.h, self.cSRCB.n)
        te.wait_ge(self.cCST.h, self.cCST.n)
        ve.wait_ge(self.cCST.h, self.cCST.n)
        sT.inc(te.matmul(PS[:, 0, 0:320], SRCB[0:64, 0, 256:384],
                         SRCB[0:64, 0, 0:2560:8], start=True, stop=True))
        ve.wait_ge(sT.h, sT.n)
        sV.inc(ve.tensor_reduce(self.MROW, PS[:, 0, 0:320], AX.X, ALU.max))
        te.wait_ge(sV.h, sV.n)
        sT.inc(te.matmul(PS[0:1, 1, 0:128], self.MROW, self.IDENTF,
                         is_transpose=True, start=True, stop=True))
        ve.wait_ge(sT.h, sT.n)
        sV.inc(ve.tensor_reduce(self.MROW[0:1, :], PS[0:1, 1, 0:128], AX.X, ALU.max))
        te.wait_ge(sV.h, sV.n)
        sT.inc(te.matmul(PS[:, 1, 0:1], self.ONESF, self.MROW[0:1, :],
                         start=True, stop=True))
        ve.wait_ge(sT.h, sT.n)
        sV.inc(ve.tensor_scalar(self.NEGM, PS[:, 1, 0:1], -1.0, -8.0, ALU.mult, ALU.add))
        v_negm = sV.n
        se.wait_ge(sV.h, v_negm)
        te.wait_ge(sV.h, v_negm)

        NI = NG * H
        t_qk = [0] * NI
        t_pv = [0] * NI
        t_bc = [0] * NI
        t_sh = [0] * NI
        s_exp = [0] * NI
        s_cp = [0] * NI
        v_msk = [0] * NI
        v_rec = [0] * NI
        v_mul = [0] * NI
        v_acc = [0] * NI
        built = set()

        ve.wait_ge(self.cY1.h, self.cY1.n)
        ve.wait_ge(sV.h, self.v_ones)

        def build_vhat(kcs):
            for kc in kcs:
                if kc in built:
                    continue
                built.add(kc)
                for wave in range(2):
                    for k in range(4):
                        dc = 4 * wave + k
                        sT.inc(te.matmul(self.PSB[:, 0, 128 * k:128 * (k + 1)],
                                         SRCB[:, dc, 128 * kc:128 * (kc + 1)],
                                         self.IDENT, is_transpose=True,
                                         start=(k == 0), stop=(k == 3)))
                    ve.wait_ge(sT.h, sT.n)
                    src_ap = self.PSB[:, 0, 0:512] \
                        .rearrange("p (a b) -> p a b", a=4) \
                        .rearrange("p a (c d) -> p (a c) d", c=2)
                    sV.inc(ve.tensor_copy(VTOK[:, kc, 8 * wave:8 * wave + 8, 0:64], src_ap))
                    te.wait_ge(sV.h, sV.n)

        def emit_qk(i):
            gg, h = divmod(i, H)
            dc, r0 = h // 2, 64 * (h % 2)
            par = i % 2
            if i >= 2:
                te.wait_ge(sS.h, s_exp[i - 2])
            mm = None
            for b in range(6):
                kc = 2 * gg + b
                mm = te.matmul(PS[:, 3 * par + b // 2, 256 * (b % 2):256 * (b % 2) + 256],
                               SRCB[r0:r0 + 64, dc, 128 * kc:128 * kc + 128],
                               SRCB[r0:r0 + 64, dc, 256 + 256 * gg:512 + 256 * gg],
                               start=(b % 2 == 0), stop=(b % 2 == 1))
            sT.inc(mm)
            t_qk[i] = sT.n

        def emit_exp(i):
            par3 = i % 3
            se.wait_ge(sT.h, t_qk[i])
            if i >= 3:
                se.wait_ge(sT.h, t_pv[i - 3])
            sS.inc(se.activation(ESB[:, par3, :], PS[:, 3 * (i % 2):3 * (i % 2) + 3, :],
                                 AF.Exp, bias=self.NEGM))
            s_exp[i] = sS.n

        def emit_mask(i):
            gg = i // H
            par3 = i % 3
            ve.wait_ge(sS.h, s_exp[i])
            if gg == 0:
                pairs = [(self.EM0, 0, 0), (self.EM0, 2, 2), (self.EM0, 4, 4)]
            elif gg == NG - 1:
                pairs = [(self.EM1, 0, 0), (self.EM1, 2, 2), (self.EM1, 4, 4)]
            else:
                pairs = [(self.BMASK, 0, 0), (self.BMASK, 4, 2)]
            mm = None
            for (m, ec, mc) in pairs:
                mm = ve.tensor_mul(ESB[:, par3, 256 * ec:256 * ec + 512],
                                   ESB[:, par3, 256 * ec:256 * ec + 512],
                                   m[:, mc:mc + 2, :])
            sV.inc(mm)
            v_msk[i] = sV.n

        def emit_pv(i):
            gg, h = divmod(i, H)
            par = i % 2
            par3 = i % 3
            te.wait_ge(sV.h, v_msk[i])
            if i >= 2:
                te.wait_ge(sV.h, v_acc[i - 2])
            mm = None
            for b in range(6):
                kc = 2 * gg + b
                mm = te.matmul(PS[0:65, 6 + par, 0:256],
                               VTOK[:, kc, h, :],
                               ESB[:, par3, 256 * b:256 * b + 256],
                               start=(b == 0), stop=(b == 5))
            sT.inc(mm)
            t_pv[i] = sT.n

        def emit_copy(i):
            par = i % 2
            se.wait_ge(sT.h, t_pv[i])
            sS.inc(se.copy(PVS[:, par, :], PS[0:65, 6 + par, 0:256]))
            s_cp[i] = sS.n

        def emit_recip(i):
            par = i % 2
            ve.wait_ge(sS.h, s_cp[i])
            with nc.allow_low_precision(reason="softmax denom reciprocal feeds a bf16 rank-1 broadcast matmul"):
                sV.inc(ve.reciprocal(RDEN[64:65, par, :], PVS[64:65, par, :]))
            v_rec[i] = sV.n

        def emit_bcast(i):
            par = i % 2
            te.wait_ge(sV.h, v_rec[i])
            sT.inc(te.matmul(PS[0:64, 6 + par, 256:512], self.ONESR[64:65, :],
                             RDEN[64:65, par, :], start=True, stop=True))
            t_bc[i] = sT.n

        def emit_mult(i):
            par = i % 2
            ve.wait_ge(sT.h, t_bc[i])
            sV.inc(ve.tensor_mul(self.PVN[:, par, :], PVS[0:64, par, :],
                                 PS[0:64, 6 + par, 256:512]))
            v_mul[i] = sV.n

        def emit_shift(i):
            par = i % 2
            te.wait_ge(sV.h, v_mul[i])
            sT.inc(te.matmul(PS[64:128, 6 + par, 256:512], self.IDENT[0:64, 0:64],
                             self.PVN[:, par, :], start=True, stop=True))
            t_sh[i] = sT.n

        def emit_add(i):
            gg, h = divmod(i, H)
            dc, r0 = h // 2, 64 * (h % 2)
            par = i % 2
            y = Y1[r0:r0 + 64, dc, 256 * gg:256 * gg + 256]
            if h % 2 == 0:
                ve.wait_ge(sV.h, v_mul[i])
                sV.inc(ve.tensor_add(y, y, self.PVN[:, par, :]))
            else:
                ve.wait_ge(sT.h, t_sh[i])
                sV.inc(ve.tensor_add(y, y, PS[64:128, 6 + par, 256:512]))
            v_acc[i] = sV.n

        for gg in range(NG):
            kcs = list(range(6)) if gg == 0 else [2 * gg + 4, 2 * gg + 5]
            i0 = gg * H
            if i0 >= 2:
                te.wait_ge(sS.h, s_exp[i0 - 1])
            build_vhat(kcs)
            for h in range(H):
                i = i0 + h
                emit_qk(i)
                if i >= 2:
                    emit_pv(i - 2)
                if i >= 3 and (i - 3) % 2 == 1:
                    emit_shift(i - 3)
                    emit_add(i - 3)
                emit_exp(i)
                emit_mask(i)
                if i >= 2:
                    emit_copy(i - 2)
                    emit_recip(i - 2)
                    emit_bcast(i - 2)
                    emit_mult(i - 2)
                    if (i - 2) % 2 == 0:
                        emit_add(i - 2)
        emit_shift(NI - 3)
        emit_add(NI - 3)
        for i in (NI - 2, NI - 1):
            emit_pv(i)
            emit_copy(i)
            emit_recip(i)
            emit_bcast(i)
            emit_mult(i)
            if i % 2 == 0:
                emit_add(i)
            else:
                emit_shift(i)
                emit_add(i)

    def ln_stats(self, plane):
        nc = self.nc
        te, ve = nc.tensor, nc.vector
        sT, sV = self.sT, self.sV
        last_mm = [0, 0]
        for dc in range(DC):
            for tt in range(NTT):
                k = (dc * NTT + tt) % 2
                ve.wait_ge(sT.h, last_mm[k])
                sV.inc(ve.tensor_copy(self.STB[:, k, :], plane[:, dc, TT * tt:TT * (tt + 1)]))
                ve.wait_ge(sV.h, sV.n)
                sV.inc(ve.tensor_mul(self.STB2[:, k, :], self.STB[:, k, :], self.STB[:, k, :]))
                te.wait_ge(sV.h, sV.n)
                te.matmul(self.PS[0:1, tt, :], self.ONES, self.STB[:, k, :],
                          start=(dc == 0), stop=(dc == DC - 1))
                sT.inc(te.matmul(self.PS[0:1, 4 + tt, :], self.ONES, self.STB2[:, k, :],
                                 start=(dc == 0), stop=(dc == DC - 1)))
                last_mm[k] = sT.n
        return sT.n

    def ln_rows(self, rowA, rowB, rowT):
        nc = self.nc
        ve, se = nc.vector, nc.scalar
        sV, sS = self.sV, self.sS
        sumv = self.PS[0:1, 0:4, :]
        ssv = self.PS[0:1, 4:8, :]
        ve.wait_ge(self.sT.h, self.sT.n)
        sV.inc(ve.tensor_scalar_mul(rowT, sumv, 1.0 / D))
        ve.wait_ge(sV.h, sV.n)
        sV.inc(ve.tensor_mul(rowB, rowT, rowT))
        ve.wait_ge(sV.h, sV.n)
        sV.inc(ve.scalar_tensor_tensor(rowA, ssv, 1.0 / D, rowB, ALU.mult, ALU.subtract))
        se.wait_ge(sV.h, sV.n)
        sS.inc(se.activation(rowB, rowA, AF.Sqrt, bias=self.EPS1))
        ve.wait_ge(sS.h, sS.n)
        sV.inc(ve.reciprocal(rowA, rowB))
        ve.wait_ge(sV.h, sV.n)
        return sV.n

    def ln_bcast(self, rowT, rowA, v_rows):
        te = self.nc.tensor
        te.wait_ge(self.sV.h, v_rows)
        for tt in range(NTT):
            te.matmul(self.PS[:, tt, :], self.ONESF, rowT[0:1, TT * tt:TT * (tt + 1)],
                      start=True, stop=True)
            self.sT.inc(te.matmul(self.PS[:, 4 + tt, :], self.ONESF,
                                  rowA[0:1, TT * tt:TT * (tt + 1)], start=True, stop=True))
        return self.sT.n

    def lna_ffn(self):
        nc = self.nc
        te, ve, se, sy, ge = nc.tensor, nc.vector, nc.scalar, nc.sync, nc.gpsimd
        PS, Y1, UBF, HQ = self.PS, self.Y1, self.UBF, self.HQ
        sT, sV, sS = self.sT, self.sV, self.sS

        self.barrier()
        se.preload_act_table(AF.Sqrt)
        t_stats_a = self.ln_stats(Y1)
        v_rows = self.ln_rows(self.ROWA, self.ROWB, self.ROWT)
        t_bc = self.ln_bcast(self.ROWT, self.ROWA, v_rows)
        v_apply = [0] * DC
        s_cast = [0] * DC
        for dc in range(DC):
            ve.wait_ge(sT.h, t_bc)
            sV.inc(ve.tensor_sub(Y1[:, dc, :], Y1[:, dc, :], PS[:, 0:4, :]))
            ve.wait_ge(sV.h, sV.n)
            sV.inc(ve.tensor_mul(Y1[:, dc, :], Y1[:, dc, :], PS[:, 4:8, :]))
            t1v = sV.n
            se.wait_ge(sV.h, t1v)
            sS.inc(se.activation(UBF[:, dc, :], Y1[:, dc, :], AF.Identity,
                                 scale=self.G1T[:, dc:dc + 1], bias=self.B1LT[:, dc:dc + 1]))
            s_cast[dc] = sS.n
            ve.wait_ge(sS.h, s_cast[dc])
            sV.inc(ve.tensor_scalar(Y1[:, dc, :], Y1[:, dc, :],
                                    self.G1T[:, dc:dc + 1], self.B1LT[:, dc:dc + 1],
                                    ALU.mult, ALU.add))
            v_apply[dc] = sV.n

        te.wait_ge(sS.h, max(s_cast))
        te.wait_ge(sV.h, max(v_apply))
        w1buf = [self.W1QA, self.W1QB]
        w1sem = [self.cW1A, self.cW1B]
        w2_r = self.d_w2.rearrange("(a p) n -> a p n", p=128)
        v_y2 = [[0] * NTT for _ in range(DC)]
        s_hq = [0] * DC
        t_a = {}
        t_b = {}
        v_bev = [0, 0]
        t_w2read = [0, 0]
        for q in range(NQ):
            wb = w1buf[q % 2]
            wsem = w1sem[q % 2]
            wcnt = 16 * (q // 2 + 1)
            if q + 1 < NQ:
                if q >= 1:
                    ge.wait_ge(sT.h, self.t_a_end)
                w1sem[(q + 1) % 2].inc(ge.dma_start(out=w1buf[(q + 1) % 2],
                                                    in_=self.w1_r[q + 1]), 16)
            if q == 0:
                ge.wait_ge(sT.h, t_stats_a)
                for m0 in range(2):
                    self.cW2S[m0].inc(ge.dma_start(
                        out=self.W2S[:, m0, :, :],
                        in_=self.d_w2[1024 * q:1024 * (q + 1), 128 * m0:128 * (m0 + 1)]
                            .rearrange("(a p) n -> p a n", p=128)), 16)
            te.wait_ge(wsem.h, wcnt)
            if q > 0:
                te.wait_ge(sV.h, v_bev[0])
                te.wait_ge(sV.h, v_bev[1])
            for hc in range(DC):
                bk = (hc % 2) * 4
                if hc >= 2:
                    te.wait_ge(sS.h, s_hq[hc - 2])
                mm = None
                for dc in range(DC):
                    for tt in range(NTT):
                        mm = te.matmul(PS[:, bk + tt, :], wb[:, dc, 128 * hc:128 * (hc + 1)],
                                       UBF[:, dc, TT * tt:TT * (tt + 1)],
                                       start=(dc == 0), stop=(dc == DC - 1))
                sT.inc(mm)
                t_a[hc] = sT.n
                se.wait_ge(sT.h, t_a[hc])
                a = None
                for tt in range(NTT):
                    a = se.activation(HQ[:, hc, TT * tt:TT * (tt + 1)], PS[:, bk + tt, :],
                                      AF.Relu, bias=self.B1T[:, 8 * q + hc:8 * q + hc + 1])
                sS.inc(a)
                s_hq[hc] = sS.n
            self.t_a_end = t_a[DC - 1]
            for m in range(DC):
                bk = (m % 2) * 4
                te.wait_ge(self.cW2S[m % 2].h, self.cW2S[m % 2].n)
                te.wait_ge(sS.h, s_hq[DC - 1])
                if m >= 2:
                    te.wait_ge(sV.h, v_bev[m % 2])
                mm = None
                for hc in range(DC):
                    for tt in range(NTT):
                        mm = te.matmul(PS[:, bk + tt, :], self.W2S[:, m % 2, hc, :],
                                       HQ[:, hc, TT * tt:TT * (tt + 1)],
                                       start=(hc == 0), stop=(hc == DC - 1))
                sT.inc(mm)
                t_b[m] = sT.n
                t_w2read[m % 2] = sT.n
                nq, nm = (q, m + 2) if m + 2 < DC else (q + 1, m + 2 - DC)
                if nq < NQ:
                    ge.wait_ge(sT.h, t_b[m])
                    self.cW2S[m % 2].inc(ge.dma_start(
                        out=self.W2S[:, m % 2, :, :],
                        in_=self.d_w2[1024 * nq:1024 * (nq + 1), 128 * nm:128 * (nm + 1)]
                            .rearrange("(a p) n -> p a n", p=128)), 16)
                ve.wait_ge(sT.h, t_b[m])
                ev = None
                for tt in range(NTT):
                    ve.wait_ge(sV.h, v_y2[m][tt])
                    if q == 0:
                        ev = ve.scalar_tensor_tensor(Y1[:, m, TT * tt:TT * (tt + 1)],
                                                     PS[:, bk + tt, :],
                                                     self.B2T[:, m:m + 1],
                                                     Y1[:, m, TT * tt:TT * (tt + 1)],
                                                     ALU.add, ALU.add)
                    else:
                        ev = ve.tensor_add(Y1[:, m, TT * tt:TT * (tt + 1)],
                                           Y1[:, m, TT * tt:TT * (tt + 1)],
                                           PS[:, bk + tt, :])
                    sV.inc(ev)
                    v_y2[m][tt] = sV.n
                v_bev[m % 2] = sV.n

    def lnb_lnc_gating(self):
        nc = self.nc
        te, ve, se, sy, ge = nc.tensor, nc.vector, nc.scalar, nc.sync, nc.gpsimd
        PS, Y1, SRC2 = self.PS, self.Y1, self.SRC2
        sT, sV, sS = self.sT, self.sV, self.sS

        self.barrier()
        t_stats = self.ln_stats(Y1)
        sy.wait_ge(sT.h, t_stats)
        src_r = self.d_srcT.rearrange("(c p) n -> p c n", p=128)
        self.cSRC2.inc(sy.dma_start(out=SRC2, in_=src_r[:, :, HALO:HALO + T]), 16)
        v_rows = self.ln_rows(self.ROWA, self.ROWB, self.ROWT)
        t_bc = self.ln_bcast(self.ROWT, self.ROWA, v_rows)
        se.wait_ge(self.cSRC2.h, self.cSRC2.n)
        s_bias = [0] * DC
        for dc in range(DC):
            sS.inc(se.add(SRC2[:, dc, :], SRC2[:, dc, :], self.B2LT[:, dc:dc + 1]))
            s_bias[dc] = sS.n
        v_y3 = [0] * DC
        for dc in range(DC):
            ve.wait_ge(sT.h, t_bc)
            sV.inc(ve.tensor_sub(Y1[:, dc, :], Y1[:, dc, :], PS[:, 0:4, :]))
            ve.wait_ge(sV.h, sV.n)
            sV.inc(ve.tensor_mul(Y1[:, dc, :], Y1[:, dc, :], PS[:, 4:8, :]))
            ve.wait_ge(sV.h, sV.n)
            ve.wait_ge(sS.h, s_bias[dc])
            sV.inc(ve.scalar_tensor_tensor(SRC2[:, dc, :], Y1[:, dc, :],
                                           self.G2T[:, dc:dc + 1], SRC2[:, dc, :],
                                           ALU.mult, ALU.add))
            v_y3[dc] = sV.n

        self.barrier()
        self.ln_stats(SRC2)
        v_rows = self.ln_rows(self.ROW2A, self.ROW2B, self.ROW2T)
        t_bc = self.ln_bcast(self.ROW2T, self.ROW2A, v_rows)
        s_x1bf = [0] * DC
        for dc in range(DC):
            ve.wait_ge(sT.h, t_bc)
            sV.inc(ve.tensor_sub(Y1[:, dc, :], SRC2[:, dc, :], PS[:, 0:4, :]))
            ve.wait_ge(sV.h, sV.n)
            sV.inc(ve.tensor_mul(Y1[:, dc, :], Y1[:, dc, :], PS[:, 4:8, :]))
            se.wait_ge(sV.h, sV.n)
            sS.inc(se.copy(self.X1BF[:, dc, :], Y1[:, dc, :]))
            s_x1bf[dc] = sS.n

        self.barrier()
        self.cSEL.inc(sy.dma_start(out=self.SEL8, in_=self.d_sel8), 16)
        se.preload_act_table(AF.Exp)
        sV.inc(ve.tensor_copy(self.WGHI, self.WG))
        ve.wait_ge(sV.h, sV.n)
        sV.inc(ve.scalar_tensor_tensor(self.WGLO, self.WGHI, -1.0, self.WG,
                                       ALU.mult, ALU.add))
        v_lo = [0] * DC
        for dc in range(DC):
            sV.inc(ve.scalar_tensor_tensor(self.LOBF[:, dc, :], self.X1BF[:, dc, :],
                                           -1.0, Y1[:, dc, :], ALU.mult, ALU.add))
            v_lo[dc] = sV.n
        te.wait_ge(sV.h, sV.n)
        started = [False] * NTT
        mm = None
        for dc in range(DC):
            for (wgt, xin) in ((self.WGHI, self.X1BF), (self.WGHI, self.LOBF),
                               (self.WGLO, self.X1BF)):
                for tt in range(NTT):
                    last = (dc == DC - 1) and (wgt is self.WGLO)
                    mm = te.matmul(PS[0:8, tt, :], wgt[:, dc, :],
                                   xin[:, dc, TT * tt:TT * (tt + 1)],
                                   start=(not started[tt]), stop=last)
                    started[tt] = True
        sT.inc(mm)
        t_lg = sT.n
        se.wait_ge(sT.h, t_lg)
        sS.inc(se.activation(self.LGS, PS[0:8, 0:4, :], AF.Identity, bias=self.BG))
        s_lg = sS.n
        te.wait_ge(sS.h, s_lg)
        for j in range(16):
            sT.inc(te.matmul(PS[:, 4, 8 * j:8 * (j + 1)], self.LGS[0:8, 128 * j:128 * (j + 1)],
                             self.IDENTF[0:8, 0:8], is_transpose=True,
                             start=(j == 0), stop=(j == 15)))
        t_tr = sT.n
        ve.wait_ge(sT.h, t_tr)
        sV.inc(ve.tensor_copy(self.LT.rearrange("p a b -> p (a b)"), PS[:, 4, 0:128]))
        ve.wait_ge(sV.h, sV.n)
        for j in range(16):
            lt = self.LT[:, j, :]
            sV.inc(ve.max(self.SRT, lt))
            ve.wait_ge(sV.h, sV.n)
            sV.inc(ve.tensor_scalar_mul(self.GS1, self.SRT[:, 0:1], -1.0))
            se.wait_ge(sV.h, sV.n)
            sS.inc(se.activation(self.GS3, lt, AF.Exp, bias=self.GS1))
            sS.inc(se.activation(self.GS2, self.SRT[:, 1:2], AF.Exp, bias=self.GS1))
            ve.wait_ge(sS.h, sS.n)
            sV.inc(ve.tensor_scalar_add(self.GS2, self.GS2, 1.0))
            ve.wait_ge(sV.h, sV.n)
            sV.inc(ve.reciprocal(self.GS2, self.GS2))
            sV.inc(ve.tensor_single_scalar(self.GS4, lt, self.SRT[:, 1:2], ALU.is_ge))
            ve.wait_ge(sV.h, sV.n)
            sV.inc(ve.tensor_mul(self.CT[:, j, :], self.GS3, self.GS4))
            ve.wait_ge(sV.h, sV.n)
            sV.inc(ve.tensor_scalar_mul(self.CT[:, j, :], self.CT[:, j, :], self.GS2))
            ve.wait_ge(sV.h, sV.n)
        v_ct = sV.n
        te.wait_ge(sV.h, v_ct)
        for w in range(4):
            for k in range(4):
                j = 4 * w + k
                sT.inc(te.matmul(PS[0:8, 5, 128 * k:128 * (k + 1)],
                                 self.CT[:, j, :], self.IDENTF,
                                 is_transpose=True, start=(k == 0), stop=(k == 3)))
            ve.wait_ge(sT.h, sT.n)
            sV.inc(ve.tensor_copy(self.CSB[0:8, 512 * w:512 * (w + 1)], PS[0:8, 5, :]))
            te.wait_ge(sV.h, sV.n)
        self.v_csb = sV.n
        self.s_x1bf_all = max(s_x1bf)

    def moe(self):
        nc = self.nc
        te, ve, se, sy, ge = nc.tensor, nc.vector, nc.scalar, nc.sync, nc.gpsimd
        PS, Y1 = self.PS, self.Y1
        sT, sV, sS = self.sT, self.sV, self.sS

        self.barrier()
        se.preload_act_table(AF.Silu)
        mw1_r = self.d_mw1.rearrange("e (c p) n -> e p c n", p=128)
        self.cW1E.inc(ge.dma_start(out=self.W1EB, in_=mw1_r[0]), 16)
        def w2_dma(e, m):
            self.cW2E[m % 2].inc(ge.dma_start(
                out=self.W2E[:, m % 2, :, :],
                in_=self.d_mw2[e, :, 128 * m:128 * (m + 1)]
                    .rearrange("(a p) n -> p a n", p=128)), 16)
        w2_dma(0, 0)
        w2_dma(0, 1)

        v_y4 = [[0] * NTT for _ in range(DC)]
        v_bev = [0, 0]
        t_w2read = [0, 0]
        t_w1_end = 0
        v_a = [0] * NTT
        for e in range(E):
            if e == 0:
                te.wait_ge(sV.h, self.v_csb)
                te.wait_ge(self.cSEL.h, 16)
            else:
                te.wait_ge(sV.h, v_bev[0])
            for tt in range(NTT):
                sT.inc(te.matmul(PS[:, tt, :], self.SEL8[:, 128 * e:128 * (e + 1)],
                                 self.CSB[0:8, TT * tt:TT * (tt + 1)],
                                 start=True, stop=True))
                ve.wait_ge(sT.h, sT.n)
                sV.inc(ve.tensor_copy(self.CBC[:, TT * tt:TT * (tt + 1)], PS[:, tt, :]))
            v_cbc = sV.n
            te.wait_ge(sV.h, v_cbc)

            te.wait_ge(self.cW1E.h, 16 * (e + 1))
            if e > 0:
                te.wait_ge(sV.h, v_bev[1])
            s_sg = [0, 0]
            for jj in range(DC):
                mm = None
                for dc in range(DC):
                    for tt in range(NTT):
                        mm = te.matmul(PS[:, tt, :], self.W1EB[:, dc, 128 * jj:128 * (jj + 1)],
                                       self.X1BF[:, dc, TT * tt:TT * (tt + 1)],
                                       start=(dc == 0), stop=(dc == DC - 1))
                sT.inc(mm)
                t_g = sT.n
                mm = None
                for dc in range(DC):
                    for tt in range(NTT):
                        mm = te.matmul(PS[:, 4 + tt, :],
                                       self.W1EB[:, dc, 1024 + 128 * jj:1024 + 128 * (jj + 1)],
                                       self.X1BF[:, dc, TT * tt:TT * (tt + 1)],
                                       start=(dc == 0), stop=(dc == DC - 1))
                sT.inc(mm)
                t_u = sT.n
                for tt in range(NTT):
                    kb = tt % 2
                    se.wait_ge(sT.h, t_g)
                    se.wait_ge(sV.h, v_a[tt - 2] if tt >= 2 else v_a[tt + 2])
                    sS.inc(se.activation(self.SGB[:, kb, :], PS[:, tt, :], AF.Silu))
                    s_sg[kb] = sS.n
                    ve.wait_ge(sS.h, s_sg[kb])
                    ve.wait_ge(sT.h, t_u)
                    sV.inc(ve.tensor_mul(self.ABF[:, jj, TT * tt:TT * (tt + 1)],
                                         self.SGB[:, kb, :], PS[:, 4 + tt, :]))
                    ve.wait_ge(sV.h, sV.n)
                    sV.inc(ve.tensor_mul(self.ABF[:, jj, TT * tt:TT * (tt + 1)],
                                         self.ABF[:, jj, TT * tt:TT * (tt + 1)],
                                         self.CBC[:, TT * tt:TT * (tt + 1)]))
                    v_a[tt] = sV.n
                te.wait_ge(sV.h, v_a[NTT - 1])
            t_w1_end = sT.n
            if e + 1 < E:
                ge.wait_ge(sT.h, t_w1_end)
                self.cW1E.inc(ge.dma_start(out=self.W1EB, in_=mw1_r[e + 1]), 16)

            for m in range(DC):
                bk = (m % 2) * 4
                te.wait_ge(self.cW2E[m % 2].h, self.cW2E[m % 2].n)
                if m >= 2 or e > 0:
                    te.wait_ge(sV.h, v_bev[m % 2])
                mm = None
                for dffc in range(DC):
                    for tt in range(NTT):
                        mm = te.matmul(PS[:, bk + tt, :], self.W2E[:, m % 2, dffc, :],
                                       self.ABF[:, dffc, TT * tt:TT * (tt + 1)],
                                       start=(dffc == 0), stop=(dffc == DC - 1))
                sT.inc(mm)
                t_w2read[m % 2] = sT.n
                ne, nm = (e, m + 2) if m + 2 < DC else (e + 1, m + 2 - DC)
                if ne < E:
                    ge.wait_ge(sT.h, t_w2read[m % 2])
                    w2_dma(ne, nm)
                ve.wait_ge(sT.h, t_w2read[m % 2])
                ev = None
                for tt in range(NTT):
                    ve.wait_ge(sV.h, v_y4[m][tt])
                    ev = ve.tensor_add(Y1[:, m, TT * tt:TT * (tt + 1)],
                                       Y1[:, m, TT * tt:TT * (tt + 1)],
                                       PS[:, bk + tt, :])
                    sV.inc(ev)
                    v_y4[m][tt] = sV.n
                v_bev[m % 2] = sV.n

    def lnd_out(self):
        nc = self.nc
        te, ve, se, sy = nc.tensor, nc.vector, nc.scalar, nc.sync
        PS, Y1 = self.PS, self.Y1
        sT, sV, sS, sO = self.sT, self.sV, self.sS, self.sO

        self.barrier()
        se.preload_act_table(AF.Sqrt)
        self.ln_stats(Y1)
        v_rows = self.ln_rows(self.ROW2A, self.ROW2B, self.ROW2T)
        t_bc = self.ln_bcast(self.ROW2T, self.ROW2A, v_rows)
        outr = self.d_out.rearrange("(c p) n -> p c n", p=128)
        for dc in range(DC):
            k = dc % 2
            ve.wait_ge(sT.h, t_bc)
            if dc >= 2:
                ve.wait_ge(self.cOUT[k].h, self.cOUT[k].n)
            sV.inc(ve.tensor_sub(self.OSTG[:, k, :], Y1[:, dc, :], PS[:, 0:4, :]))
            ve.wait_ge(sV.h, sV.n)
            sV.inc(ve.tensor_mul(self.OSTG[:, k, :], self.OSTG[:, k, :], PS[:, 4:8, :]))
            sy.wait_ge(sV.h, sV.n)
            self.cOUT[k].inc(sy.dma_start(out=outr[:, dc, :], in_=self.OSTG[:, k, :]), 16)
        sy.wait_ge(self.cOUT[0].h, self.cOUT[0].n)
        sy.wait_ge(self.cOUT[1].h, self.cOUT[1].n)


_NC_CACHE = {}


def get_nc(stage="full"):
    if stage not in _NC_CACHE:
        _NC_CACHE[stage] = Builder(stage).nc
    return _NC_CACHE[stage]


def make_masks(core):
    t0 = 2048 * (core % 4)
    jj = np.arange(128)[:, None]
    ii = np.arange(256)[None, :]
    bm = np.zeros((6, 128, 256), np.float32)
    for b in range(6):
        j = 128 * b + jj
        bm[b] = ((ii <= j) & (j <= ii + 512)).astype(np.float32)
    em0 = np.zeros_like(bm)
    em1 = np.zeros_like(bm)
    for b in range(6):
        j = 128 * b + jj
        k0 = t0 - 256 + j
        k7 = t0 + 1792 - 256 + j
        em0[b] = bm[b] * ((k0 >= 0) & (k0 < S))
        em1[b] = bm[b] * ((k7 >= 0) & (k7 < S))
    bf = ml_dtypes.bfloat16
    return (bm[[0, 1, 4, 5]].astype(bf), em0.astype(bf), em1.astype(bf))


def make_inputs(core, X):
    b, t0 = core // 4, 2048 * (core % 4)
    src = np.asarray(X["L0_attn_src"][b], np.float32)
    pad = np.zeros((TH, D), np.float32)
    lo, hi = max(0, t0 - HALO), min(S, t0 + T + HALO)
    pad[lo - (t0 - HALO):hi - (t0 - HALO)] = src[lo:hi]
    srcT = np.ascontiguousarray(pad.T)
    bm, em0, em1 = make_masks(core)
    vec6 = np.stack([
        np.asarray(X["L0_attn_b2"], np.float32).reshape(8, 128).T,
        np.asarray(X["L0_attn_ln1_g"], np.float32).reshape(8, 128).T,
        np.asarray(X["L0_attn_ln1_b"], np.float32).reshape(8, 128).T,
        np.asarray(X["L0_attn_ln2_g"], np.float32).reshape(8, 128).T,
        np.asarray(X["L0_attn_ln2_b"], np.float32).reshape(8, 128).T,
    ])
    return {
        "srcT": srcT,
        "w1": np.ascontiguousarray(np.asarray(X["L0_attn_W1"], np.float32)),
        "w2": np.ascontiguousarray(np.asarray(X["L0_attn_W2"], np.float32)),
        "b1T": np.ascontiguousarray(np.asarray(X["L0_attn_b1"], np.float32).reshape(32, 128).T),
        "vec6": np.ascontiguousarray(vec6),
        "bg": np.asarray(X["L0_ffn_bg"], np.float32).reshape(8, 1),
        "wgT": np.ascontiguousarray(np.asarray(X["L0_ffn_Wg"], np.float32).T),
        "mw1": np.ascontiguousarray(np.asarray(X["L0_ffn_w1"], np.float32).transpose(0, 2, 1)),
        "mw2": np.ascontiguousarray(np.asarray(X["L0_ffn_w2"], np.float32).transpose(0, 2, 1)),
        "bmask": np.ascontiguousarray(bm),
        "em0": np.ascontiguousarray(em0),
        "em1": np.ascontiguousarray(em1),
        "identm": np.eye(128, dtype=ml_dtypes.bfloat16),
        "sel8": np.ascontiguousarray(np.kron(np.eye(8, dtype=np.float32), np.ones((1, 128), np.float32)).astype(ml_dtypes.bfloat16)),
    }


def kernel(**inputs):
    nc = get_nc("full")
    in_maps = [make_inputs(c, inputs) for c in range(NCORES)]
    res = run_bass_kernel_spmd(nc, in_maps, core_ids=list(range(NCORES)))
    out = np.empty((2, S, D), np.float32)
    for c in range(NCORES):
        b, t0 = c // 4, 2048 * (c % 4)
        out[b, t0:t0 + T] = np.asarray(res.results[c]["outT"]).T
    return out


# version 07
# speedup vs baseline: 1.1779x; 1.1779x over previous
import numpy as np
import ml_dtypes
import concourse.bass as bass
import concourse.mybir as mybir
from concourse.bass_utils import run_bass_kernel_spmd

F32 = mybir.dt.float32
BF16 = mybir.dt.bfloat16
AF = mybir.ActivationFunctionType
ALU = mybir.AluOpType
AX = mybir.AxisListType

NCORES = 8
T = 2048
HALO = 256
TH = T + 2 * HALO
D = 1024
DC = 8
H = 16
DH = 64
NG = T // 256
S = 8192
DFF = 4096
NQ = 4
NTT = 4
TT = 512
E = 8
MDFF = 2048
LN_EPS = 1e-5
NARROW = True

SBUF_BYTES = 229376 - 64
SBUF_BASE = 16512

OFF_Y1 = SBUF_BASE
OFF_VEC = SBUF_BASE + 65536
PHASE = SBUF_BASE + 69632

OFF_SRCB = PHASE
OFF_VTOK = OFF_SRCB + 40960
OFF_BMASK = OFF_VTOK + 41600
OFF_EM0 = OFF_BMASK + 2048
OFF_EM1 = OFF_EM0 + 3072
OFF_ESB = OFF_EM1 + 3072
OFF_PVS = OFF_ESB + 9216
OFF_RDEN = OFF_PVS + 2048
OFF_PVN = OFF_RDEN + 1024
OFF_W1QA = SBUF_BYTES - 16384
OFF_STB = SBUF_BASE + 180224
OFF_STB2 = OFF_STB + 4096
OFF_LNR = OFF_STB2 + 4096

OFF_ROWA = PHASE
OFF_ROWB = OFF_ROWA + 8192
OFF_ROWT = OFF_ROWB + 8192
OFF_UBF = PHASE + 24576
OFF_HQ = OFF_UBF + 32768
OFF_W1QB = OFF_HQ + 32768
OFF_W2S = OFF_W1QB + 16384
OFF_SRC2 = OFF_UBF

OFF_X1BF = OFF_UBF
OFF_ABF = OFF_HQ
OFF_W1E = OFF_HQ + 32768
OFF_CSB = PHASE
OFF_CB = PHASE + 4096
OFF_SEL = PHASE + 12288
OFF_SG = PHASE + 14336
OFF_W2E = PHASE + 16384
OFF_ROW2A = OFF_W1E
OFF_ROW2B = OFF_W1E + 8192
OFF_ROW2T = OFF_W1E + 16384
OFF_LG = OFF_W1E + 24576
OFF_OSTG = OFF_X1BF

VEC = {}
_vo = OFF_VEC
def _vecslot(name, nbytes, align=32):
    global _vo
    _vo = (_vo + align - 1) // align * align
    VEC[name] = _vo
    _vo += nbytes
_vecslot("b1T", 32 * 4)
_vecslot("b2T", 8 * 4)
_vecslot("g1T", 8 * 4)
_vecslot("b1lT", 8 * 4)
_vecslot("g2T", 8 * 4)
_vecslot("b2lT", 8 * 4)
_vecslot("bg", 4)
_vecslot("negM", 4)
_vecslot("mrow", 4)
_vecslot("wg", 64 * 4)
_vecslot("wghi", 64 * 2)
_vecslot("wglo", 64 * 2)
_vecslot("ident", 128 * 2)
_vecslot("ones_bf", 2)
_vecslot("onesr_bf", 64 * 2)
_vecslot("lt", 16 * 8 * 4)
_vecslot("ct", 16 * 8 * 4)
_vecslot("srt", 8 * 4)
_vecslot("gs1", 4)
_vecslot("gs2", 4)
_vecslot("gs3", 8 * 4)
_vecslot("gs4", 8 * 4)
_vecslot("onesf", 128 * 4)
_vecslot("identf", 128 * 4)
_vecslot("eps1", 4)
_vecslot("cz", 4)
_vecslot("co", 4)
_vecslot("cn1", 4)
_vecslot("cnh", 4)
assert _vo <= PHASE, _vo
assert PHASE + 24576 + 32768 + 32768 + 16384 + 4096 <= OFF_W1QA


class Ctr:
    def __init__(self, nc, name):
        self.h = nc.alloc_semaphore(name)
        self.n = 0

    def inc(self, instr, k=1):
        instr.then_inc(self.h, k)
        self.n += k
        return self.n


class Builder:
    def __init__(self, stage="full"):
        self.stage = stage
        nc = self.nc = bass.Bass("TRN2", target_bir_lowering=False)
        assert nc.SBUF_PARTITION_SIZE_BYTES >= SBUF_BYTES
        assert nc.sbuf_base <= SBUF_BASE, nc.sbuf_base

        def dram(name, shape, kind="ExternalInput", dt=F32):
            return nc.dram_tensor(name, shape, dt, kind=kind).ap()

        self.d_srcT = dram("srcT", [D, TH])
        self.d_w1 = dram("w1", [D, DFF])
        self.d_w2 = dram("w2", [DFF, D])
        self.d_b1T = dram("b1T", [128, 32])
        self.d_vec6 = dram("vec6", [5, 128, 8])
        self.d_bg = dram("bg", [8, 1])
        self.d_wgT = dram("wgT", [D, 8])
        self.d_mw1 = dram("mw1", [E, D, MDFF])
        self.d_mw2 = dram("mw2", [E, D, D])
        self.d_bmask = dram("bmask", [4, 128, 256], dt=BF16)
        self.d_em0 = dram("em0", [6, 128, 256], dt=BF16)
        self.d_em1 = dram("em1", [6, 128, 256], dt=BF16)
        self.d_ident = dram("identm", [128, 128], dt=BF16)
        self.d_sel8 = dram("sel8", [8, 1024], dt=BF16)
        self.d_tmask = dram("tmask", [4, 128, 128], dt=BF16)
        self.d_out = dram("outT", [D, T], kind="ExternalOutput")

        sb = lambda name, shape, dt, off: nc.alloc_sbuf_tensor_at(name, shape, dt, offset=off).ap()
        self.Y1 = sb("y1", [128, DC, T], F32, OFF_Y1)
        self.SRCB = sb("srcb", [128, DC, TH], BF16, OFF_SRCB)
        self.VTOK = sb("vtok", [128, 20, H, 65], BF16, OFF_VTOK)
        self.BMASK = sb("bmask_s", [128, 4, 256], BF16, OFF_BMASK)
        self.EM0 = sb("em0_s", [128, 6, 256], BF16, OFF_EM0)
        self.EM1 = sb("em1_s", [128, 6, 256], BF16, OFF_EM1)
        self.ESB = sb("esb", [128, 3, 1536], BF16, OFF_ESB)
        self.PVS = sb("pvs", [65, 2, 256], F32, OFF_PVS)
        self.RDEN = sb("rden", [128, 2, 256], BF16, OFF_RDEN)
        self.PVN = sb("pvn", [64, 2, 256], BF16, OFF_PVN)
        self.W1QA = sb("w1qa", [128, DC, 1024], BF16, OFF_W1QA)
        self.W1QB = sb("w1qb", [128, DC, 1024], BF16, OFF_W1QB)
        self.W2S = sb("w2s", [128, 2, DC, 128], BF16, OFF_W2S)
        self.ROWA = sb("rowa", [1, T], F32, OFF_ROWA)
        self.ROWB = sb("rowb", [1, T], F32, OFF_ROWB)
        self.ROWT = sb("rowt", [1, T], F32, OFF_ROWT)
        self.UBF = sb("ubf", [128, DC, T], BF16, OFF_UBF)
        self.HQ = sb("hq", [128, DC, T], BF16, OFF_HQ)
        self.SRC2 = sb("src2", [128, DC, T], F32, OFF_SRC2)
        self.X1BF = sb("x1bf", [128, DC, T], BF16, OFF_X1BF)
        self.LOBF = sb("lobf", [128, DC, T], BF16, OFF_ABF)
        self.ABF = sb("abf", [128, DC, T], BF16, OFF_ABF)
        self.W1EB = sb("w1e", [128, DC, MDFF], BF16, OFF_W1E)
        self.CSB = sb("csb", [8, T], BF16, OFF_CSB)
        self.SEL8 = sb("sel8_s", [8, 1024], BF16, OFF_SEL)
        self.TMASK = sb("tmask_s", [128, 4, 128], BF16, OFF_LNR + 3072)
        self.CBC = sb("cb", [128, T], BF16, OFF_CB)
        self.SGB = sb("sgb", [128, 2, TT], BF16, OFF_SG)
        self.W2E = sb("w2e", [128, 2, DC, 128], BF16, OFF_W2E)
        self.ROW2A = sb("row2a", [1, T], F32, OFF_ROW2A)
        self.ROW2B = sb("row2b", [1, T], F32, OFF_ROW2B)
        self.ROW2T = sb("row2t", [1, T], F32, OFF_ROW2T)
        self.LGS = sb("lg", [8, T], F32, OFF_LG)
        self.OSTG = sb("ostg", [128, 2, T], F32, OFF_OSTG)

        self.B1T = sb("b1t_s", [128, 32], F32, VEC["b1T"])
        self.B2T = sb("b2t_s", [128, 8], F32, VEC["b2T"])
        self.G1T = sb("g1t_s", [128, 8], F32, VEC["g1T"])
        self.B1LT = sb("b1lt_s", [128, 8], F32, VEC["b1lT"])
        self.G2T = sb("g2t_s", [128, 8], F32, VEC["g2T"])
        self.B2LT = sb("b2lt_s", [128, 8], F32, VEC["b2lT"])
        self.BG = sb("bg_s", [8, 1], F32, VEC["bg"])
        self.NEGM = sb("negm_s", [128, 1], F32, VEC["negM"])
        self.MROW = sb("mrow_s", [128, 1], F32, VEC["mrow"])
        self.WG = sb("wg_s", [128, DC, 8], F32, VEC["wg"])
        self.WGHI = sb("wghi_s", [128, DC, 8], BF16, VEC["wghi"])
        self.WGLO = sb("wglo_s", [128, DC, 8], BF16, VEC["wglo"])
        self.IDENT = sb("ident_s", [128, 128], BF16, VEC["ident"])
        self.ONES = sb("ones_s", [128, 1], BF16, VEC["ones_bf"])
        self.ONESR = sb("onesr_s", [128, 64], BF16, VEC["onesr_bf"])
        self.LT = sb("lt_s", [128, 16, 8], F32, VEC["lt"])
        self.CT = sb("ct_s", [128, 16, 8], F32, VEC["ct"])
        self.SRT = sb("srt_s", [128, 8], F32, VEC["srt"])
        self.GS1 = sb("gs1_s", [128, 1], F32, VEC["gs1"])
        self.GS2 = sb("gs2_s", [128, 1], F32, VEC["gs2"])
        self.GS3 = sb("gs3_s", [128, 8], F32, VEC["gs3"])
        self.GS4 = sb("gs4_s", [128, 8], F32, VEC["gs4"])
        self.ONESF = sb("onesf_s", [1, 128], F32, VEC["onesf"])
        self.IDENTF = sb("identf_s", [128, 128], F32, VEC["identf"])
        self.STB = sb("stb_s", [128, 4, 512], BF16, OFF_STB)
        self.STB2 = sb("stb2_s", [128, 4, 512], BF16, OFF_STB2)
        self.LNR = sb("lnr_s", [65, 2, 256], F32, OFF_LNR)
        self.ONED = sb("oned_s", [65, 256], F32, OFF_LNR + 2048)
        self.EPS1 = sb("eps1_s", [1, 1], F32, VEC["eps1"])
        self.CZ = sb("cz_s", [128, 1], F32, VEC["cz"])
        self.CO = sb("co_s", [128, 1], F32, VEC["co"])
        nc.const_aps.register(F32, 0.0, self.CZ)
        nc.const_aps.register(F32, 1.0, self.CO)
        self.CN1 = sb("cn1_s", [128, 1], F32, VEC["cn1"])
        self.CNH = sb("cnh_s", [128, 1], F32, VEC["cnh"])
        nc.const_aps.register(F32, -1.0, self.CN1)
        nc.const_aps.register(F32, -0.5, self.CNH)

        self.PS = nc.alloc_psum_tensor("ps", [128, 8, 512], F32).ap()
        self.PSB = self.PS.bitcast(BF16)

        self.sT = Ctr(nc, "c_st")
        self.sV = Ctr(nc, "c_sv")
        self.sS = Ctr(nc, "c_ss")
        self.sP = Ctr(nc, "c_sp")
        self.sO = Ctr(nc, "c_so")
        self.cY1 = Ctr(nc, "c_y1")
        self.cCST = Ctr(nc, "c_cst")
        self.cSRCB = Ctr(nc, "c_srcb")
        self.cW1A = Ctr(nc, "c_w1a")
        self.cW1B = Ctr(nc, "c_w1b")
        self.cW2S = [Ctr(nc, "c_w2s0"), Ctr(nc, "c_w2s1")]
        self.cSRC2 = Ctr(nc, "c_src2")
        self.cW1E = Ctr(nc, "c_w1e")
        self.cSEL = Ctr(nc, "c_sel")
        self.cOUT = [Ctr(nc, "c_out0"), Ctr(nc, "c_out1")]
        self.cW2E = [Ctr(nc, "c_w2e0"), Ctr(nc, "c_w2e1")]

        self.build()

    def barrier(self):
        self.nc.all_engine_barrier()

    def preload_set(self, name):
        from concourse.hw_specs import get_activation_tables
        names = list(get_activation_tables(self.nc.m.arch).keys())
        idx = names.index(name)
        se = self.nc.scalar
        se.add_instruction(mybir.InstLoadActFuncSet(
            name=self.nc.get_next_instruction_name(), act_func_set_id=idx,
            ins=[], outs=[]))

    def build(self):
        nc = self.nc
        self.prologue()
        self.attn()
        if self.stage == "attn":
            self.emit_out(self.Y1)
            return
        self.lna_ffn()
        if self.stage == "ffn":
            self.emit_out(self.Y1)
            return
        self.lnb_lnc_gating()
        self.moe()
        self.lnd_out()

    def emit_out(self, plane):
        nc, sy = self.nc, self.nc.sync
        self.barrier()
        outr = self.d_out.rearrange("(c p) n -> p c n", p=128)
        for dc in range(DC):
            self.sO.inc(sy.dma_start(out=outr[:, dc, :], in_=plane[:, dc, :]), 16)
        sy.wait_ge(self.sO.h, self.sO.n)

    def prologue(self):
        nc = self.nc
        te, ve, se, sy, ge = nc.tensor, nc.vector, nc.scalar, nc.sync, nc.gpsimd
        src_r = self.d_srcT.rearrange("(c p) n -> p c n", p=128)
        self.cY1.inc(sy.dma_start(out=self.Y1, in_=src_r[:, :, HALO:HALO + T]), 16)
        self.cCST.inc(sy.dma_start(out=self.BMASK, in_=self.d_bmask.rearrange("c p n -> p c n")), 16)
        self.cCST.inc(sy.dma_start(out=self.EM0, in_=self.d_em0.rearrange("c p n -> p c n")), 16)
        self.cCST.inc(sy.dma_start(out=self.EM1, in_=self.d_em1.rearrange("c p n -> p c n")), 16)
        self.cCST.inc(sy.dma_start(out=self.IDENT, in_=self.d_ident), 16)
        self.cCST.inc(sy.dma_start(out=self.B1T, in_=self.d_b1T), 16)
        for i, t in enumerate([self.B2T, self.G1T, self.B1LT, self.G2T, self.B2LT]):
            self.cCST.inc(sy.dma_start(out=t, in_=self.d_vec6[i]), 16)
        self.cCST.inc(sy.dma_start(out=self.BG, in_=self.d_bg), 16)
        self.cCST.inc(sy.dma_start(out=self.TMASK, in_=self.d_tmask.rearrange("c p n -> p c n")), 16)
        self.cCST.inc(sy.dma_start(out=self.WG, in_=self.d_wgT.rearrange("(c p) e -> p c e", p=128)), 16)
        self.cSRCB.inc(ge.dma_start(out=self.SRCB, in_=src_r), 16)
        self.w1_r = self.d_w1.rearrange("(c p) (q n) -> q p c n", p=128, n=1024)
        self.cW1A.inc(ge.dma_start(out=self.W1QA, in_=self.w1_r[0]), 16)

        ve.memset(self.ONES, 1.0)
        ve.memset(self.ONESR[64:65, :], 1.0)
        ve.memset(self.ONED[64:65, :], 1.0)
        ve.memset(self.ONESF, 1.0)
        ve.memset(self.EPS1, float(LN_EPS))
        ve.memzero(self.CZ)
        ve.memset(self.CO, 1.0)
        ve.memset(self.CN1, -1.0)
        ve.memset(self.CNH, -0.5)
        ve.wait_ge(self.cCST.h, self.cCST.n)
        ve.tensor_copy(self.IDENTF, self.IDENT)
        self.sV.inc(ve.memset(self.VTOK[:, :, :, 64:65], 1.0))
        self.v_ones = self.sV.n

    def attn(self):
        nc = self.nc
        te, ve, se, ge = nc.tensor, nc.vector, nc.scalar, nc.gpsimd
        PS, SRCB, ESB, VTOK, PVS, RDEN, Y1 = (self.PS, self.SRCB, self.ESB, self.VTOK,
                                              self.PVS, self.RDEN, self.Y1)
        sT, sV, sS, sP = self.sT, self.sV, self.sS, self.sP

        self.preload_set("natural_log_exp_and_others")

        te.wait_ge(self.cSRCB.h, self.cSRCB.n)
        te.wait_ge(self.cCST.h, self.cCST.n)
        ve.wait_ge(self.cCST.h, self.cCST.n)
        sT.inc(te.matmul(PS[:, 0, 0:320], SRCB[0:64, 0, 256:384],
                         SRCB[0:64, 0, 0:2560:8], start=True, stop=True))
        ve.wait_ge(sT.h, sT.n)
        sV.inc(ve.tensor_reduce(self.MROW, PS[:, 0, 0:320], AX.X, ALU.max))
        te.wait_ge(sV.h, sV.n)
        sT.inc(te.matmul(PS[0:1, 1, 0:128], self.MROW, self.IDENTF,
                         is_transpose=True, start=True, stop=True))
        ve.wait_ge(sT.h, sT.n)
        sV.inc(ve.tensor_reduce(self.MROW[0:1, :], PS[0:1, 1, 0:128], AX.X, ALU.max))
        te.wait_ge(sV.h, sV.n)
        sT.inc(te.matmul(PS[:, 1, 0:1], self.ONESF, self.MROW[0:1, :],
                         start=True, stop=True))
        ve.wait_ge(sT.h, sT.n)
        sV.inc(ve.tensor_scalar(self.NEGM, PS[:, 1, 0:1], -1.0, -8.0, ALU.mult, ALU.add))
        v_negm = sV.n
        se.wait_ge(sV.h, v_negm)
        te.wait_ge(sV.h, v_negm)

        NI = NG * H
        t_qk = [0] * NI
        t_pv = [0] * NI
        t_bc = [0] * NI
        t_sh = [0] * NI
        s_exp = [0] * NI
        v_cp = [0] * NI
        v_msk = [0] * NI
        s_rec = [0] * NI
        v_mul = [0] * NI
        v_acc = [0] * NI
        built = set()

        def chunk_cols(gg, b):
            if not NARROW or gg in (0, NG - 1):
                return 0, 256
            if b == 0:
                return 0, 128
            if b == 5:
                return 128, 256
            return 0, 256

        ve.wait_ge(self.cY1.h, self.cY1.n)
        ve.wait_ge(sV.h, self.v_ones)

        def build_vhat(kcs):
            for kc in kcs:
                if kc in built:
                    continue
                built.add(kc)
                for wave in range(2):
                    for k in range(4):
                        dc = 4 * wave + k
                        sT.inc(te.matmul(self.PSB[:, 0, 128 * k:128 * (k + 1)],
                                         SRCB[:, dc, 128 * kc:128 * (kc + 1)],
                                         self.IDENT, is_transpose=True,
                                         start=(k == 0), stop=(k == 3)))
                    ve.wait_ge(sT.h, sT.n)
                    src_ap = self.PSB[:, 0, 0:512] \
                        .rearrange("p (a b) -> p a b", a=4) \
                        .rearrange("p a (c d) -> p (a c) d", c=2)
                    sV.inc(ve.tensor_copy(VTOK[:, kc, 8 * wave:8 * wave + 8, 0:64], src_ap))
                    te.wait_ge(sV.h, sV.n)

        def emit_qk(i):
            gg, h = divmod(i, H)
            dc, r0 = h // 2, 64 * (h % 2)
            par = i % 2
            if i >= 2:
                te.wait_ge(sS.h, s_exp[i - 2])
            mm = None
            for b in range(6):
                kc = 2 * gg + b
                c0, c1 = chunk_cols(gg, b)
                mm = te.matmul(PS[:, 3 * par + b // 2, 256 * (b % 2) + c0:256 * (b % 2) + c1],
                               SRCB[r0:r0 + 64, dc, 128 * kc:128 * kc + 128],
                               SRCB[r0:r0 + 64, dc, 256 + 256 * gg + c0:256 + 256 * gg + c1],
                               start=(b % 2 == 0), stop=(b % 2 == 1))
            sT.inc(mm)
            t_qk[i] = sT.n

        def emit_exp(i):
            par3 = i % 3
            se.wait_ge(sT.h, t_qk[i])
            if i >= 3:
                se.wait_ge(sT.h, t_pv[i - 3])
            sS.inc(se.activation(ESB[:, par3, :], PS[:, 3 * (i % 2):3 * (i % 2) + 3, :],
                                 AF.Exp, bias=self.NEGM))
            s_exp[i] = sS.n

        def emit_mask(i):
            gg = i // H
            par3 = i % 3
            ve.wait_ge(sS.h, s_exp[i])
            mm = None
            if gg == 0:
                for (ec, mc) in ((0, 0), (2, 2), (4, 4)):
                    mm = ve.tensor_mul(ESB[:, par3, 256 * ec:256 * ec + 512],
                                       ESB[:, par3, 256 * ec:256 * ec + 512],
                                       self.EM0[:, mc:mc + 2, :])
            elif gg == NG - 1:
                for (ec, mc) in ((0, 0), (2, 2), (4, 4)):
                    mm = ve.tensor_mul(ESB[:, par3, 256 * ec:256 * ec + 512],
                                       ESB[:, par3, 256 * ec:256 * ec + 512],
                                       self.EM1[:, mc:mc + 2, :])
            elif NARROW:
                tri = bass.AP(ESB.tensor, ESB.offset + par3 * 1536,
                              [[4608, 128], [1024, 2], [384, 2], [1, 128]])
                tm = self.TMASK.rearrange("p (a c) x -> p a c x", a=2)
                mm = ve.tensor_mul(tri, tri, tm)
            else:
                for (ec, mc) in ((0, 0), (4, 2)):
                    mm = ve.tensor_mul(ESB[:, par3, 256 * ec:256 * ec + 512],
                                       ESB[:, par3, 256 * ec:256 * ec + 512],
                                       self.BMASK[:, mc:mc + 2, :])
            sV.inc(mm)
            v_msk[i] = sV.n

        def emit_pv(i):
            gg, h = divmod(i, H)
            par = i % 2
            par3 = i % 3
            te.wait_ge(sV.h, v_msk[i])
            if i >= 2:
                te.wait_ge(sV.h, v_acc[i - 2])
            mm = None
            bs = range(6) if (not NARROW or gg in (0, NG - 1)) else (1, 0, 2, 3, 4, 5)
            for bi, b in enumerate(bs):
                kc = 2 * gg + b
                c0, c1 = chunk_cols(gg, b)
                mm = te.matmul(PS[0:65, 6 + par, c0:c1],
                               VTOK[:, kc, h, :],
                               ESB[:, par3, 256 * b + c0:256 * b + c1],
                               start=(bi == 0), stop=(bi == 5))
            sT.inc(mm)
            t_pv[i] = sT.n

        def emit_copy(i):
            par = i % 2
            se.wait_ge(sT.h, t_pv[i])
            sS.inc(se.copy(PVS[:, par, :], PS[0:65, 6 + par, 0:256]))
            v_cp[i] = sS.n

        def emit_recip(i):
            par = i % 2
            ve.wait_ge(sS.h, v_cp[i])
            if i >= 2:
                ve.wait_ge(sT.h, t_bc[i - 2])
            with nc.allow_low_precision(reason="softmax denom reciprocal feeds a bf16 rank-1 broadcast matmul"):
                sV.inc(ve.reciprocal(RDEN[64:65, par, :], PVS[64:65, par, :]))
            s_rec[i] = sV.n

        def emit_bcast(i):
            par = i % 2
            te.wait_ge(sV.h, s_rec[i])
            sT.inc(te.matmul(PS[0:64, 6 + par, 256:512], self.ONESR[64:65, :],
                             RDEN[64:65, par, :], start=True, stop=True))
            t_bc[i] = sT.n

        def emit_mult(i):
            par = i % 2
            ve.wait_ge(sT.h, t_bc[i])
            sV.inc(ve.tensor_mul(self.PVN[:, par, :], PVS[0:64, par, :],
                                 PS[0:64, 6 + par, 256:512]))
            v_mul[i] = sV.n

        def emit_shift(i):
            par = i % 2
            te.wait_ge(sV.h, v_mul[i])
            sT.inc(te.matmul(PS[64:128, 6 + par, 256:512], self.IDENT[0:64, 0:64],
                             self.PVN[:, par, :], start=True, stop=True))
            t_sh[i] = sT.n

        def emit_add(i):
            gg, h = divmod(i, H)
            dc, r0 = h // 2, 64 * (h % 2)
            par = i % 2
            y = Y1[r0:r0 + 64, dc, 256 * gg:256 * gg + 256]
            if h % 2 == 0:
                ve.wait_ge(sV.h, v_mul[i])
                sV.inc(ve.tensor_add(y, y, self.PVN[:, par, :]))
            else:
                ve.wait_ge(sT.h, t_sh[i])
                sV.inc(ve.tensor_add(y, y, PS[64:128, 6 + par, 256:512]))
            v_acc[i] = sV.n

        for i in range(NI + 4):
            j4 = i - 4
            if 0 <= j4 < NI:
                emit_mult(j4)
                if j4 % 2 == 1:
                    emit_shift(j4)
                emit_add(j4)
            if i < NI:
                if i % H == 0:
                    gg = i // H
                    kcs = list(range(6)) if gg == 0 else [2 * gg + 4, 2 * gg + 5]
                    if i >= 2:
                        te.wait_ge(sS.h, s_exp[i - 1])
                    build_vhat(kcs)
                emit_qk(i)
            if 0 <= i - 3 < NI:
                emit_copy(i - 3)
            if i < NI:
                emit_exp(i)
            if 0 <= i - 1 < NI:
                emit_mask(i - 1)
            if 0 <= i - 2 < NI:
                emit_pv(i - 2)
            if 0 <= i - 3 < NI:
                emit_recip(i - 3)
                emit_bcast(i - 3)

    def ln_stats(self, plane):
        nc = self.nc
        te, ve, se = nc.tensor, nc.vector, nc.scalar
        sT, sV, sS = self.sT, self.sV, self.sS
        last_mm = [0, 0, 0, 0]
        for dc in range(DC):
            for tt in range(NTT):
                k = (dc * NTT + tt) % 4
                tile = plane[:, dc, TT * tt:TT * (tt + 1)]
                ve.wait_ge(sT.h, last_mm[k])
                se.wait_ge(sT.h, last_mm[k])
                sV.inc(ve.tensor_copy(self.STB[:, k, :], tile))
                sS.inc(se.activation(self.STB2[:, k, :], tile, AF.Square))
                te.wait_ge(sV.h, sV.n)
                te.wait_ge(sS.h, sS.n)
                te.matmul(self.PS[0:1, tt, :], self.ONES, self.STB[:, k, :],
                          start=(dc == 0), stop=(dc == DC - 1))
                sT.inc(te.matmul(self.PS[0:1, 4 + tt, :], self.ONES, self.STB2[:, k, :],
                                 start=(dc == 0), stop=(dc == DC - 1)))
                last_mm[k] = sT.n
        return sT.n

    def ln_rows(self, rowA, rowB, rowT):
        nc = self.nc
        ve, se = nc.vector, nc.scalar
        sV, sS = self.sV, self.sS
        sumv = self.PS[0:1, 0:4, :]
        ssv = self.PS[0:1, 4:8, :]
        ve.wait_ge(self.sT.h, self.sT.n)
        sV.inc(ve.tensor_scalar_mul(rowT, sumv, 1.0 / D))
        ve.wait_ge(sV.h, sV.n)
        sV.inc(ve.tensor_mul(rowB, rowT, rowT))
        ve.wait_ge(sV.h, sV.n)
        sV.inc(ve.scalar_tensor_tensor(rowA, ssv, 1.0 / D, rowB, ALU.mult, ALU.subtract))
        se.wait_ge(sV.h, sV.n)
        sS.inc(se.activation(rowB, rowA, AF.Ln, bias=self.EPS1))
        se.wait_ge(sS.h, sS.n)
        sS.inc(se.activation(rowA, rowB, AF.Exp, scale=-0.5))
        return sS.n

    def ln_bcast(self, rowT, rowA, s_rows):
        te = self.nc.tensor
        te.wait_ge(self.sS.h, s_rows)
        for tt in range(NTT):
            te.matmul(self.PS[:, tt, :], self.ONESF, rowT[0:1, TT * tt:TT * (tt + 1)],
                      start=True, stop=True)
            self.sT.inc(te.matmul(self.PS[:, 4 + tt, :], self.ONESF,
                                  rowA[0:1, TT * tt:TT * (tt + 1)], start=True, stop=True))
        return self.sT.n

    def lna_ffn(self):
        nc = self.nc
        te, ve, se, sy, ge = nc.tensor, nc.vector, nc.scalar, nc.sync, nc.gpsimd
        PS, Y1, UBF, HQ = self.PS, self.Y1, self.UBF, self.HQ
        sT, sV, sS = self.sT, self.sV, self.sS

        self.barrier()
        t_stats_a = self.ln_stats(Y1)
        v_rows = self.ln_rows(self.ROWA, self.ROWB, self.ROWT)
        t_bc = self.ln_bcast(self.ROWT, self.ROWA, v_rows)
        v_apply = [0] * DC
        s_cast = [0] * DC
        for dc in range(DC):
            ve.wait_ge(sT.h, t_bc)
            sV.inc(ve.tensor_sub(Y1[:, dc, :], Y1[:, dc, :], PS[:, 0:4, :]))
            ve.wait_ge(sV.h, sV.n)
            sV.inc(ve.tensor_mul(Y1[:, dc, :], Y1[:, dc, :], PS[:, 4:8, :]))
            t1v = sV.n
            se.wait_ge(sV.h, t1v)
            sS.inc(se.activation(UBF[:, dc, :], Y1[:, dc, :], AF.Identity,
                                 scale=self.G1T[:, dc:dc + 1], bias=self.B1LT[:, dc:dc + 1]))
            s_cast[dc] = sS.n
            ve.wait_ge(sS.h, s_cast[dc])
            sV.inc(ve.tensor_scalar(Y1[:, dc, :], Y1[:, dc, :],
                                    self.G1T[:, dc:dc + 1], self.B1LT[:, dc:dc + 1],
                                    ALU.mult, ALU.add))
            v_apply[dc] = sV.n

        te.wait_ge(sS.h, max(s_cast))
        te.wait_ge(sV.h, max(v_apply))
        w1buf = [self.W1QA, self.W1QB]
        w1sem = [self.cW1A, self.cW1B]
        w2_r = self.d_w2.rearrange("(a p) n -> a p n", p=128)
        v_y2 = [[0] * NTT for _ in range(DC)]
        s_hq = [0] * DC
        t_a = {}
        t_b = {}
        v_bev = [0, 0]
        t_w2read = [0, 0]
        for q in range(NQ):
            wb = w1buf[q % 2]
            wsem = w1sem[q % 2]
            wcnt = 16 * (q // 2 + 1)
            if q + 1 < NQ:
                if q >= 1:
                    ge.wait_ge(sT.h, self.t_a_end)
                w1sem[(q + 1) % 2].inc(ge.dma_start(out=w1buf[(q + 1) % 2],
                                                    in_=self.w1_r[q + 1]), 16)
            if q == 0:
                ge.wait_ge(sT.h, t_stats_a)
                for m0 in range(2):
                    self.cW2S[m0].inc(ge.dma_start(
                        out=self.W2S[:, m0, :, :],
                        in_=self.d_w2[1024 * q:1024 * (q + 1), 128 * m0:128 * (m0 + 1)]
                            .rearrange("(a p) n -> p a n", p=128)), 16)
            te.wait_ge(wsem.h, wcnt)
            if q > 0:
                te.wait_ge(sV.h, v_bev[0])
                te.wait_ge(sV.h, v_bev[1])
            for hc in range(DC):
                bk = (hc % 2) * 4
                if hc >= 2:
                    te.wait_ge(sS.h, s_hq[hc - 2])
                mm = None
                for dc in range(DC):
                    for tt in range(NTT):
                        mm = te.matmul(PS[:, bk + tt, :], wb[:, dc, 128 * hc:128 * (hc + 1)],
                                       UBF[:, dc, TT * tt:TT * (tt + 1)],
                                       start=(dc == 0), stop=(dc == DC - 1))
                sT.inc(mm)
                t_a[hc] = sT.n
                se.wait_ge(sT.h, t_a[hc])
                a = None
                for tt in range(NTT):
                    a = se.activation(HQ[:, hc, TT * tt:TT * (tt + 1)], PS[:, bk + tt, :],
                                      AF.Relu, bias=self.B1T[:, 8 * q + hc:8 * q + hc + 1])
                sS.inc(a)
                s_hq[hc] = sS.n
            self.t_a_end = t_a[DC - 1]
            for m in range(DC):
                bk = (m % 2) * 4
                te.wait_ge(self.cW2S[m % 2].h, self.cW2S[m % 2].n)
                te.wait_ge(sS.h, s_hq[DC - 1])
                if m >= 2:
                    te.wait_ge(sV.h, v_bev[m % 2])
                mm = None
                for hc in range(DC):
                    for tt in range(NTT):
                        mm = te.matmul(PS[:, bk + tt, :], self.W2S[:, m % 2, hc, :],
                                       HQ[:, hc, TT * tt:TT * (tt + 1)],
                                       start=(hc == 0), stop=(hc == DC - 1))
                sT.inc(mm)
                t_b[m] = sT.n
                t_w2read[m % 2] = sT.n
                nq, nm = (q, m + 2) if m + 2 < DC else (q + 1, m + 2 - DC)
                if nq < NQ:
                    ge.wait_ge(sT.h, t_b[m])
                    self.cW2S[m % 2].inc(ge.dma_start(
                        out=self.W2S[:, m % 2, :, :],
                        in_=self.d_w2[1024 * nq:1024 * (nq + 1), 128 * nm:128 * (nm + 1)]
                            .rearrange("(a p) n -> p a n", p=128)), 16)
                ve.wait_ge(sT.h, t_b[m])
                ev = None
                for tt in range(NTT):
                    ve.wait_ge(sV.h, v_y2[m][tt])
                    if q == 0:
                        ev = ve.scalar_tensor_tensor(Y1[:, m, TT * tt:TT * (tt + 1)],
                                                     PS[:, bk + tt, :],
                                                     self.B2T[:, m:m + 1],
                                                     Y1[:, m, TT * tt:TT * (tt + 1)],
                                                     ALU.add, ALU.add)
                    else:
                        ev = ve.tensor_add(Y1[:, m, TT * tt:TT * (tt + 1)],
                                           Y1[:, m, TT * tt:TT * (tt + 1)],
                                           PS[:, bk + tt, :])
                    sV.inc(ev)
                    v_y2[m][tt] = sV.n
                v_bev[m % 2] = sV.n

    def lnb_lnc_gating(self):
        nc = self.nc
        te, ve, se, sy, ge = nc.tensor, nc.vector, nc.scalar, nc.sync, nc.gpsimd
        PS, Y1, SRC2 = self.PS, self.Y1, self.SRC2
        sT, sV, sS = self.sT, self.sV, self.sS

        self.barrier()
        t_stats = self.ln_stats(Y1)
        sy.wait_ge(sT.h, t_stats)
        src_r = self.d_srcT.rearrange("(c p) n -> p c n", p=128)
        self.cSRC2.inc(sy.dma_start(out=SRC2, in_=src_r[:, :, HALO:HALO + T]), 16)
        v_rows = self.ln_rows(self.ROWA, self.ROWB, self.ROWT)
        t_bc = self.ln_bcast(self.ROWT, self.ROWA, v_rows)
        se.wait_ge(self.cSRC2.h, self.cSRC2.n)
        s_bias = [0] * DC
        for dc in range(DC):
            sS.inc(se.add(SRC2[:, dc, :], SRC2[:, dc, :], self.B2LT[:, dc:dc + 1]))
            s_bias[dc] = sS.n
        v_y3 = [0] * DC
        for dc in range(DC):
            ve.wait_ge(sT.h, t_bc)
            sV.inc(ve.tensor_sub(Y1[:, dc, :], Y1[:, dc, :], PS[:, 0:4, :]))
            ve.wait_ge(sV.h, sV.n)
            sV.inc(ve.tensor_mul(Y1[:, dc, :], Y1[:, dc, :], PS[:, 4:8, :]))
            ve.wait_ge(sV.h, sV.n)
            ve.wait_ge(sS.h, s_bias[dc])
            sV.inc(ve.scalar_tensor_tensor(SRC2[:, dc, :], Y1[:, dc, :],
                                           self.G2T[:, dc:dc + 1], SRC2[:, dc, :],
                                           ALU.mult, ALU.add))
            v_y3[dc] = sV.n

        self.barrier()
        self.ln_stats(SRC2)
        v_rows = self.ln_rows(self.ROW2A, self.ROW2B, self.ROW2T)
        t_bc = self.ln_bcast(self.ROW2T, self.ROW2A, v_rows)
        s_x1bf = [0] * DC
        for dc in range(DC):
            ve.wait_ge(sT.h, t_bc)
            sV.inc(ve.tensor_sub(Y1[:, dc, :], SRC2[:, dc, :], PS[:, 0:4, :]))
            ve.wait_ge(sV.h, sV.n)
            sV.inc(ve.tensor_mul(Y1[:, dc, :], Y1[:, dc, :], PS[:, 4:8, :]))
            se.wait_ge(sV.h, sV.n)
            sS.inc(se.copy(self.X1BF[:, dc, :], Y1[:, dc, :]))
            s_x1bf[dc] = sS.n

        self.barrier()
        self.cSEL.inc(sy.dma_start(out=self.SEL8, in_=self.d_sel8), 16)
        sV.inc(ve.tensor_copy(self.WGHI, self.WG))
        ve.wait_ge(sV.h, sV.n)
        sV.inc(ve.scalar_tensor_tensor(self.WGLO, self.WGHI, -1.0, self.WG,
                                       ALU.mult, ALU.add))
        v_lo = [0] * DC
        for dc in range(DC):
            sV.inc(ve.scalar_tensor_tensor(self.LOBF[:, dc, :], self.X1BF[:, dc, :],
                                           -1.0, Y1[:, dc, :], ALU.mult, ALU.add))
            v_lo[dc] = sV.n
        te.wait_ge(sV.h, sV.n)
        started = [False] * NTT
        mm = None
        for dc in range(DC):
            for (wgt, xin) in ((self.WGHI, self.X1BF), (self.WGHI, self.LOBF),
                               (self.WGLO, self.X1BF)):
                for tt in range(NTT):
                    last = (dc == DC - 1) and (wgt is self.WGLO)
                    mm = te.matmul(PS[0:8, tt, :], wgt[:, dc, :],
                                   xin[:, dc, TT * tt:TT * (tt + 1)],
                                   start=(not started[tt]), stop=last)
                    started[tt] = True
        sT.inc(mm)
        t_lg = sT.n
        se.wait_ge(sT.h, t_lg)
        sS.inc(se.activation(self.LGS, PS[0:8, 0:4, :], AF.Identity, bias=self.BG))
        s_lg = sS.n
        te.wait_ge(sS.h, s_lg)
        for j in range(16):
            sT.inc(te.matmul(PS[:, 4, 8 * j:8 * (j + 1)], self.LGS[0:8, 128 * j:128 * (j + 1)],
                             self.IDENTF[0:8, 0:8], is_transpose=True,
                             start=(j == 0), stop=(j == 15)))
        t_tr = sT.n
        ve.wait_ge(sT.h, t_tr)
        sV.inc(ve.tensor_copy(self.LT.rearrange("p a b -> p (a b)"), PS[:, 4, 0:128]))
        ve.wait_ge(sV.h, sV.n)
        for j in range(16):
            lt = self.LT[:, j, :]
            sV.inc(ve.max(self.SRT, lt))
            ve.wait_ge(sV.h, sV.n)
            sV.inc(ve.tensor_scalar_mul(self.GS1, self.SRT[:, 0:1], -1.0))
            se.wait_ge(sV.h, sV.n)
            sS.inc(se.activation(self.GS3, lt, AF.Exp, bias=self.GS1))
            sS.inc(se.activation(self.GS2, self.SRT[:, 1:2], AF.Exp, bias=self.GS1))
            ve.wait_ge(sS.h, sS.n)
            sV.inc(ve.tensor_scalar_add(self.GS2, self.GS2, 1.0))
            ve.wait_ge(sV.h, sV.n)
            sV.inc(ve.reciprocal(self.GS2, self.GS2))
            sV.inc(ve.tensor_single_scalar(self.GS4, lt, self.SRT[:, 1:2], ALU.is_ge))
            ve.wait_ge(sV.h, sV.n)
            sV.inc(ve.tensor_mul(self.CT[:, j, :], self.GS3, self.GS4))
            ve.wait_ge(sV.h, sV.n)
            sV.inc(ve.tensor_scalar_mul(self.CT[:, j, :], self.CT[:, j, :], self.GS2))
            ve.wait_ge(sV.h, sV.n)
        v_ct = sV.n
        te.wait_ge(sV.h, v_ct)
        for w in range(4):
            for k in range(4):
                j = 4 * w + k
                sT.inc(te.matmul(PS[0:8, 5, 128 * k:128 * (k + 1)],
                                 self.CT[:, j, :], self.IDENTF,
                                 is_transpose=True, start=(k == 0), stop=(k == 3)))
            ve.wait_ge(sT.h, sT.n)
            sV.inc(ve.tensor_copy(self.CSB[0:8, 512 * w:512 * (w + 1)], PS[0:8, 5, :]))
            te.wait_ge(sV.h, sV.n)
        self.v_csb = sV.n
        self.s_x1bf_all = max(s_x1bf)

    def moe(self):
        nc = self.nc
        te, ve, se, sy, ge = nc.tensor, nc.vector, nc.scalar, nc.sync, nc.gpsimd
        PS, Y1 = self.PS, self.Y1
        sT, sV, sS = self.sT, self.sV, self.sS

        self.barrier()
        se.preload_act_table(AF.Silu)
        mw1_r = self.d_mw1.rearrange("e (c p) n -> e p c n", p=128)
        self.cW1E.inc(ge.dma_start(out=self.W1EB, in_=mw1_r[0]), 16)
        def w2_dma(e, m):
            self.cW2E[m % 2].inc(ge.dma_start(
                out=self.W2E[:, m % 2, :, :],
                in_=self.d_mw2[e, :, 128 * m:128 * (m + 1)]
                    .rearrange("(a p) n -> p a n", p=128)), 16)
        w2_dma(0, 0)
        w2_dma(0, 1)

        v_y4 = [[0] * NTT for _ in range(DC)]
        v_bev = [0, 0]
        t_w2read = [0, 0]
        t_w1_end = 0
        v_a = [0] * NTT
        for e in range(E):
            if e == 0:
                te.wait_ge(sV.h, self.v_csb)
                te.wait_ge(self.cSEL.h, 16)
            else:
                te.wait_ge(sV.h, v_bev[0])
            t_cb = [0] * NTT
            for tt in range(NTT):
                sT.inc(te.matmul(PS[:, tt, :], self.SEL8[:, 128 * e:128 * (e + 1)],
                                 self.CSB[0:8, TT * tt:TT * (tt + 1)],
                                 start=True, stop=True))
                t_cb[tt] = sT.n
            for tt in range(NTT):
                ve.wait_ge(sT.h, t_cb[tt])
                sV.inc(ve.tensor_copy(self.CBC[:, TT * tt:TT * (tt + 1)], PS[:, tt, :]))
            v_cbc = sV.n
            te.wait_ge(sV.h, v_cbc)

            te.wait_ge(self.cW1E.h, 16 * (e + 1))
            if e > 0:
                te.wait_ge(sV.h, v_bev[1])
            s_sg4 = getattr(self, "_moe_ssg4", [0] * NTT)
            for jj in range(DC):
                mm = None
                for dc in range(DC):
                    for tt in range(NTT):
                        if dc == 0:
                            te.wait_ge(sS.h, s_sg4[tt])
                        mm = te.matmul(PS[:, tt, :], self.W1EB[:, dc, 128 * jj:128 * (jj + 1)],
                                       self.X1BF[:, dc, TT * tt:TT * (tt + 1)],
                                       start=(dc == 0), stop=(dc == DC - 1))
                sT.inc(mm)
                t_g = sT.n
                mm = None
                for dc in range(DC):
                    for tt in range(NTT):
                        if dc == 0:
                            te.wait_ge(sV.h, v_a[tt])
                        mm = te.matmul(PS[:, 4 + tt, :],
                                       self.W1EB[:, dc, 1024 + 128 * jj:1024 + 128 * (jj + 1)],
                                       self.X1BF[:, dc, TT * tt:TT * (tt + 1)],
                                       start=(dc == 0), stop=(dc == DC - 1))
                sT.inc(mm)
                t_u = sT.n
                for tt in range(NTT):
                    kb = tt % 2
                    se.wait_ge(sT.h, t_g)
                    se.wait_ge(sV.h, v_a[tt - 2] if tt >= 2 else v_a[tt + 2])
                    sS.inc(se.activation(self.SGB[:, kb, :], PS[:, tt, :], AF.Silu))
                    s_sg4[tt] = sS.n
                    ve.wait_ge(sS.h, s_sg4[tt])
                    ve.wait_ge(sT.h, t_u)
                    sV.inc(ve.tensor_mul(self.ABF[:, jj, TT * tt:TT * (tt + 1)],
                                         self.SGB[:, kb, :], PS[:, 4 + tt, :]))
                    ve.wait_ge(sV.h, sV.n)
                    sV.inc(ve.tensor_mul(self.ABF[:, jj, TT * tt:TT * (tt + 1)],
                                         self.ABF[:, jj, TT * tt:TT * (tt + 1)],
                                         self.CBC[:, TT * tt:TT * (tt + 1)]))
                    v_a[tt] = sV.n
            self._moe_ssg4 = s_sg4
            t_w1_end = sT.n
            if e + 1 < E:
                ge.wait_ge(sT.h, t_w1_end)
                self.cW1E.inc(ge.dma_start(out=self.W1EB, in_=mw1_r[e + 1]), 16)

            te.wait_ge(sS.h, max(s_sg4))
            te.wait_ge(sV.h, max(v_a))
            for m in range(DC):
                bk = (m % 2) * 4
                te.wait_ge(self.cW2E[m % 2].h, self.cW2E[m % 2].n)
                if m >= 2 or e > 0:
                    te.wait_ge(sV.h, v_bev[m % 2])
                mm = None
                for dffc in range(DC):
                    for tt in range(NTT):
                        mm = te.matmul(PS[:, bk + tt, :], self.W2E[:, m % 2, dffc, :],
                                       self.ABF[:, dffc, TT * tt:TT * (tt + 1)],
                                       start=(dffc == 0), stop=(dffc == DC - 1))
                sT.inc(mm)
                t_w2read[m % 2] = sT.n
                ne, nm = (e, m + 2) if m + 2 < DC else (e + 1, m + 2 - DC)
                if ne < E:
                    ge.wait_ge(sT.h, t_w2read[m % 2])
                    w2_dma(ne, nm)
                ve.wait_ge(sT.h, t_w2read[m % 2])
                ev = None
                for tt in range(NTT):
                    ve.wait_ge(sV.h, v_y4[m][tt])
                    ev = ve.tensor_add(Y1[:, m, TT * tt:TT * (tt + 1)],
                                       Y1[:, m, TT * tt:TT * (tt + 1)],
                                       PS[:, bk + tt, :])
                    sV.inc(ev)
                    v_y4[m][tt] = sV.n
                v_bev[m % 2] = sV.n

    def lnd_out(self):
        nc = self.nc
        te, ve, se, sy = nc.tensor, nc.vector, nc.scalar, nc.sync
        PS, Y1 = self.PS, self.Y1
        sT, sV, sS, sO = self.sT, self.sV, self.sS, self.sO

        self.barrier()
        self.preload_set("natural_log_exp_and_others")
        self.ln_stats(Y1)
        v_rows = self.ln_rows(self.ROW2A, self.ROW2B, self.ROW2T)
        t_bc = self.ln_bcast(self.ROW2T, self.ROW2A, v_rows)
        outr = self.d_out.rearrange("(c p) n -> p c n", p=128)
        for dc in range(DC):
            k = dc % 2
            ve.wait_ge(sT.h, t_bc)
            if dc >= 2:
                ve.wait_ge(self.cOUT[k].h, self.cOUT[k].n)
            sV.inc(ve.tensor_sub(self.OSTG[:, k, :], Y1[:, dc, :], PS[:, 0:4, :]))
            ve.wait_ge(sV.h, sV.n)
            sV.inc(ve.tensor_mul(self.OSTG[:, k, :], self.OSTG[:, k, :], PS[:, 4:8, :]))
            sy.wait_ge(sV.h, sV.n)
            self.cOUT[k].inc(sy.dma_start(out=outr[:, dc, :], in_=self.OSTG[:, k, :]), 16)
        sy.wait_ge(self.cOUT[0].h, self.cOUT[0].n)
        sy.wait_ge(self.cOUT[1].h, self.cOUT[1].n)


_NC_CACHE = {}


def get_nc(stage="full"):
    if stage not in _NC_CACHE:
        _NC_CACHE[stage] = Builder(stage).nc
    return _NC_CACHE[stage]


def make_masks(core):
    t0 = 2048 * (core % 4)
    jj = np.arange(128)[:, None]
    ii = np.arange(256)[None, :]
    bm = np.zeros((6, 128, 256), np.float32)
    for b in range(6):
        j = 128 * b + jj
        bm[b] = ((ii <= j) & (j <= ii + 512)).astype(np.float32)
    em0 = np.zeros_like(bm)
    em1 = np.zeros_like(bm)
    for b in range(6):
        j = 128 * b + jj
        k0 = t0 - 256 + j
        k7 = t0 + 1792 - 256 + j
        em0[b] = bm[b] * ((k0 >= 0) & (k0 < S))
        em1[b] = bm[b] * ((k7 >= 0) & (k7 < S))
    bf = ml_dtypes.bfloat16
    return (bm[[0, 1, 4, 5]].astype(bf), em0.astype(bf), em1.astype(bf))


def make_inputs(core, X):
    b, t0 = core // 4, 2048 * (core % 4)
    src = np.asarray(X["L0_attn_src"][b], np.float32)
    pad = np.zeros((TH, D), np.float32)
    lo, hi = max(0, t0 - HALO), min(S, t0 + T + HALO)
    pad[lo - (t0 - HALO):hi - (t0 - HALO)] = src[lo:hi]
    srcT = np.ascontiguousarray(pad.T)
    bm, em0, em1 = make_masks(core)
    tmask = np.stack([bm[0][:, 0:128], bm[1][:, 128:256],
                      bm[2][:, 0:128], bm[3][:, 128:256]])
    vec6 = np.stack([
        np.asarray(X["L0_attn_b2"], np.float32).reshape(8, 128).T,
        np.asarray(X["L0_attn_ln1_g"], np.float32).reshape(8, 128).T,
        np.asarray(X["L0_attn_ln1_b"], np.float32).reshape(8, 128).T,
        np.asarray(X["L0_attn_ln2_g"], np.float32).reshape(8, 128).T,
        np.asarray(X["L0_attn_ln2_b"], np.float32).reshape(8, 128).T,
    ])
    return {
        "srcT": srcT,
        "w1": np.ascontiguousarray(np.asarray(X["L0_attn_W1"], np.float32)),
        "w2": np.ascontiguousarray(np.asarray(X["L0_attn_W2"], np.float32)),
        "b1T": np.ascontiguousarray(np.asarray(X["L0_attn_b1"], np.float32).reshape(32, 128).T),
        "vec6": np.ascontiguousarray(vec6),
        "bg": np.asarray(X["L0_ffn_bg"], np.float32).reshape(8, 1),
        "wgT": np.ascontiguousarray(np.asarray(X["L0_ffn_Wg"], np.float32).T),
        "mw1": np.ascontiguousarray(np.asarray(X["L0_ffn_w1"], np.float32).transpose(0, 2, 1)),
        "mw2": np.ascontiguousarray(np.asarray(X["L0_ffn_w2"], np.float32).transpose(0, 2, 1)),
        "bmask": np.ascontiguousarray(bm),
        "tmask": np.ascontiguousarray(tmask),
        "em0": np.ascontiguousarray(em0),
        "em1": np.ascontiguousarray(em1),
        "identm": np.eye(128, dtype=ml_dtypes.bfloat16),
        "sel8": np.ascontiguousarray(np.kron(np.eye(8, dtype=np.float32), np.ones((1, 128), np.float32)).astype(ml_dtypes.bfloat16)),
    }


def kernel(**inputs):
    nc = get_nc("full")
    in_maps = [make_inputs(c, inputs) for c in range(NCORES)]
    res = run_bass_kernel_spmd(nc, in_maps, core_ids=list(range(NCORES)))
    out = np.empty((2, S, D), np.float32)
    for c in range(NCORES):
        b, t0 = c // 4, 2048 * (c % 4)
        out[b, t0:t0 + T] = np.asarray(res.results[c]["outT"]).T
    return out


# version 08
# speedup vs baseline: 1.1862x; 1.0070x over previous
import numpy as np
import ml_dtypes
import concourse.bass as bass
import concourse.mybir as mybir
from concourse.bass_utils import run_bass_kernel_spmd

F32 = mybir.dt.float32
BF16 = mybir.dt.bfloat16
AF = mybir.ActivationFunctionType
ALU = mybir.AluOpType
AX = mybir.AxisListType

NCORES = 8
T = 2048
HALO = 256
TH = T + 2 * HALO
D = 1024
DC = 8
H = 16
DH = 64
NG = T // 256
S = 8192
DFF = 4096
NQ = 4
NTT = 4
TT = 512
E = 8
MDFF = 2048
LN_EPS = 1e-5
NARROW = True

SBUF_BYTES = 229376 - 64
SBUF_BASE = 16512

OFF_Y1 = SBUF_BASE
OFF_VEC = SBUF_BASE + 65536
PHASE = SBUF_BASE + 69632

OFF_SRCB = PHASE
OFF_VTOK = OFF_SRCB + 40960
OFF_BMASK = OFF_VTOK + 41600
OFF_EM0 = OFF_BMASK + 2048
OFF_EM1 = OFF_EM0 + 3072
OFF_ESB = OFF_EM1 + 3072
OFF_PVS = OFF_ESB + 9216
OFF_RDEN = OFF_PVS + 2048
OFF_PVN = OFF_RDEN + 1024
OFF_W1QA = SBUF_BYTES - 16384
OFF_STB = SBUF_BASE + 180224
OFF_STB2 = OFF_STB + 4096
OFF_LNR = OFF_STB2 + 4096

OFF_ROWA = PHASE
OFF_ROWB = OFF_ROWA + 8192
OFF_ROWT = OFF_ROWB + 8192
OFF_UBF = PHASE + 24576
OFF_HQ = OFF_UBF + 32768
OFF_W1QB = OFF_HQ + 32768
OFF_W2S = OFF_W1QB + 16384
OFF_SRC2 = OFF_UBF

OFF_X1BF = OFF_UBF
OFF_ABF = OFF_HQ
OFF_W1E = OFF_HQ + 32768
OFF_CSB = PHASE
OFF_CB = PHASE + 4096
OFF_SEL = PHASE + 12288
OFF_SG = PHASE + 14336
OFF_W2E = PHASE + 16384
OFF_ROW2A = OFF_W1E
OFF_ROW2B = OFF_W1E + 8192
OFF_ROW2T = OFF_W1E + 16384
OFF_LG = OFF_W1E + 24576
OFF_OSTG = OFF_X1BF

VEC = {}
_vo = OFF_VEC
def _vecslot(name, nbytes, align=32):
    global _vo
    _vo = (_vo + align - 1) // align * align
    VEC[name] = _vo
    _vo += nbytes
_vecslot("b1T", 32 * 4)
_vecslot("b2T", 8 * 4)
_vecslot("g1T", 8 * 4)
_vecslot("b1lT", 8 * 4)
_vecslot("g2T", 8 * 4)
_vecslot("b2lT", 8 * 4)
_vecslot("bg", 4)
_vecslot("negM", 4)
_vecslot("mrow", 4)
_vecslot("wg", 64 * 4)
_vecslot("wghi", 64 * 2)
_vecslot("wglo", 64 * 2)
_vecslot("ident", 128 * 2)
_vecslot("ones_bf", 2)
_vecslot("onesr_bf", 64 * 2)
_vecslot("lt", 16 * 8 * 4)
_vecslot("ct", 16 * 8 * 4)
_vecslot("srt", 8 * 4)
_vecslot("gs1", 4)
_vecslot("gs2", 4)
_vecslot("gs3", 8 * 4)
_vecslot("gs4", 8 * 4)
_vecslot("onesf", 128 * 4)
_vecslot("identf", 128 * 4)
_vecslot("eps1", 4)
_vecslot("cz", 4)
_vecslot("co", 4)
_vecslot("cn1", 4)
_vecslot("cnh", 4)
assert _vo <= PHASE, _vo
assert PHASE + 24576 + 32768 + 32768 + 16384 + 4096 <= OFF_W1QA


class Ctr:
    def __init__(self, nc, name):
        self.h = nc.alloc_semaphore(name)
        self.n = 0

    def inc(self, instr, k=1):
        instr.then_inc(self.h, k)
        self.n += k
        return self.n


class Builder:
    def __init__(self, stage="full"):
        self.stage = stage
        nc = self.nc = bass.Bass("TRN2", target_bir_lowering=False)
        assert nc.SBUF_PARTITION_SIZE_BYTES >= SBUF_BYTES
        assert nc.sbuf_base <= SBUF_BASE, nc.sbuf_base

        def dram(name, shape, kind="ExternalInput", dt=F32):
            return nc.dram_tensor(name, shape, dt, kind=kind).ap()

        self.d_srcT = dram("srcT", [D, TH])
        self.d_w1 = dram("w1", [D, DFF])
        self.d_w2 = dram("w2", [DFF, D])
        self.d_b1T = dram("b1T", [128, 32])
        self.d_vec6 = dram("vec6", [5, 128, 8])
        self.d_bg = dram("bg", [8, 1])
        self.d_wgT = dram("wgT", [D, 8])
        self.d_mw1 = dram("mw1", [E, D, MDFF])
        self.d_mw2 = dram("mw2", [E, D, D])
        self.d_bmask = dram("bmask", [4, 128, 256], dt=BF16)
        self.d_em0 = dram("em0", [6, 128, 256], dt=BF16)
        self.d_em1 = dram("em1", [6, 128, 256], dt=BF16)
        self.d_ident = dram("identm", [128, 128], dt=BF16)
        self.d_sel8 = dram("sel8", [8, 1024], dt=BF16)
        self.d_tmask = dram("tmask", [4, 128, 128], dt=BF16)
        self.d_out = dram("outT", [D, T], kind="ExternalOutput")

        sb = lambda name, shape, dt, off: nc.alloc_sbuf_tensor_at(name, shape, dt, offset=off).ap()
        self.Y1 = sb("y1", [128, DC, T], F32, OFF_Y1)
        self.SRCB = sb("srcb", [128, DC, TH], BF16, OFF_SRCB)
        self.VTOK = sb("vtok", [128, 20, H, 65], BF16, OFF_VTOK)
        self.BMASK = sb("bmask_s", [128, 4, 256], BF16, OFF_BMASK)
        self.EM0 = sb("em0_s", [128, 6, 256], BF16, OFF_EM0)
        self.EM1 = sb("em1_s", [128, 6, 256], BF16, OFF_EM1)
        self.ESB = sb("esb", [128, 3, 1536], BF16, OFF_ESB)
        self.PVS = sb("pvs", [65, 2, 256], F32, OFF_PVS)
        self.RDEN = sb("rden", [128, 2, 256], BF16, OFF_RDEN)
        self.PVN = sb("pvn", [64, 2, 256], BF16, OFF_PVN)
        self.W1QA = sb("w1qa", [128, DC, 1024], BF16, OFF_W1QA)
        self.W1QB = sb("w1qb", [128, DC, 1024], BF16, OFF_W1QB)
        self.W2S = sb("w2s", [128, 2, DC, 128], BF16, OFF_W2S)
        self.ROWA = sb("rowa", [1, T], F32, OFF_ROWA)
        self.ROWB = sb("rowb", [1, T], F32, OFF_ROWB)
        self.ROWT = sb("rowt", [1, T], F32, OFF_ROWT)
        self.UBF = sb("ubf", [128, DC, T], BF16, OFF_UBF)
        self.HQ = sb("hq", [128, DC, T], BF16, OFF_HQ)
        self.SRC2 = sb("src2", [128, DC, T], F32, OFF_SRC2)
        self.X1BF = sb("x1bf", [128, DC, T], BF16, OFF_X1BF)
        self.LOBF = sb("lobf", [128, DC, T], BF16, OFF_ABF)
        self.ABF = sb("abf", [128, DC, T], BF16, OFF_ABF)
        self.W1EB = sb("w1e", [128, DC, MDFF], BF16, OFF_W1E)
        self.CSB = sb("csb", [8, T], BF16, OFF_CSB)
        self.SEL8 = sb("sel8_s", [8, 1024], BF16, OFF_SEL)
        self.TMASK = sb("tmask_s", [128, 4, 128], BF16, OFF_LNR + 3072)
        self.CBC = sb("cb", [128, T], BF16, OFF_CB)
        self.SGB = sb("sgb", [128, 2, TT], BF16, OFF_SG)
        self.W2E = sb("w2e", [128, 2, DC, 128], BF16, OFF_W2E)
        self.ROW2A = sb("row2a", [1, T], F32, OFF_ROW2A)
        self.ROW2B = sb("row2b", [1, T], F32, OFF_ROW2B)
        self.ROW2T = sb("row2t", [1, T], F32, OFF_ROW2T)
        self.LGS = sb("lg", [8, T], F32, OFF_LG)
        self.OSTG = sb("ostg", [128, 2, T], F32, OFF_OSTG)

        self.B1T = sb("b1t_s", [128, 32], F32, VEC["b1T"])
        self.B2T = sb("b2t_s", [128, 8], F32, VEC["b2T"])
        self.G1T = sb("g1t_s", [128, 8], F32, VEC["g1T"])
        self.B1LT = sb("b1lt_s", [128, 8], F32, VEC["b1lT"])
        self.G2T = sb("g2t_s", [128, 8], F32, VEC["g2T"])
        self.B2LT = sb("b2lt_s", [128, 8], F32, VEC["b2lT"])
        self.BG = sb("bg_s", [8, 1], F32, VEC["bg"])
        self.NEGM = sb("negm_s", [128, 1], F32, VEC["negM"])
        self.MROW = sb("mrow_s", [128, 1], F32, VEC["mrow"])
        self.WG = sb("wg_s", [128, DC, 8], F32, VEC["wg"])
        self.WGHI = sb("wghi_s", [128, DC, 8], BF16, VEC["wghi"])
        self.WGLO = sb("wglo_s", [128, DC, 8], BF16, VEC["wglo"])
        self.IDENT = sb("ident_s", [128, 128], BF16, VEC["ident"])
        self.ONES = sb("ones_s", [128, 1], BF16, VEC["ones_bf"])
        self.ONESR = sb("onesr_s", [128, 64], BF16, VEC["onesr_bf"])
        self.LT = sb("lt_s", [128, 16, 8], F32, VEC["lt"])
        self.CT = sb("ct_s", [128, 16, 8], F32, VEC["ct"])
        self.SRT = sb("srt_s", [128, 8], F32, VEC["srt"])
        self.GS1 = sb("gs1_s", [128, 1], F32, VEC["gs1"])
        self.GS2 = sb("gs2_s", [128, 1], F32, VEC["gs2"])
        self.GS3 = sb("gs3_s", [128, 8], F32, VEC["gs3"])
        self.GS4 = sb("gs4_s", [128, 8], F32, VEC["gs4"])
        self.ONESF = sb("onesf_s", [1, 128], F32, VEC["onesf"])
        self.IDENTF = sb("identf_s", [128, 128], F32, VEC["identf"])
        self.STB = sb("stb_s", [128, 4, 512], BF16, OFF_STB)
        self.STB2 = sb("stb2_s", [128, 4, 512], BF16, OFF_STB2)
        self.LNR = sb("lnr_s", [65, 2, 256], F32, OFF_LNR)
        self.ONED = sb("oned_s", [65, 256], F32, OFF_LNR + 2048)
        self.EPS1 = sb("eps1_s", [1, 1], F32, VEC["eps1"])
        self.CZ = sb("cz_s", [128, 1], F32, VEC["cz"])
        self.CO = sb("co_s", [128, 1], F32, VEC["co"])
        nc.const_aps.register(F32, 0.0, self.CZ)
        nc.const_aps.register(F32, 1.0, self.CO)
        self.CN1 = sb("cn1_s", [128, 1], F32, VEC["cn1"])
        self.CNH = sb("cnh_s", [128, 1], F32, VEC["cnh"])
        nc.const_aps.register(F32, -1.0, self.CN1)
        nc.const_aps.register(F32, -0.5, self.CNH)

        self.PS = nc.alloc_psum_tensor("ps", [128, 8, 512], F32).ap()
        self.PSB = self.PS.bitcast(BF16)

        self.sT = Ctr(nc, "c_st")
        self.sV = Ctr(nc, "c_sv")
        self.sS = Ctr(nc, "c_ss")
        self.sP = Ctr(nc, "c_sp")
        self.sO = Ctr(nc, "c_so")
        self.cY1 = Ctr(nc, "c_y1")
        self.cCST = Ctr(nc, "c_cst")
        self.cSRCB = Ctr(nc, "c_srcb")
        self.cW1A = Ctr(nc, "c_w1a")
        self.cW1B = Ctr(nc, "c_w1b")
        self.cW2S = [Ctr(nc, "c_w2s0"), Ctr(nc, "c_w2s1")]
        self.cSRC2 = Ctr(nc, "c_src2")
        self.cW1E = Ctr(nc, "c_w1e")
        self.cSEL = Ctr(nc, "c_sel")
        self.cOUT = [Ctr(nc, "c_out0"), Ctr(nc, "c_out1")]
        self.cW2E = [Ctr(nc, "c_w2e0"), Ctr(nc, "c_w2e1")]

        self.build()

    def barrier(self):
        self.nc.all_engine_barrier()

    def preload_set(self, name):
        from concourse.hw_specs import get_activation_tables
        names = list(get_activation_tables(self.nc.m.arch).keys())
        idx = names.index(name)
        se = self.nc.scalar
        se.add_instruction(mybir.InstLoadActFuncSet(
            name=self.nc.get_next_instruction_name(), act_func_set_id=idx,
            ins=[], outs=[]))

    def build(self):
        nc = self.nc
        self.prologue()
        self.attn()
        if self.stage == "attn":
            self.emit_out(self.Y1)
            return
        self.lna_ffn()
        if self.stage == "ffn":
            self.emit_out(self.Y1)
            return
        self.lnb_lnc_gating()
        self.moe()
        self.lnd_out()

    def emit_out(self, plane):
        nc, sy = self.nc, self.nc.sync
        self.barrier()
        outr = self.d_out.rearrange("(c p) n -> p c n", p=128)
        for dc in range(DC):
            self.sO.inc(sy.dma_start(out=outr[:, dc, :], in_=plane[:, dc, :]), 16)
        sy.wait_ge(self.sO.h, self.sO.n)

    def prologue(self):
        nc = self.nc
        te, ve, se, sy, ge = nc.tensor, nc.vector, nc.scalar, nc.sync, nc.gpsimd
        src_r = self.d_srcT.rearrange("(c p) n -> p c n", p=128)
        self.cY1.inc(sy.dma_start(out=self.Y1, in_=src_r[:, :, HALO:HALO + T]), 16)
        self.cCST.inc(sy.dma_start(out=self.BMASK, in_=self.d_bmask.rearrange("c p n -> p c n")), 16)
        self.cCST.inc(sy.dma_start(out=self.EM0, in_=self.d_em0.rearrange("c p n -> p c n")), 16)
        self.cCST.inc(sy.dma_start(out=self.EM1, in_=self.d_em1.rearrange("c p n -> p c n")), 16)
        self.cCST.inc(sy.dma_start(out=self.IDENT, in_=self.d_ident), 16)
        self.cCST.inc(sy.dma_start(out=self.B1T, in_=self.d_b1T), 16)
        for i, t in enumerate([self.B2T, self.G1T, self.B1LT, self.G2T, self.B2LT]):
            self.cCST.inc(sy.dma_start(out=t, in_=self.d_vec6[i]), 16)
        self.cCST.inc(sy.dma_start(out=self.BG, in_=self.d_bg), 16)
        self.cCST.inc(sy.dma_start(out=self.TMASK, in_=self.d_tmask.rearrange("c p n -> p c n")), 16)
        self.cCST.inc(sy.dma_start(out=self.WG, in_=self.d_wgT.rearrange("(c p) e -> p c e", p=128)), 16)
        self.cSRCB.inc(ge.dma_start(out=self.SRCB, in_=src_r), 16)
        self.w1_r = self.d_w1.rearrange("(c p) (q n) -> q p c n", p=128, n=1024)
        self.cW1A.inc(ge.dma_start(out=self.W1QA, in_=self.w1_r[0]), 16)

        ve.memset(self.ONES, 1.0)
        ve.memset(self.ONESR[64:65, :], 1.0)
        ve.memset(self.ONED[64:65, :], 1.0)
        ve.memset(self.ONESF, 1.0)
        ve.memset(self.EPS1, float(LN_EPS))
        ve.memzero(self.CZ)
        ve.memset(self.CO, 1.0)
        ve.memset(self.CN1, -1.0)
        ve.memset(self.CNH, -0.5)
        ve.wait_ge(self.cCST.h, self.cCST.n)
        ve.tensor_copy(self.IDENTF, self.IDENT)
        self.sV.inc(ve.memset(self.VTOK[:, :, :, 64:65], 1.0))
        self.v_ones = self.sV.n

    def attn(self):
        nc = self.nc
        te, ve, se, ge = nc.tensor, nc.vector, nc.scalar, nc.gpsimd
        PS, SRCB, ESB, VTOK, PVS, RDEN, Y1 = (self.PS, self.SRCB, self.ESB, self.VTOK,
                                              self.PVS, self.RDEN, self.Y1)
        sT, sV, sS, sP = self.sT, self.sV, self.sS, self.sP

        self.preload_set("natural_log_exp_and_others")

        te.wait_ge(self.cSRCB.h, self.cSRCB.n)
        te.wait_ge(self.cCST.h, self.cCST.n)
        ve.wait_ge(self.cCST.h, self.cCST.n)
        sT.inc(te.matmul(PS[:, 0, 0:320], SRCB[0:64, 0, 256:384],
                         SRCB[0:64, 0, 0:2560:8], start=True, stop=True))
        ve.wait_ge(sT.h, sT.n)
        sV.inc(ve.tensor_reduce(self.MROW, PS[:, 0, 0:320], AX.X, ALU.max))
        te.wait_ge(sV.h, sV.n)
        sT.inc(te.matmul(PS[0:1, 1, 0:128], self.MROW, self.IDENTF,
                         is_transpose=True, start=True, stop=True))
        ve.wait_ge(sT.h, sT.n)
        sV.inc(ve.tensor_reduce(self.MROW[0:1, :], PS[0:1, 1, 0:128], AX.X, ALU.max))
        te.wait_ge(sV.h, sV.n)
        sT.inc(te.matmul(PS[:, 1, 0:1], self.ONESF, self.MROW[0:1, :],
                         start=True, stop=True))
        ve.wait_ge(sT.h, sT.n)
        sV.inc(ve.tensor_scalar(self.NEGM, PS[:, 1, 0:1], -1.0, -8.0, ALU.mult, ALU.add))
        v_negm = sV.n
        se.wait_ge(sV.h, v_negm)
        te.wait_ge(sV.h, v_negm)

        NI = NG * H
        t_qk = [0] * NI
        t_pv = [0] * NI
        t_bc = [0] * NI
        t_sh = [0] * NI
        s_exp = [0] * NI
        v_cp = [0] * NI
        v_msk = [0] * NI
        s_rec = [0] * NI
        v_mul = [0] * NI
        v_acc = [0] * NI
        built = set()

        def chunk_cols(gg, b):
            if not NARROW or gg in (0, NG - 1):
                return 0, 256
            if b == 0:
                return 0, 128
            if b == 5:
                return 128, 256
            return 0, 256

        ve.wait_ge(self.cY1.h, self.cY1.n)
        ve.wait_ge(sV.h, self.v_ones)

        def build_vhat(kcs):
            for kc in kcs:
                if kc in built:
                    continue
                built.add(kc)
                for wave in range(2):
                    for k in range(4):
                        dc = 4 * wave + k
                        sT.inc(te.matmul(self.PSB[:, 0, 128 * k:128 * (k + 1)],
                                         SRCB[:, dc, 128 * kc:128 * (kc + 1)],
                                         self.IDENT, is_transpose=True,
                                         start=(k == 0), stop=(k == 3)))
                    ve.wait_ge(sT.h, sT.n)
                    src_ap = self.PSB[:, 0, 0:512] \
                        .rearrange("p (a b) -> p a b", a=4) \
                        .rearrange("p a (c d) -> p (a c) d", c=2)
                    sV.inc(ve.tensor_copy(VTOK[:, kc, 8 * wave:8 * wave + 8, 0:64], src_ap))
                    te.wait_ge(sV.h, sV.n)

        def emit_qk(i):
            gg, h = divmod(i, H)
            dc, r0 = h // 2, 64 * (h % 2)
            par = i % 2
            if i >= 2:
                te.wait_ge(sS.h, s_exp[i - 2])
            mm = None
            for b in range(6):
                kc = 2 * gg + b
                c0, c1 = chunk_cols(gg, b)
                mm = te.matmul(PS[:, 3 * par + b // 2, 256 * (b % 2) + c0:256 * (b % 2) + c1],
                               SRCB[r0:r0 + 64, dc, 128 * kc:128 * kc + 128],
                               SRCB[r0:r0 + 64, dc, 256 + 256 * gg + c0:256 + 256 * gg + c1],
                               start=(b % 2 == 0), stop=(b % 2 == 1))
            sT.inc(mm)
            t_qk[i] = sT.n

        def emit_exp(i):
            par3 = i % 3
            se.wait_ge(sT.h, t_qk[i])
            if i >= 3:
                se.wait_ge(sT.h, t_pv[i - 3])
            sS.inc(se.activation(ESB[:, par3, :], PS[:, 3 * (i % 2):3 * (i % 2) + 3, :],
                                 AF.Exp, bias=self.NEGM))
            s_exp[i] = sS.n

        def emit_mask(i):
            gg = i // H
            par3 = i % 3
            ve.wait_ge(sS.h, s_exp[i])
            mm = None
            if gg == 0:
                for (ec, mc) in ((0, 0), (2, 2), (4, 4)):
                    mm = ve.tensor_mul(ESB[:, par3, 256 * ec:256 * ec + 512],
                                       ESB[:, par3, 256 * ec:256 * ec + 512],
                                       self.EM0[:, mc:mc + 2, :])
            elif gg == NG - 1:
                for (ec, mc) in ((0, 0), (2, 2), (4, 4)):
                    mm = ve.tensor_mul(ESB[:, par3, 256 * ec:256 * ec + 512],
                                       ESB[:, par3, 256 * ec:256 * ec + 512],
                                       self.EM1[:, mc:mc + 2, :])
            elif NARROW:
                tri = bass.AP(ESB.tensor, ESB.offset + par3 * 1536,
                              [[4608, 128], [1024, 2], [384, 2], [1, 128]])
                tm = self.TMASK.rearrange("p (a c) x -> p a c x", a=2)
                mm = ve.tensor_mul(tri, tri, tm)
            else:
                for (ec, mc) in ((0, 0), (4, 2)):
                    mm = ve.tensor_mul(ESB[:, par3, 256 * ec:256 * ec + 512],
                                       ESB[:, par3, 256 * ec:256 * ec + 512],
                                       self.BMASK[:, mc:mc + 2, :])
            sV.inc(mm)
            v_msk[i] = sV.n

        def emit_pv(i):
            gg, h = divmod(i, H)
            par = i % 2
            par3 = i % 3
            te.wait_ge(sV.h, v_msk[i])
            if i >= 2:
                te.wait_ge(sV.h, v_acc[i - 2])
            mm = None
            bs = range(6) if (not NARROW or gg in (0, NG - 1)) else (1, 0, 2, 3, 4, 5)
            for bi, b in enumerate(bs):
                kc = 2 * gg + b
                c0, c1 = chunk_cols(gg, b)
                mm = te.matmul(PS[0:65, 6 + par, c0:c1],
                               VTOK[:, kc, h, :],
                               ESB[:, par3, 256 * b + c0:256 * b + c1],
                               start=(bi == 0), stop=(bi == 5))
            sT.inc(mm)
            t_pv[i] = sT.n

        def emit_copy(i):
            par = i % 2
            se.wait_ge(sT.h, t_pv[i])
            sS.inc(se.copy(PVS[:, par, :], PS[0:65, 6 + par, 0:256]))
            v_cp[i] = sS.n

        def emit_recip(i):
            par = i % 2
            ve.wait_ge(sS.h, v_cp[i])
            if i >= 2:
                ve.wait_ge(sT.h, t_bc[i - 2])
            with nc.allow_low_precision(reason="softmax denom reciprocal feeds a bf16 rank-1 broadcast matmul"):
                sV.inc(ve.reciprocal(RDEN[64:65, par, :], PVS[64:65, par, :]))
            s_rec[i] = sV.n

        def emit_bcast(i):
            par = i % 2
            te.wait_ge(sV.h, s_rec[i])
            sT.inc(te.matmul(PS[0:64, 6 + par, 256:512], self.ONESR[64:65, :],
                             RDEN[64:65, par, :], start=True, stop=True))
            t_bc[i] = sT.n

        def emit_mult(i):
            par = i % 2
            ve.wait_ge(sT.h, t_bc[i])
            sV.inc(ve.tensor_mul(self.PVN[:, par, :], PVS[0:64, par, :],
                                 PS[0:64, 6 + par, 256:512]))
            v_mul[i] = sV.n

        def emit_shift(i):
            par = i % 2
            te.wait_ge(sV.h, v_mul[i])
            sT.inc(te.matmul(PS[64:128, 6 + par, 256:512], self.IDENT[0:64, 0:64],
                             self.PVN[:, par, :], start=True, stop=True))
            t_sh[i] = sT.n

        def emit_add(i):
            gg, h = divmod(i, H)
            dc, r0 = h // 2, 64 * (h % 2)
            par = i % 2
            y = Y1[r0:r0 + 64, dc, 256 * gg:256 * gg + 256]
            if h % 2 == 0:
                ve.wait_ge(sV.h, v_mul[i])
                sV.inc(ve.tensor_add(y, y, self.PVN[:, par, :]))
            else:
                ve.wait_ge(sT.h, t_sh[i])
                sV.inc(ve.tensor_add(y, y, PS[64:128, 6 + par, 256:512]))
            v_acc[i] = sV.n

        for i in range(NI + 4):
            j4 = i - 4
            if 0 <= j4 < NI:
                emit_mult(j4)
                if j4 % 2 == 1:
                    emit_shift(j4)
                emit_add(j4)
            if i < NI:
                if i % H == 0:
                    gg = i // H
                    kcs = list(range(6)) if gg == 0 else [2 * gg + 4, 2 * gg + 5]
                    if i >= 2:
                        te.wait_ge(sS.h, s_exp[i - 1])
                    build_vhat(kcs)
                emit_qk(i)
            if 0 <= i - 3 < NI:
                emit_copy(i - 3)
            if i < NI:
                emit_exp(i)
            if 0 <= i - 1 < NI:
                emit_mask(i - 1)
            if 0 <= i - 2 < NI:
                emit_pv(i - 2)
            if 0 <= i - 3 < NI:
                emit_recip(i - 3)
                emit_bcast(i - 3)

    def ln_stats(self, plane, gate=None):
        nc = self.nc
        te, ve, se = nc.tensor, nc.vector, nc.scalar
        sT, sV, sS = self.sT, self.sV, self.sS
        last_mm = [0, 0, 0, 0]
        for dc in range(DC):
            for tt in range(NTT):
                k = (dc * NTT + tt) % 4
                tile = plane[:, dc, TT * tt:TT * (tt + 1)]
                if gate is not None:
                    for (h, n) in gate(dc, tt):
                        ve.wait_ge(h, n)
                        se.wait_ge(h, n)
                        te.wait_ge(h, n)
                ve.wait_ge(sT.h, last_mm[k])
                se.wait_ge(sT.h, last_mm[k])
                sV.inc(ve.tensor_copy(self.STB[:, k, :], tile))
                sS.inc(se.activation(self.STB2[:, k, :], tile, AF.Square))
                te.wait_ge(sV.h, sV.n)
                te.wait_ge(sS.h, sS.n)
                te.matmul(self.PS[0:1, tt, :], self.ONES, self.STB[:, k, :],
                          start=(dc == 0), stop=(dc == DC - 1))
                sT.inc(te.matmul(self.PS[0:1, 4 + tt, :], self.ONES, self.STB2[:, k, :],
                                 start=(dc == 0), stop=(dc == DC - 1)))
                last_mm[k] = sT.n
        return sT.n

    def ln_rows(self, rowA, rowB, rowT):
        nc = self.nc
        ve, se = nc.vector, nc.scalar
        sV, sS = self.sV, self.sS
        sumv = self.PS[0:1, 0:4, :]
        ssv = self.PS[0:1, 4:8, :]
        ve.wait_ge(self.sT.h, self.sT.n)
        sV.inc(ve.tensor_scalar_mul(rowT, sumv, 1.0 / D))
        ve.wait_ge(sV.h, sV.n)
        sV.inc(ve.tensor_mul(rowB, rowT, rowT))
        ve.wait_ge(sV.h, sV.n)
        sV.inc(ve.scalar_tensor_tensor(rowA, ssv, 1.0 / D, rowB, ALU.mult, ALU.subtract))
        se.wait_ge(sV.h, sV.n)
        sS.inc(se.activation(rowB, rowA, AF.Ln, bias=self.EPS1))
        se.wait_ge(sS.h, sS.n)
        sS.inc(se.activation(rowA, rowB, AF.Exp, scale=-0.5))
        return sS.n

    def ln_bcast(self, rowT, rowA, s_rows):
        te = self.nc.tensor
        te.wait_ge(self.sS.h, s_rows)
        for tt in range(NTT):
            te.matmul(self.PS[:, tt, :], self.ONESF, rowT[0:1, TT * tt:TT * (tt + 1)],
                      start=True, stop=True)
            self.sT.inc(te.matmul(self.PS[:, 4 + tt, :], self.ONESF,
                                  rowA[0:1, TT * tt:TT * (tt + 1)], start=True, stop=True))
        return self.sT.n

    def ln_apply_pass1(self, plane, t_bc):
        ve, sV = self.nc.vector, self.sV
        for dc in range(DC):
            ve.wait_ge(self.sT.h, t_bc)
            sV.inc(ve.tensor_sub(plane[:, dc, :], plane[:, dc, :], self.PS[:, 0:4, :]))
        return sV.n

    def lna_ffn(self):
        nc = self.nc
        te, ve, se, sy, ge = nc.tensor, nc.vector, nc.scalar, nc.sync, nc.gpsimd
        PS, Y1, UBF, HQ = self.PS, self.Y1, self.UBF, self.HQ
        sT, sV, sS = self.sT, self.sV, self.sS

        self.barrier()
        t_stats_a = self.ln_stats(Y1)
        s_rows = self.ln_rows(self.ROWA, self.ROWB, self.ROWT)
        t_bc = self.ln_bcast(self.ROWT, self.ROWA, s_rows)
        v_p1 = self.ln_apply_pass1(Y1, t_bc)
        v_apply = [0] * DC
        s_cast = [0] * DC
        ve.wait_ge(sV.h, v_p1)
        for dc in range(DC):
            sV.inc(ve.tensor_mul(Y1[:, dc, :], Y1[:, dc, :], PS[:, 4:8, :]))
            t1v = sV.n
            se.wait_ge(sV.h, t1v)
            sS.inc(se.activation(UBF[:, dc, :], Y1[:, dc, :], AF.Identity,
                                 scale=self.G1T[:, dc:dc + 1], bias=self.B1LT[:, dc:dc + 1]))
            s_cast[dc] = sS.n
            ve.wait_ge(sS.h, s_cast[dc])
            sV.inc(ve.tensor_scalar(Y1[:, dc, :], Y1[:, dc, :],
                                    self.G1T[:, dc:dc + 1], self.B1LT[:, dc:dc + 1],
                                    ALU.mult, ALU.add))
            v_apply[dc] = sV.n
        v_p2 = v_apply[DC - 1]

        w1buf = [self.W1QA, self.W1QB]
        w1sem = [self.cW1A, self.cW1B]
        v_y2 = [[0] * NTT for _ in range(DC)]
        s_hq = [0] * DC
        t_a = {}
        t_b = {}
        v_bev = [0, 0]
        t_w2read = [0, 0]
        for q in range(NQ):
            wb = w1buf[q % 2]
            wsem = w1sem[q % 2]
            wcnt = 16 * (q // 2 + 1)
            if q + 1 < NQ:
                if q >= 1:
                    ge.wait_ge(sT.h, self.t_a_end)
                w1sem[(q + 1) % 2].inc(ge.dma_start(out=w1buf[(q + 1) % 2],
                                                    in_=self.w1_r[q + 1]), 16)
            if q == 0:
                ge.wait_ge(sT.h, t_stats_a)
                for m0 in range(2):
                    self.cW2S[m0].inc(ge.dma_start(
                        out=self.W2S[:, m0, :, :],
                        in_=self.d_w2[1024 * q:1024 * (q + 1), 128 * m0:128 * (m0 + 1)]
                            .rearrange("(a p) n -> p a n", p=128)), 16)
            te.wait_ge(wsem.h, wcnt)
            if q > 0:
                te.wait_ge(sV.h, v_bev[0])
                te.wait_ge(sV.h, v_bev[1])
            for hc in range(DC):
                bk = (hc % 2) * 4
                if q == 0 and hc == 0:
                    te.wait_ge(sV.h, v_p1)
                if q == 0 and hc == 1:
                    te.wait_ge(sV.h, v_p2)
                if hc >= 2:
                    te.wait_ge(sS.h, s_hq[hc - 2])
                mm = None
                for dc in range(DC):
                    if q == 0 and hc == 0:
                        te.wait_ge(sS.h, s_cast[dc])
                    for tt in range(NTT):
                        mm = te.matmul(PS[:, bk + tt, :], wb[:, dc, 128 * hc:128 * (hc + 1)],
                                       UBF[:, dc, TT * tt:TT * (tt + 1)],
                                       start=(dc == 0), stop=(dc == DC - 1))
                sT.inc(mm)
                t_a[hc] = sT.n
                se.wait_ge(sT.h, t_a[hc])
                a = None
                for tt in range(NTT):
                    a = se.activation(HQ[:, hc, TT * tt:TT * (tt + 1)], PS[:, bk + tt, :],
                                      AF.Relu, bias=self.B1T[:, 8 * q + hc:8 * q + hc + 1])
                sS.inc(a)
                s_hq[hc] = sS.n
            self.t_a_end = t_a[DC - 1]
            for m in range(DC):
                bk = (m % 2) * 4
                te.wait_ge(self.cW2S[m % 2].h, self.cW2S[m % 2].n)
                te.wait_ge(sS.h, s_hq[DC - 1])
                if m >= 2:
                    te.wait_ge(sV.h, v_bev[m % 2])
                mm = None
                for hc in range(DC):
                    for tt in range(NTT):
                        mm = te.matmul(PS[:, bk + tt, :], self.W2S[:, m % 2, hc, :],
                                       HQ[:, hc, TT * tt:TT * (tt + 1)],
                                       start=(hc == 0), stop=(hc == DC - 1))
                sT.inc(mm)
                t_b[m] = sT.n
                t_w2read[m % 2] = sT.n
                nq, nm = (q, m + 2) if m + 2 < DC else (q + 1, m + 2 - DC)
                if nq < NQ:
                    ge.wait_ge(sT.h, t_b[m])
                    self.cW2S[m % 2].inc(ge.dma_start(
                        out=self.W2S[:, m % 2, :, :],
                        in_=self.d_w2[1024 * nq:1024 * (nq + 1), 128 * nm:128 * (nm + 1)]
                            .rearrange("(a p) n -> p a n", p=128)), 16)
                ve.wait_ge(sT.h, t_b[m])
                ev = None
                for tt in range(NTT):
                    ve.wait_ge(sV.h, v_y2[m][tt])
                    if q == 0:
                        ev = ve.scalar_tensor_tensor(Y1[:, m, TT * tt:TT * (tt + 1)],
                                                     PS[:, bk + tt, :],
                                                     self.B2T[:, m:m + 1],
                                                     Y1[:, m, TT * tt:TT * (tt + 1)],
                                                     ALU.add, ALU.add)
                    else:
                        ev = ve.tensor_add(Y1[:, m, TT * tt:TT * (tt + 1)],
                                           Y1[:, m, TT * tt:TT * (tt + 1)],
                                           PS[:, bk + tt, :])
                    sV.inc(ev)
                    v_y2[m][tt] = sV.n
                v_bev[m % 2] = sV.n
        self.v_y2 = v_y2
        self.v_bev_ffn = v_bev
        self.t_ffn_end = sT.n

    def lnb_lnc_gating(self):
        nc = self.nc
        te, ve, se, sy, ge = nc.tensor, nc.vector, nc.scalar, nc.sync, nc.gpsimd
        PS, Y1, SRC2 = self.PS, self.Y1, self.SRC2
        sT, sV, sS = self.sT, self.sV, self.sS

        v_y2 = self.v_y2
        v_bev = self.v_bev_ffn

        def gate_b(dc, tt):
            g = [(sV.h, v_y2[dc][tt])]
            if dc == 0 and tt == 0:
                g.append((sV.h, v_bev[0]))
                g.append((sV.h, v_bev[1]))
            return g
        t_stats = self.ln_stats(Y1, gate=gate_b)
        sy.wait_ge(sT.h, self.t_ffn_end)
        src_r = self.d_srcT.rearrange("(c p) n -> p c n", p=128)
        self.cSRC2.inc(sy.dma_start(out=SRC2, in_=src_r[:, :, HALO:HALO + T]), 16)
        s_rows = self.ln_rows(self.ROWA, self.ROWB, self.ROWT)
        s_rows_b = s_rows
        t_bc = self.ln_bcast(self.ROWT, self.ROWA, s_rows)
        se.wait_ge(self.cSRC2.h, self.cSRC2.n)
        s_bias = [0] * DC
        for dc in range(DC):
            sS.inc(se.add(SRC2[:, dc, :], SRC2[:, dc, :], self.B2LT[:, dc:dc + 1]))
            s_bias[dc] = sS.n
        v_p1 = self.ln_apply_pass1(Y1, t_bc)
        v_y3 = [0] * DC
        ve.wait_ge(sV.h, v_p1)
        for dc in range(DC):
            sV.inc(ve.tensor_mul(Y1[:, dc, :], Y1[:, dc, :], PS[:, 4:8, :]))
            ve.wait_ge(sV.h, sV.n)
            ve.wait_ge(sS.h, s_bias[dc])
            sV.inc(ve.scalar_tensor_tensor(SRC2[:, dc, :], Y1[:, dc, :],
                                           self.G2T[:, dc:dc + 1], SRC2[:, dc, :],
                                           ALU.mult, ALU.add))
            v_y3[dc] = sV.n

        def gate_c(dc, tt):
            g = [(sV.h, v_y3[dc])]
            if dc == 0 and tt == 0:
                g.append((sV.h, v_y3[DC - 1]))
            return g
        t_stats = self.ln_stats(SRC2, gate=gate_c)
        s_rows = self.ln_rows(self.ROW2A, self.ROW2B, self.ROW2T)
        t_bc = self.ln_bcast(self.ROW2T, self.ROW2A, s_rows)
        s_x1bf = [0] * DC
        v_x1 = [0] * DC
        for dc in range(DC):
            ve.wait_ge(sT.h, t_bc)
            sV.inc(ve.tensor_sub(Y1[:, dc, :], SRC2[:, dc, :], PS[:, 0:4, :]))
            ve.wait_ge(sV.h, sV.n)
            sV.inc(ve.tensor_mul(Y1[:, dc, :], Y1[:, dc, :], PS[:, 4:8, :]))
            v_x1[dc] = sV.n
            se.wait_ge(sV.h, v_x1[dc])
            sS.inc(se.copy(self.X1BF[:, dc, :], Y1[:, dc, :]))
            s_x1bf[dc] = sS.n
        self.v_x1 = v_x1
        self.s_x1bf = s_x1bf

        sy.wait_ge(sS.h, s_rows_b)
        self.cSEL.inc(sy.dma_start(out=self.SEL8, in_=self.d_sel8), 16)
        sV.inc(ve.tensor_copy(self.WGHI, self.WG))
        ve.wait_ge(sV.h, sV.n)
        sV.inc(ve.scalar_tensor_tensor(self.WGLO, self.WGHI, -1.0, self.WG,
                                       ALU.mult, ALU.add))
        v_lo = [0] * DC
        for dc in range(DC):
            ve.wait_ge(sS.h, s_x1bf[dc])
            sV.inc(ve.scalar_tensor_tensor(self.LOBF[:, dc, :], self.X1BF[:, dc, :],
                                           -1.0, Y1[:, dc, :], ALU.mult, ALU.add))
            v_lo[dc] = sV.n
        started = [False] * NTT
        mm = None
        for dc in range(DC):
            te.wait_ge(sV.h, v_lo[dc])
            if dc == 0:
                te.wait_ge(sV.h, self.v_x1[DC - 1])
            for (wgt, xin) in ((self.WGHI, self.X1BF), (self.WGHI, self.LOBF),
                               (self.WGLO, self.X1BF)):
                for tt in range(NTT):
                    last = (dc == DC - 1) and (wgt is self.WGLO)
                    mm = te.matmul(PS[0:8, tt, :], wgt[:, dc, :],
                                   xin[:, dc, TT * tt:TT * (tt + 1)],
                                   start=(not started[tt]), stop=last)
                    started[tt] = True
        sT.inc(mm)
        t_lg = sT.n
        se.wait_ge(sT.h, t_lg)
        sS.inc(se.activation(self.LGS, PS[0:8, 0:4, :], AF.Identity, bias=self.BG))
        s_lg = sS.n
        te.wait_ge(sS.h, s_lg)
        for j in range(16):
            sT.inc(te.matmul(PS[:, 4, 8 * j:8 * (j + 1)], self.LGS[0:8, 128 * j:128 * (j + 1)],
                             self.IDENTF[0:8, 0:8], is_transpose=True,
                             start=(j == 0), stop=(j == 15)))
        t_tr = sT.n
        ve.wait_ge(sT.h, t_tr)
        sV.inc(ve.tensor_copy(self.LT.rearrange("p a b -> p (a b)"), PS[:, 4, 0:128]))
        ve.wait_ge(sV.h, sV.n)
        for j in range(16):
            lt = self.LT[:, j, :]
            sV.inc(ve.max(self.SRT, lt))
            ve.wait_ge(sV.h, sV.n)
            sV.inc(ve.tensor_scalar_mul(self.GS1, self.SRT[:, 0:1], -1.0))
            se.wait_ge(sV.h, sV.n)
            sS.inc(se.activation(self.GS3, lt, AF.Exp, bias=self.GS1))
            sS.inc(se.activation(self.GS2, self.SRT[:, 1:2], AF.Exp, bias=self.GS1))
            ve.wait_ge(sS.h, sS.n)
            sV.inc(ve.tensor_scalar_add(self.GS2, self.GS2, 1.0))
            ve.wait_ge(sV.h, sV.n)
            sV.inc(ve.reciprocal(self.GS2, self.GS2))
            sV.inc(ve.tensor_single_scalar(self.GS4, lt, self.SRT[:, 1:2], ALU.is_ge))
            ve.wait_ge(sV.h, sV.n)
            sV.inc(ve.tensor_mul(self.CT[:, j, :], self.GS3, self.GS4))
            ve.wait_ge(sV.h, sV.n)
            sV.inc(ve.tensor_scalar_mul(self.CT[:, j, :], self.CT[:, j, :], self.GS2))
            ve.wait_ge(sV.h, sV.n)
        v_ct = sV.n
        te.wait_ge(sV.h, v_ct)
        for w in range(4):
            for k in range(4):
                j = 4 * w + k
                sT.inc(te.matmul(PS[0:8, 5, 128 * k:128 * (k + 1)],
                                 self.CT[:, j, :], self.IDENTF,
                                 is_transpose=True, start=(k == 0), stop=(k == 3)))
            ve.wait_ge(sT.h, sT.n)
            sV.inc(ve.tensor_copy(self.CSB[0:8, 512 * w:512 * (w + 1)], PS[0:8, 5, :]))
            te.wait_ge(sV.h, sV.n)
        self.v_csb = sV.n
        self.t_gate_end = sT.n

    def moe(self):
        nc = self.nc
        te, ve, se, sy, ge = nc.tensor, nc.vector, nc.scalar, nc.sync, nc.gpsimd
        PS, Y1 = self.PS, self.Y1
        sT, sV, sS = self.sT, self.sV, self.sS

        se.preload_act_table(AF.Silu)
        mw1_r = self.d_mw1.rearrange("e (c p) n -> e p c n", p=128)
        ge.wait_ge(sT.h, self.t_gate_end)
        self.cW1E.inc(ge.dma_start(out=self.W1EB, in_=mw1_r[0]), 16)

        def w2_dma(e, m):
            self.cW2E[m % 2].inc(ge.dma_start(
                out=self.W2E[:, m % 2, :, :],
                in_=self.d_mw2[e, :, 128 * m:128 * (m + 1)]
                    .rearrange("(a p) n -> p a n", p=128)), 16)
        w2_dma(0, 0)
        w2_dma(0, 1)

        v_y4 = [[0] * NTT for _ in range(DC)]
        v_bev = [0, 0]
        t_w2read = [0, 0]
        v_a = [0] * NTT
        s_sg4 = [0] * NTT
        v_cbc = 0
        for e in range(E):
            if e == 0:
                te.wait_ge(sV.h, self.v_csb)
                te.wait_ge(self.cSEL.h, 16)
            else:
                te.wait_ge(sV.h, v_bev[0])
            t_cb = [0] * NTT
            for tt in range(NTT):
                sT.inc(te.matmul(PS[:, tt, :], self.SEL8[:, 128 * e:128 * (e + 1)],
                                 self.CSB[0:8, TT * tt:TT * (tt + 1)],
                                 start=True, stop=True))
                t_cb[tt] = sT.n
            for tt in range(NTT):
                ve.wait_ge(sT.h, t_cb[tt])
                sV.inc(ve.tensor_copy(self.CBC[:, TT * tt:TT * (tt + 1)], PS[:, tt, :]))
            v_cbc = sV.n
            te.wait_ge(sV.h, v_cbc)

            te.wait_ge(self.cW1E.h, 16 * (e + 1))
            if e > 0:
                te.wait_ge(sV.h, v_bev[1])
            for jj in range(DC):
                mm = None
                for dc in range(DC):
                    if e == 0 and jj == 0:
                        te.wait_ge(sS.h, self.s_x1bf[dc])
                    for tt in range(NTT):
                        if dc == 0:
                            te.wait_ge(sS.h, s_sg4[tt])
                        mm = te.matmul(PS[:, tt, :], self.W1EB[:, dc, 128 * jj:128 * (jj + 1)],
                                       self.X1BF[:, dc, TT * tt:TT * (tt + 1)],
                                       start=(dc == 0), stop=(dc == DC - 1))
                sT.inc(mm)
                t_g = sT.n
                mm = None
                for dc in range(DC):
                    for tt in range(NTT):
                        if dc == 0:
                            te.wait_ge(sV.h, v_a[tt])
                        mm = te.matmul(PS[:, 4 + tt, :],
                                       self.W1EB[:, dc, 1024 + 128 * jj:1024 + 128 * (jj + 1)],
                                       self.X1BF[:, dc, TT * tt:TT * (tt + 1)],
                                       start=(dc == 0), stop=(dc == DC - 1))
                sT.inc(mm)
                t_u = sT.n
                for tt in range(NTT):
                    kb = tt % 2
                    se.wait_ge(sT.h, t_g)
                    se.wait_ge(sV.h, v_a[tt - 2] if tt >= 2 else v_a[tt + 2])
                    sS.inc(se.activation(self.SGB[:, kb, :], PS[:, tt, :], AF.Silu))
                    s_sg4[tt] = sS.n
                    ve.wait_ge(sS.h, s_sg4[tt])
                    ve.wait_ge(sT.h, t_u)
                    sV.inc(ve.tensor_mul(self.ABF[:, jj, TT * tt:TT * (tt + 1)],
                                         self.SGB[:, kb, :], PS[:, 4 + tt, :]))
                    ve.wait_ge(sV.h, sV.n)
                    sV.inc(ve.tensor_mul(self.ABF[:, jj, TT * tt:TT * (tt + 1)],
                                         self.ABF[:, jj, TT * tt:TT * (tt + 1)],
                                         self.CBC[:, TT * tt:TT * (tt + 1)]))
                    v_a[tt] = sV.n
            t_w1_end = sT.n
            if e + 1 < E:
                ge.wait_ge(sT.h, t_w1_end)
                self.cW1E.inc(ge.dma_start(out=self.W1EB, in_=mw1_r[e + 1]), 16)

            te.wait_ge(sS.h, max(s_sg4))
            te.wait_ge(sV.h, max(v_a))
            for m in range(DC):
                bk = (m % 2) * 4
                te.wait_ge(self.cW2E[m % 2].h, self.cW2E[m % 2].n)
                if m >= 2 or e > 0:
                    te.wait_ge(sV.h, v_bev[m % 2])
                mm = None
                for dffc in range(DC):
                    for tt in range(NTT):
                        mm = te.matmul(PS[:, bk + tt, :], self.W2E[:, m % 2, dffc, :],
                                       self.ABF[:, dffc, TT * tt:TT * (tt + 1)],
                                       start=(dffc == 0), stop=(dffc == DC - 1))
                sT.inc(mm)
                t_w2read[m % 2] = sT.n
                ne, nm = (e, m + 2) if m + 2 < DC else (e + 1, m + 2 - DC)
                if ne < E:
                    ge.wait_ge(sT.h, t_w2read[m % 2])
                    w2_dma(ne, nm)
                ve.wait_ge(sT.h, t_w2read[m % 2])
                ev = None
                for tt in range(NTT):
                    ve.wait_ge(sV.h, v_y4[m][tt])
                    ev = ve.tensor_add(Y1[:, m, TT * tt:TT * (tt + 1)],
                                       Y1[:, m, TT * tt:TT * (tt + 1)],
                                       PS[:, bk + tt, :])
                    sV.inc(ev)
                    v_y4[m][tt] = sV.n
                v_bev[m % 2] = sV.n
        self.v_y4 = v_y4
        self.v_bev_moe = v_bev

    def lnd_out(self):
        nc = self.nc
        te, ve, se, sy = nc.tensor, nc.vector, nc.scalar, nc.sync
        PS, Y1 = self.PS, self.Y1
        sT, sV, sS, sO = self.sT, self.sV, self.sS, self.sO

        self.preload_set("natural_log_exp_and_others")
        v_y4 = self.v_y4
        v_bev = self.v_bev_moe

        def gate_d(dc, tt):
            g = [(sV.h, v_y4[dc][tt])]
            if dc == 0 and tt == 0:
                g.append((sV.h, v_bev[0]))
                g.append((sV.h, v_bev[1]))
            return g
        self.ln_stats(Y1, gate=gate_d)
        s_rows = self.ln_rows(self.ROW2A, self.ROW2B, self.ROW2T)
        t_bc = self.ln_bcast(self.ROW2T, self.ROW2A, s_rows)
        outr = self.d_out.rearrange("(c p) n -> p c n", p=128)
        for dc in range(DC):
            k = dc % 2
            ve.wait_ge(sT.h, t_bc)
            if dc >= 2:
                ve.wait_ge(self.cOUT[k].h, self.cOUT[k].n)
            sV.inc(ve.tensor_sub(self.OSTG[:, k, :], Y1[:, dc, :], PS[:, 0:4, :]))
            ve.wait_ge(sV.h, sV.n)
            sV.inc(ve.tensor_mul(self.OSTG[:, k, :], self.OSTG[:, k, :], PS[:, 4:8, :]))
            sy.wait_ge(sV.h, sV.n)
            self.cOUT[k].inc(sy.dma_start(out=outr[:, dc, :], in_=self.OSTG[:, k, :]), 16)
        sy.wait_ge(self.cOUT[0].h, self.cOUT[0].n)
        sy.wait_ge(self.cOUT[1].h, self.cOUT[1].n)

_NC_CACHE = {}


def get_nc(stage="full"):
    if stage not in _NC_CACHE:
        _NC_CACHE[stage] = Builder(stage).nc
    return _NC_CACHE[stage]


def make_masks(core):
    t0 = 2048 * (core % 4)
    jj = np.arange(128)[:, None]
    ii = np.arange(256)[None, :]
    bm = np.zeros((6, 128, 256), np.float32)
    for b in range(6):
        j = 128 * b + jj
        bm[b] = ((ii <= j) & (j <= ii + 512)).astype(np.float32)
    em0 = np.zeros_like(bm)
    em1 = np.zeros_like(bm)
    for b in range(6):
        j = 128 * b + jj
        k0 = t0 - 256 + j
        k7 = t0 + 1792 - 256 + j
        em0[b] = bm[b] * ((k0 >= 0) & (k0 < S))
        em1[b] = bm[b] * ((k7 >= 0) & (k7 < S))
    bf = ml_dtypes.bfloat16
    return (bm[[0, 1, 4, 5]].astype(bf), em0.astype(bf), em1.astype(bf))


def make_inputs(core, X):
    b, t0 = core // 4, 2048 * (core % 4)
    src = np.asarray(X["L0_attn_src"][b], np.float32)
    pad = np.zeros((TH, D), np.float32)
    lo, hi = max(0, t0 - HALO), min(S, t0 + T + HALO)
    pad[lo - (t0 - HALO):hi - (t0 - HALO)] = src[lo:hi]
    srcT = np.ascontiguousarray(pad.T)
    bm, em0, em1 = make_masks(core)
    tmask = np.stack([bm[0][:, 0:128], bm[1][:, 128:256],
                      bm[2][:, 0:128], bm[3][:, 128:256]])
    vec6 = np.stack([
        np.asarray(X["L0_attn_b2"], np.float32).reshape(8, 128).T,
        np.asarray(X["L0_attn_ln1_g"], np.float32).reshape(8, 128).T,
        np.asarray(X["L0_attn_ln1_b"], np.float32).reshape(8, 128).T,
        np.asarray(X["L0_attn_ln2_g"], np.float32).reshape(8, 128).T,
        np.asarray(X["L0_attn_ln2_b"], np.float32).reshape(8, 128).T,
    ])
    return {
        "srcT": srcT,
        "w1": np.ascontiguousarray(np.asarray(X["L0_attn_W1"], np.float32)),
        "w2": np.ascontiguousarray(np.asarray(X["L0_attn_W2"], np.float32)),
        "b1T": np.ascontiguousarray(np.asarray(X["L0_attn_b1"], np.float32).reshape(32, 128).T),
        "vec6": np.ascontiguousarray(vec6),
        "bg": np.asarray(X["L0_ffn_bg"], np.float32).reshape(8, 1),
        "wgT": np.ascontiguousarray(np.asarray(X["L0_ffn_Wg"], np.float32).T),
        "mw1": np.ascontiguousarray(np.asarray(X["L0_ffn_w1"], np.float32).transpose(0, 2, 1)),
        "mw2": np.ascontiguousarray(np.asarray(X["L0_ffn_w2"], np.float32).transpose(0, 2, 1)),
        "bmask": np.ascontiguousarray(bm),
        "tmask": np.ascontiguousarray(tmask),
        "em0": np.ascontiguousarray(em0),
        "em1": np.ascontiguousarray(em1),
        "identm": np.eye(128, dtype=ml_dtypes.bfloat16),
        "sel8": np.ascontiguousarray(np.kron(np.eye(8, dtype=np.float32), np.ones((1, 128), np.float32)).astype(ml_dtypes.bfloat16)),
    }


def kernel(**inputs):
    nc = get_nc("full")
    in_maps = [make_inputs(c, inputs) for c in range(NCORES)]
    res = run_bass_kernel_spmd(nc, in_maps, core_ids=list(range(NCORES)))
    out = np.empty((2, S, D), np.float32)
    for c in range(NCORES):
        b, t0 = c // 4, 2048 * (c % 4)
        out[b, t0:t0 + T] = np.asarray(res.results[c]["outT"]).T
    return out


# version 09
# speedup vs baseline: 1.2223x; 1.0304x over previous
import numpy as np
import ml_dtypes
import concourse.bass as bass
import concourse.mybir as mybir
from concourse.bass_utils import run_bass_kernel_spmd

F32 = mybir.dt.float32
BF16 = mybir.dt.bfloat16
AF = mybir.ActivationFunctionType
ALU = mybir.AluOpType
AX = mybir.AxisListType

NCORES = 8
T = 2048
HALO = 256
TH = T + 2 * HALO
D = 1024
DC = 8
H = 16
DH = 64
NG = T // 256
S = 8192
DFF = 4096
NQ = 4
NTT = 4
TT = 512
E = 8
MDFF = 2048
LN_EPS = 1e-5
NARROW = True

SBUF_BYTES = 229376 - 64
SBUF_BASE = 16512

OFF_Y1 = SBUF_BASE
OFF_VEC = SBUF_BASE + 65536
PHASE = SBUF_BASE + 69632

OFF_SRCB = PHASE
OFF_VTOK = OFF_SRCB + 40960
OFF_BMASK = OFF_VTOK + 41600
OFF_EM0 = OFF_BMASK + 2048
OFF_EM1 = OFF_EM0 + 3072
OFF_ESB = OFF_EM1 + 3072
OFF_PVS = OFF_ESB + 9216
OFF_RDEN = OFF_PVS + 2048
OFF_PVN = OFF_RDEN + 1024
OFF_W1QA = SBUF_BYTES - 16384
OFF_STB = SBUF_BASE + 182272
OFF_STB2 = OFF_STB + 4096
OFF_TM = OFF_STB2 + 4096
OFF_SRTB = OFF_TM + 1024
OFF_EX8 = OFF_SRTB + 512
OFF_IND = OFF_EX8 + 512
OFF_GS2C = OFF_IND + 512

OFF_ROWA = PHASE
OFF_ROWB = OFF_ROWA + 8192
OFF_ROWT = OFF_ROWB + 8192
OFF_UBF = PHASE + 24576
OFF_HQ = OFF_UBF + 32768
OFF_W1QB = OFF_HQ + 32768
OFF_W2S = OFF_W1QB + 16384
OFF_SRC2 = OFF_UBF

OFF_X1BF = OFF_UBF
OFF_ABF = OFF_HQ
OFF_W1E = OFF_HQ + 32768
OFF_CSB = PHASE
OFF_CB = PHASE + 4096
OFF_SEL = PHASE + 12288
OFF_SG = PHASE + 14336
OFF_W2E = PHASE + 16384
OFF_ROW2A = OFF_W1E
OFF_ROW2B = OFF_W1E + 8192
OFF_ROW2T = OFF_W1E + 16384
OFF_LG = OFF_W1E + 24576
OFF_OSTG = OFF_X1BF

VEC = {}
_vo = OFF_VEC
def _vecslot(name, nbytes, align=32):
    global _vo
    _vo = (_vo + align - 1) // align * align
    VEC[name] = _vo
    _vo += nbytes
_vecslot("b1T", 32 * 4)
_vecslot("b2T", 8 * 4)
_vecslot("g1T", 8 * 4)
_vecslot("b1lT", 8 * 4)
_vecslot("g2T", 8 * 4)
_vecslot("b2lT", 8 * 4)
_vecslot("bg", 4)
_vecslot("negM", 4)
_vecslot("mrow", 4)
_vecslot("wg", 64 * 4)
_vecslot("wghi", 64 * 2)
_vecslot("wglo", 64 * 2)
_vecslot("ident", 128 * 2)
_vecslot("ones_bf", 2)
_vecslot("onesr_bf", 64 * 2)
_vecslot("lt", 16 * 8 * 4)
_vecslot("ct", 16 * 8 * 4)
_vecslot("srt", 8 * 4)
_vecslot("gs1", 4)
_vecslot("gs2", 4)
_vecslot("gs3", 8 * 4)
_vecslot("gs4", 8 * 4)
_vecslot("onesf", 128 * 4)
_vecslot("identf", 128 * 4)
_vecslot("eps1", 4)
_vecslot("cz", 4)
_vecslot("co", 4)
_vecslot("cn1", 4)
_vecslot("cnh", 4)
assert _vo <= PHASE, _vo
assert PHASE + 24576 + 32768 + 32768 + 16384 + 4096 <= OFF_W1QA


class Ctr:
    def __init__(self, nc, name):
        self.h = nc.alloc_semaphore(name)
        self.n = 0

    def inc(self, instr, k=1):
        instr.then_inc(self.h, k)
        self.n += k
        return self.n


class Builder:
    def __init__(self, stage="full"):
        self.stage = stage
        nc = self.nc = bass.Bass("TRN2", target_bir_lowering=False)
        assert nc.SBUF_PARTITION_SIZE_BYTES >= SBUF_BYTES
        assert nc.sbuf_base <= SBUF_BASE, nc.sbuf_base

        def dram(name, shape, kind="ExternalInput", dt=F32):
            return nc.dram_tensor(name, shape, dt, kind=kind).ap()

        self.d_srcT = dram("srcT", [D, TH])
        self.d_w1 = dram("w1", [D, DFF])
        self.d_w2 = dram("w2", [DFF, D])
        self.d_b1T = dram("b1T", [128, 32])
        self.d_vec6 = dram("vec6", [5, 128, 8])
        self.d_bg = dram("bg", [8, 1])
        self.d_wgT = dram("wgT", [D, 8])
        self.d_mw1 = dram("mw1", [E, D, MDFF])
        self.d_mw2 = dram("mw2", [E, D, D])
        self.d_bmask = dram("bmask", [4, 128, 256], dt=BF16)
        self.d_em0 = dram("em0", [6, 128, 256], dt=BF16)
        self.d_em1 = dram("em1", [6, 128, 256], dt=BF16)
        self.d_ident = dram("identm", [128, 128], dt=BF16)
        self.d_sel8 = dram("sel8", [8, 1024], dt=BF16)
        self.d_tmask = dram("tmask", [4, 128, 128], dt=BF16)
        self.d_out = dram("outT", [D, T], kind="ExternalOutput")

        sb = lambda name, shape, dt, off: nc.alloc_sbuf_tensor_at(name, shape, dt, offset=off).ap()
        self.Y1 = sb("y1", [128, DC, T], F32, OFF_Y1)
        self.SRCB = sb("srcb", [128, DC, TH], BF16, OFF_SRCB)
        self.VTOK = sb("vtok", [128, 20, H, 65], BF16, OFF_VTOK)
        self.BMASK = sb("bmask_s", [128, 4, 256], BF16, OFF_BMASK)
        self.EM0 = sb("em0_s", [128, 6, 256], BF16, OFF_EM0)
        self.EM1 = sb("em1_s", [128, 6, 256], BF16, OFF_EM1)
        self.ESB = sb("esb", [128, 3, 1536], BF16, OFF_ESB)
        self.PVS = sb("pvs", [65, 2, 256], F32, OFF_PVS)
        self.RDEN = sb("rden", [128, 2, 256], BF16, OFF_RDEN)
        self.PVN = sb("pvn", [64, 2, 256], BF16, OFF_PVN)
        self.W1QA = sb("w1qa", [128, DC, 1024], BF16, OFF_W1QA)
        self.W1QB = sb("w1qb", [128, DC, 1024], BF16, OFF_W1QB)
        self.W2S = sb("w2s", [128, 3, DC, 128], BF16, OFF_W2S)
        self.ROWA = sb("rowa", [1, T], F32, OFF_ROWA)
        self.ROWB = sb("rowb", [1, T], F32, OFF_ROWB)
        self.ROWT = sb("rowt", [1, T], F32, OFF_ROWT)
        self.UBF = sb("ubf", [128, DC, T], BF16, OFF_UBF)
        self.HQ = sb("hq", [128, DC, T], BF16, OFF_HQ)
        self.SRC2 = sb("src2", [128, DC, T], F32, OFF_SRC2)
        self.X1BF = sb("x1bf", [128, DC, T], BF16, OFF_X1BF)
        self.LOBF = sb("lobf", [128, DC, T], BF16, OFF_ABF)
        self.ABF = sb("abf", [128, DC, T], BF16, OFF_ABF)
        self.W1EB = sb("w1e", [128, DC, MDFF], BF16, OFF_W1E)
        self.CSB = sb("csb", [8, T], BF16, OFF_CSB)
        self.SEL8 = sb("sel8_s", [8, 1024], BF16, OFF_SEL)
        self.TMASK = sb("tmask_s", [128, 4, 128], BF16, OFF_TM)
        self.SRTB = sb("srtb_s", [128, 16, 8], F32, OFF_SRTB)
        self.EX8 = sb("ex8_s", [128, 16, 8], F32, OFF_EX8)
        self.IND = sb("ind_s", [128, 16, 8], F32, OFF_IND)
        self.GS2C = sb("gs2c_s", [128, 16], F32, OFF_GS2C)
        self.CBC = sb("cb", [128, T], BF16, OFF_CB)
        self.SGB = sb("sgb", [128, 2, TT], BF16, OFF_SG)
        self.W2E = sb("w2e", [128, 3, DC, 128], BF16, OFF_W2E)
        self.ROW2A = sb("row2a", [1, T], F32, OFF_ROW2A)
        self.ROW2B = sb("row2b", [1, T], F32, OFF_ROW2B)
        self.ROW2T = sb("row2t", [1, T], F32, OFF_ROW2T)
        self.LGS = sb("lg", [8, T], F32, OFF_LG)
        self.OSTG = sb("ostg", [128, 2, T], F32, OFF_OSTG)

        self.B1T = sb("b1t_s", [128, 32], F32, VEC["b1T"])
        self.B2T = sb("b2t_s", [128, 8], F32, VEC["b2T"])
        self.G1T = sb("g1t_s", [128, 8], F32, VEC["g1T"])
        self.B1LT = sb("b1lt_s", [128, 8], F32, VEC["b1lT"])
        self.G2T = sb("g2t_s", [128, 8], F32, VEC["g2T"])
        self.B2LT = sb("b2lt_s", [128, 8], F32, VEC["b2lT"])
        self.BG = sb("bg_s", [8, 1], F32, VEC["bg"])
        self.NEGM = sb("negm_s", [128, 1], F32, VEC["negM"])
        self.MROW = sb("mrow_s", [128, 1], F32, VEC["mrow"])
        self.WG = sb("wg_s", [128, DC, 8], F32, VEC["wg"])
        self.WGHI = sb("wghi_s", [128, DC, 8], BF16, VEC["wghi"])
        self.WGLO = sb("wglo_s", [128, DC, 8], BF16, VEC["wglo"])
        self.IDENT = sb("ident_s", [128, 128], BF16, VEC["ident"])
        self.ONES = sb("ones_s", [128, 1], BF16, VEC["ones_bf"])
        self.ONESR = sb("onesr_s", [128, 64], BF16, VEC["onesr_bf"])
        self.LT = sb("lt_s", [128, 16, 8], F32, VEC["lt"])
        self.CT = sb("ct_s", [128, 16, 8], F32, VEC["ct"])
        self.SRT = sb("srt_s", [128, 8], F32, VEC["srt"])
        self.GS1 = sb("gs1_s", [128, 1], F32, VEC["gs1"])
        self.GS2 = sb("gs2_s", [128, 1], F32, VEC["gs2"])
        self.GS3 = sb("gs3_s", [128, 8], F32, VEC["gs3"])
        self.GS4 = sb("gs4_s", [128, 8], F32, VEC["gs4"])
        self.ONESF = sb("onesf_s", [1, 128], F32, VEC["onesf"])
        self.IDENTF = sb("identf_s", [128, 128], F32, VEC["identf"])
        self.STB = sb("stb_s", [128, 4, 512], BF16, OFF_STB)
        self.STB2 = sb("stb2_s", [128, 4, 512], BF16, OFF_STB2)
        self.EPS1 = sb("eps1_s", [1, 1], F32, VEC["eps1"])
        self.CZ = sb("cz_s", [128, 1], F32, VEC["cz"])
        self.CO = sb("co_s", [128, 1], F32, VEC["co"])
        nc.const_aps.register(F32, 0.0, self.CZ)
        nc.const_aps.register(F32, 1.0, self.CO)
        self.CN1 = sb("cn1_s", [128, 1], F32, VEC["cn1"])
        self.CNH = sb("cnh_s", [128, 1], F32, VEC["cnh"])
        nc.const_aps.register(F32, -1.0, self.CN1)
        nc.const_aps.register(F32, -0.5, self.CNH)

        self.PS = nc.alloc_psum_tensor("ps", [128, 8, 512], F32).ap()
        self.PSB = self.PS.bitcast(BF16)

        self.sT = Ctr(nc, "c_st")
        self.sV = Ctr(nc, "c_sv")
        self.sS = Ctr(nc, "c_ss")
        self.sP = Ctr(nc, "c_sp")
        self.sO = Ctr(nc, "c_so")
        self.cY1 = Ctr(nc, "c_y1")
        self.cCST = Ctr(nc, "c_cst")
        self.cSRCB = Ctr(nc, "c_srcb")
        self.cW1A = Ctr(nc, "c_w1a")
        self.cW1B = Ctr(nc, "c_w1b")
        self.cW2S = [Ctr(nc, "c_w2s0"), Ctr(nc, "c_w2s1"), Ctr(nc, "c_w2s2")]
        self.cSRC2 = Ctr(nc, "c_src2")
        self.cW1E = Ctr(nc, "c_w1e")
        self.cSEL = Ctr(nc, "c_sel")
        self.cOUT = [Ctr(nc, "c_out0"), Ctr(nc, "c_out1")]
        self.cW2E = [Ctr(nc, "c_w2e0"), Ctr(nc, "c_w2e1"), Ctr(nc, "c_w2e2")]

        self.build()

    def barrier(self):
        self.nc.all_engine_barrier()

    def preload_set(self, name):
        from concourse.hw_specs import get_activation_tables
        names = list(get_activation_tables(self.nc.m.arch).keys())
        idx = names.index(name)
        se = self.nc.scalar
        se.add_instruction(mybir.InstLoadActFuncSet(
            name=self.nc.get_next_instruction_name(), act_func_set_id=idx,
            ins=[], outs=[]))

    def build(self):
        nc = self.nc
        self.prologue()
        self.attn()
        if self.stage == "attn":
            self.emit_out(self.Y1)
            return
        self.lna_ffn()
        if self.stage == "ffn":
            self.emit_out(self.Y1)
            return
        self.lnb_lnc_gating()
        self.moe()
        self.lnd_out()

    def emit_out(self, plane):
        nc, sy = self.nc, self.nc.sync
        self.barrier()
        outr = self.d_out.rearrange("(c p) n -> p c n", p=128)
        for dc in range(DC):
            self.sO.inc(sy.dma_start(out=outr[:, dc, :], in_=plane[:, dc, :]), 16)
        sy.wait_ge(self.sO.h, self.sO.n)

    def prologue(self):
        nc = self.nc
        te, ve, se, sy, ge = nc.tensor, nc.vector, nc.scalar, nc.sync, nc.gpsimd
        src_r = self.d_srcT.rearrange("(c p) n -> p c n", p=128)
        self.cY1.inc(sy.dma_start(out=self.Y1, in_=src_r[:, :, HALO:HALO + T]), 16)
        self.cCST.inc(sy.dma_start(out=self.BMASK, in_=self.d_bmask.rearrange("c p n -> p c n")), 16)
        self.cCST.inc(sy.dma_start(out=self.EM0, in_=self.d_em0.rearrange("c p n -> p c n")), 16)
        self.cCST.inc(sy.dma_start(out=self.EM1, in_=self.d_em1.rearrange("c p n -> p c n")), 16)
        self.cCST.inc(sy.dma_start(out=self.IDENT, in_=self.d_ident), 16)
        self.cCST.inc(sy.dma_start(out=self.B1T, in_=self.d_b1T), 16)
        for i, t in enumerate([self.B2T, self.G1T, self.B1LT, self.G2T, self.B2LT]):
            self.cCST.inc(sy.dma_start(out=t, in_=self.d_vec6[i]), 16)
        self.cCST.inc(sy.dma_start(out=self.BG, in_=self.d_bg), 16)
        self.cCST.inc(sy.dma_start(out=self.TMASK, in_=self.d_tmask.rearrange("c p n -> p c n")), 16)
        self.cCST.inc(sy.dma_start(out=self.WG, in_=self.d_wgT.rearrange("(c p) e -> p c e", p=128)), 16)
        self.cSRCB.inc(ge.dma_start(out=self.SRCB, in_=src_r), 16)
        self.w1_r = self.d_w1.rearrange("(c p) (q n) -> q p c n", p=128, n=1024)
        self.cW1A.inc(ge.dma_start(out=self.W1QA, in_=self.w1_r[0]), 16)

        ve.memset(self.ONES, 1.0)
        ve.memset(self.ONESR[64:65, :], 1.0)
        ve.memset(self.ONESF, 1.0)
        ve.memset(self.EPS1, float(LN_EPS))
        ve.memzero(self.CZ)
        ve.memset(self.CO, 1.0)
        ve.memset(self.CN1, -1.0)
        ve.memset(self.CNH, -0.5)
        ve.wait_ge(self.cCST.h, self.cCST.n)
        ve.tensor_copy(self.IDENTF, self.IDENT)
        self.sV.inc(ve.memset(self.VTOK[:, :, :, 64:65], 1.0))
        self.v_ones = self.sV.n

    def attn(self):
        nc = self.nc
        te, ve, se, ge = nc.tensor, nc.vector, nc.scalar, nc.gpsimd
        PS, SRCB, ESB, VTOK, PVS, RDEN, Y1 = (self.PS, self.SRCB, self.ESB, self.VTOK,
                                              self.PVS, self.RDEN, self.Y1)
        sT, sV, sS, sP = self.sT, self.sV, self.sS, self.sP

        self.preload_set("natural_log_exp_and_others")

        te.wait_ge(self.cSRCB.h, self.cSRCB.n)
        te.wait_ge(self.cCST.h, self.cCST.n)
        ve.wait_ge(self.cCST.h, self.cCST.n)
        sT.inc(te.matmul(PS[:, 0, 0:320], SRCB[0:64, 0, 256:384],
                         SRCB[0:64, 0, 0:2560:8], start=True, stop=True))
        ve.wait_ge(sT.h, sT.n)
        sV.inc(ve.tensor_reduce(self.MROW, PS[:, 0, 0:320], AX.X, ALU.max))
        te.wait_ge(sV.h, sV.n)
        sT.inc(te.matmul(PS[0:1, 1, 0:128], self.MROW, self.IDENTF,
                         is_transpose=True, start=True, stop=True))
        ve.wait_ge(sT.h, sT.n)
        sV.inc(ve.tensor_reduce(self.MROW[0:1, :], PS[0:1, 1, 0:128], AX.X, ALU.max))
        te.wait_ge(sV.h, sV.n)
        sT.inc(te.matmul(PS[:, 1, 0:1], self.ONESF, self.MROW[0:1, :],
                         start=True, stop=True))
        ve.wait_ge(sT.h, sT.n)
        sV.inc(ve.tensor_scalar(self.NEGM, PS[:, 1, 0:1], -1.0, -8.0, ALU.mult, ALU.add))
        v_negm = sV.n
        se.wait_ge(sV.h, v_negm)
        te.wait_ge(sV.h, v_negm)

        NI = NG * H
        t_qk = [0] * NI
        t_pv = [0] * NI
        t_bc = [0] * NI
        t_sh = [0] * NI
        s_exp = [0] * NI
        v_cp = [0] * NI
        v_msk = [0] * NI
        s_rec = [0] * NI
        v_mul = [0] * NI
        v_acc = [0] * NI
        built = set()

        def chunk_cols(gg, b):
            if not NARROW or gg in (0, NG - 1):
                return 0, 256
            if b == 0:
                return 0, 128
            if b == 5:
                return 128, 256
            return 0, 256

        ve.wait_ge(self.cY1.h, self.cY1.n)
        ve.wait_ge(sV.h, self.v_ones)

        def build_vhat(kcs):
            for kc in kcs:
                if kc in built:
                    continue
                built.add(kc)
                for wave in range(2):
                    for k in range(4):
                        dc = 4 * wave + k
                        sT.inc(te.matmul(self.PSB[:, 0, 128 * k:128 * (k + 1)],
                                         SRCB[:, dc, 128 * kc:128 * (kc + 1)],
                                         self.IDENT, is_transpose=True,
                                         start=(k == 0), stop=(k == 3)))
                    ve.wait_ge(sT.h, sT.n)
                    src_ap = self.PSB[:, 0, 0:512] \
                        .rearrange("p (a b) -> p a b", a=4) \
                        .rearrange("p a (c d) -> p (a c) d", c=2)
                    sV.inc(ve.tensor_copy(VTOK[:, kc, 8 * wave:8 * wave + 8, 0:64], src_ap))
                    te.wait_ge(sV.h, sV.n)

        def emit_qk(i):
            gg, h = divmod(i, H)
            dc, r0 = h // 2, 64 * (h % 2)
            par = i % 2
            if i >= 2:
                te.wait_ge(sS.h, s_exp[i - 2])
            mm = None
            for b in range(6):
                kc = 2 * gg + b
                c0, c1 = chunk_cols(gg, b)
                mm = te.matmul(PS[:, 3 * par + b // 2, 256 * (b % 2) + c0:256 * (b % 2) + c1],
                               SRCB[r0:r0 + 64, dc, 128 * kc:128 * kc + 128],
                               SRCB[r0:r0 + 64, dc, 256 + 256 * gg + c0:256 + 256 * gg + c1],
                               start=(b % 2 == 0), stop=(b % 2 == 1))
            sT.inc(mm)
            t_qk[i] = sT.n

        def emit_exp(i):
            par3 = i % 3
            se.wait_ge(sT.h, t_qk[i])
            if i >= 3:
                se.wait_ge(sT.h, t_pv[i - 3])
            sS.inc(se.activation(ESB[:, par3, :], PS[:, 3 * (i % 2):3 * (i % 2) + 3, :],
                                 AF.Exp, bias=self.NEGM))
            s_exp[i] = sS.n

        def emit_mask(i):
            gg = i // H
            par3 = i % 3
            ve.wait_ge(sS.h, s_exp[i])
            mm = None
            if gg == 0:
                for (ec, mc) in ((0, 0), (2, 2), (4, 4)):
                    mm = ve.tensor_mul(ESB[:, par3, 256 * ec:256 * ec + 512],
                                       ESB[:, par3, 256 * ec:256 * ec + 512],
                                       self.EM0[:, mc:mc + 2, :])
            elif gg == NG - 1:
                for (ec, mc) in ((0, 0), (2, 2), (4, 4)):
                    mm = ve.tensor_mul(ESB[:, par3, 256 * ec:256 * ec + 512],
                                       ESB[:, par3, 256 * ec:256 * ec + 512],
                                       self.EM1[:, mc:mc + 2, :])
            elif NARROW:
                tri = bass.AP(ESB.tensor, ESB.offset + par3 * 1536,
                              [[4608, 128], [1024, 2], [384, 2], [1, 128]])
                tm = self.TMASK.rearrange("p (a c) x -> p a c x", a=2)
                mm = ve.tensor_mul(tri, tri, tm)
            else:
                for (ec, mc) in ((0, 0), (4, 2)):
                    mm = ve.tensor_mul(ESB[:, par3, 256 * ec:256 * ec + 512],
                                       ESB[:, par3, 256 * ec:256 * ec + 512],
                                       self.BMASK[:, mc:mc + 2, :])
            sV.inc(mm)
            v_msk[i] = sV.n

        def emit_pv(i):
            gg, h = divmod(i, H)
            par = i % 2
            par3 = i % 3
            te.wait_ge(sV.h, v_msk[i])
            if i >= 2:
                te.wait_ge(sV.h, v_acc[i - 2])
            mm = None
            bs = range(6) if (not NARROW or gg in (0, NG - 1)) else (1, 0, 2, 3, 4, 5)
            for bi, b in enumerate(bs):
                kc = 2 * gg + b
                c0, c1 = chunk_cols(gg, b)
                mm = te.matmul(PS[0:65, 6 + par, c0:c1],
                               VTOK[:, kc, h, :],
                               ESB[:, par3, 256 * b + c0:256 * b + c1],
                               start=(bi == 0), stop=(bi == 5))
            sT.inc(mm)
            t_pv[i] = sT.n

        def emit_copy(i):
            par = i % 2
            se.wait_ge(sT.h, t_pv[i])
            sS.inc(se.copy(PVS[:, par, :], PS[0:65, 6 + par, 0:256]))
            v_cp[i] = sS.n

        def emit_recip(i):
            par = i % 2
            ve.wait_ge(sS.h, v_cp[i])
            if i >= 2:
                ve.wait_ge(sT.h, t_bc[i - 2])
            with nc.allow_low_precision(reason="softmax denom reciprocal feeds a bf16 rank-1 broadcast matmul"):
                sV.inc(ve.reciprocal(RDEN[64:65, par, :], PVS[64:65, par, :]))
            s_rec[i] = sV.n

        def emit_bcast(i):
            par = i % 2
            te.wait_ge(sV.h, s_rec[i])
            sT.inc(te.matmul(PS[0:64, 6 + par, 256:512], self.ONESR[64:65, :],
                             RDEN[64:65, par, :], start=True, stop=True))
            t_bc[i] = sT.n

        def emit_mult(i):
            par = i % 2
            ve.wait_ge(sT.h, t_bc[i])
            sV.inc(ve.tensor_mul(self.PVN[:, par, :], PVS[0:64, par, :],
                                 PS[0:64, 6 + par, 256:512]))
            v_mul[i] = sV.n

        def emit_shift(i):
            par = i % 2
            te.wait_ge(sV.h, v_mul[i])
            sT.inc(te.matmul(PS[64:128, 6 + par, 256:512], self.IDENT[0:64, 0:64],
                             self.PVN[:, par, :], start=True, stop=True))
            t_sh[i] = sT.n

        def emit_add(i):
            gg, h = divmod(i, H)
            dc, r0 = h // 2, 64 * (h % 2)
            par = i % 2
            y = Y1[r0:r0 + 64, dc, 256 * gg:256 * gg + 256]
            if h % 2 == 0:
                ve.wait_ge(sV.h, v_mul[i])
                sV.inc(ve.tensor_add(y, y, self.PVN[:, par, :]))
            else:
                ve.wait_ge(sT.h, t_sh[i])
                sV.inc(ve.tensor_add(y, y, PS[64:128, 6 + par, 256:512]))
            v_acc[i] = sV.n

        for i in range(NI + 4):
            j4 = i - 4
            if 0 <= j4 < NI:
                emit_mult(j4)
                if j4 % 2 == 1:
                    emit_shift(j4)
                emit_add(j4)
            if i < NI:
                if i % H == 0:
                    gg = i // H
                    kcs = list(range(6)) if gg == 0 else [2 * gg + 4, 2 * gg + 5]
                    if i >= 2:
                        te.wait_ge(sS.h, s_exp[i - 1])
                    build_vhat(kcs)
                emit_qk(i)
            if 0 <= i - 3 < NI:
                emit_copy(i - 3)
            if i < NI:
                emit_exp(i)
            if 0 <= i - 1 < NI:
                emit_mask(i - 1)
            if 0 <= i - 2 < NI:
                emit_pv(i - 2)
            if 0 <= i - 3 < NI:
                emit_recip(i - 3)
                emit_bcast(i - 3)

    def ln_stats(self, plane, gate=None):
        nc = self.nc
        te, ve, se = nc.tensor, nc.vector, nc.scalar
        sT, sV, sS = self.sT, self.sV, self.sS
        last_mm = [0, 0, 0, 0]
        for dc in range(DC):
            for tt in range(NTT):
                k = (dc * NTT + tt) % 4
                tile = plane[:, dc, TT * tt:TT * (tt + 1)]
                if gate is not None:
                    for (h, n) in gate(dc, tt):
                        ve.wait_ge(h, n)
                        se.wait_ge(h, n)
                        te.wait_ge(h, n)
                ve.wait_ge(sT.h, last_mm[k])
                se.wait_ge(sT.h, last_mm[k])
                sV.inc(ve.tensor_copy(self.STB[:, k, :], tile))
                sS.inc(se.activation(self.STB2[:, k, :], tile, AF.Square))
                te.wait_ge(sV.h, sV.n)
                te.wait_ge(sS.h, sS.n)
                te.matmul(self.PS[0:1, tt, :], self.ONES, self.STB[:, k, :],
                          start=(dc == 0), stop=(dc == DC - 1))
                sT.inc(te.matmul(self.PS[0:1, 4 + tt, :], self.ONES, self.STB2[:, k, :],
                                 start=(dc == 0), stop=(dc == DC - 1)))
                last_mm[k] = sT.n
        return sT.n

    def ln_rows(self, rowA, rowB, rowT):
        nc = self.nc
        ve, se = nc.vector, nc.scalar
        sV, sS = self.sV, self.sS
        sumv = self.PS[0:1, 0:4, :]
        ssv = self.PS[0:1, 4:8, :]
        ve.wait_ge(self.sT.h, self.sT.n)
        sV.inc(ve.tensor_scalar_mul(rowT, sumv, 1.0 / D))
        ve.wait_ge(sV.h, sV.n)
        sV.inc(ve.tensor_mul(rowB, rowT, rowT))
        ve.wait_ge(sV.h, sV.n)
        sV.inc(ve.scalar_tensor_tensor(rowA, ssv, 1.0 / D, rowB, ALU.mult, ALU.subtract))
        se.wait_ge(sV.h, sV.n)
        sS.inc(se.activation(rowB, rowA, AF.Ln, bias=self.EPS1))
        se.wait_ge(sS.h, sS.n)
        sS.inc(se.activation(rowA, rowB, AF.Exp, scale=-0.5))
        return sS.n

    def ln_bcast(self, rowT, rowA, s_rows):
        te = self.nc.tensor
        te.wait_ge(self.sS.h, s_rows)
        for tt in range(NTT):
            te.matmul(self.PS[:, tt, :], self.ONESF, rowT[0:1, TT * tt:TT * (tt + 1)],
                      start=True, stop=True)
            self.sT.inc(te.matmul(self.PS[:, 4 + tt, :], self.ONESF,
                                  rowA[0:1, TT * tt:TT * (tt + 1)], start=True, stop=True))
        return self.sT.n

    def ln_apply_pass1(self, plane, t_bc):
        ve, sV = self.nc.vector, self.sV
        for dc in range(DC):
            ve.wait_ge(self.sT.h, t_bc)
            sV.inc(ve.tensor_sub(plane[:, dc, :], plane[:, dc, :], self.PS[:, 0:4, :]))
        return sV.n

    def lna_ffn(self):
        nc = self.nc
        te, ve, se, sy, ge = nc.tensor, nc.vector, nc.scalar, nc.sync, nc.gpsimd
        PS, Y1, UBF, HQ = self.PS, self.Y1, self.UBF, self.HQ
        sT, sV, sS = self.sT, self.sV, self.sS

        self.barrier()
        t_stats_a = self.ln_stats(Y1)
        s_rows = self.ln_rows(self.ROWA, self.ROWB, self.ROWT)
        t_bc = self.ln_bcast(self.ROWT, self.ROWA, s_rows)
        v_p1 = self.ln_apply_pass1(Y1, t_bc)
        v_apply = [0] * DC
        s_cast = [0] * DC
        ve.wait_ge(sV.h, v_p1)
        for dc in range(DC):
            sV.inc(ve.tensor_mul(Y1[:, dc, :], Y1[:, dc, :], PS[:, 4:8, :]))
            t1v = sV.n
            se.wait_ge(sV.h, t1v)
            sS.inc(se.activation(UBF[:, dc, :], Y1[:, dc, :], AF.Identity,
                                 scale=self.G1T[:, dc:dc + 1], bias=self.B1LT[:, dc:dc + 1]))
            s_cast[dc] = sS.n
            ve.wait_ge(sS.h, s_cast[dc])
            sV.inc(ve.tensor_scalar(Y1[:, dc, :], Y1[:, dc, :],
                                    self.G1T[:, dc:dc + 1], self.B1LT[:, dc:dc + 1],
                                    ALU.mult, ALU.add))
            v_apply[dc] = sV.n
        v_p2 = v_apply[DC - 1]

        w1buf = [self.W1QA, self.W1QB]
        w1sem = [self.cW1A, self.cW1B]
        v_y2 = [[0] * NTT for _ in range(DC)]
        s_hq = [0] * DC
        t_a = {}
        t_b = {}
        v_bev = [0, 0]
        t_w2read = [0, 0, 0]
        for q in range(NQ):
            wb = w1buf[q % 2]
            wsem = w1sem[q % 2]
            wcnt = 16 * (q // 2 + 1)
            if q + 1 < NQ:
                if q >= 1:
                    ge.wait_ge(sT.h, self.t_a_end)
                w1sem[(q + 1) % 2].inc(ge.dma_start(out=w1buf[(q + 1) % 2],
                                                    in_=self.w1_r[q + 1]), 16)
            if q == 0:
                for m0 in range(3):
                    self.cW2S[m0].inc(ge.dma_start(
                        out=self.W2S[:, m0, :, :],
                        in_=self.d_w2[1024 * q:1024 * (q + 1), 128 * m0:128 * (m0 + 1)]
                            .rearrange("(a p) n -> p a n", p=128)), 16)
            te.wait_ge(wsem.h, wcnt)
            if q > 0:
                te.wait_ge(sV.h, v_bev[0])
                te.wait_ge(sV.h, v_bev[1])
            for hc in range(DC):
                bk = (hc % 2) * 4
                if q == 0 and hc == 0:
                    te.wait_ge(sV.h, v_p1)
                if q == 0 and hc == 1:
                    te.wait_ge(sV.h, v_p2)
                if hc >= 2:
                    te.wait_ge(sS.h, s_hq[hc - 2])
                mm = None
                for dc in range(DC):
                    if q == 0 and hc == 0:
                        te.wait_ge(sS.h, s_cast[dc])
                    for tt in range(NTT):
                        mm = te.matmul(PS[:, bk + tt, :], wb[:, dc, 128 * hc:128 * (hc + 1)],
                                       UBF[:, dc, TT * tt:TT * (tt + 1)],
                                       start=(dc == 0), stop=(dc == DC - 1))
                sT.inc(mm)
                t_a[hc] = sT.n
                se.wait_ge(sT.h, t_a[hc])
                a = None
                for tt in range(NTT):
                    a = se.activation(HQ[:, hc, TT * tt:TT * (tt + 1)], PS[:, bk + tt, :],
                                      AF.Relu, bias=self.B1T[:, 8 * q + hc:8 * q + hc + 1])
                sS.inc(a)
                s_hq[hc] = sS.n
            self.t_a_end = t_a[DC - 1]
            for m in range(DC):
                bk = (m % 2) * 4
                gidx = q * DC + m
                sl = gidx % 3
                te.wait_ge(self.cW2S[sl].h, self.cW2S[sl].n)
                te.wait_ge(sS.h, s_hq[DC - 1])
                if m >= 2:
                    te.wait_ge(sV.h, v_bev[m % 2])
                mm = None
                for hc in range(DC):
                    for tt in range(NTT):
                        mm = te.matmul(PS[:, bk + tt, :], self.W2S[:, sl, hc, :],
                                       HQ[:, hc, TT * tt:TT * (tt + 1)],
                                       start=(hc == 0), stop=(hc == DC - 1))
                sT.inc(mm)
                t_b[m] = sT.n
                t_w2read[sl] = sT.n
                ngidx = gidx + 3
                if ngidx < NQ * DC:
                    nq, nm = divmod(ngidx, DC)
                    ge.wait_ge(sT.h, t_b[m])
                    self.cW2S[sl].inc(ge.dma_start(
                        out=self.W2S[:, sl, :, :],
                        in_=self.d_w2[1024 * nq:1024 * (nq + 1), 128 * nm:128 * (nm + 1)]
                            .rearrange("(a p) n -> p a n", p=128)), 16)
                ve.wait_ge(sT.h, t_b[m])
                ev = None
                for tt in range(NTT):
                    ve.wait_ge(sV.h, v_y2[m][tt])
                    if q == 0:
                        ev = ve.scalar_tensor_tensor(Y1[:, m, TT * tt:TT * (tt + 1)],
                                                     PS[:, bk + tt, :],
                                                     self.B2T[:, m:m + 1],
                                                     Y1[:, m, TT * tt:TT * (tt + 1)],
                                                     ALU.add, ALU.add)
                    else:
                        ev = ve.tensor_add(Y1[:, m, TT * tt:TT * (tt + 1)],
                                           Y1[:, m, TT * tt:TT * (tt + 1)],
                                           PS[:, bk + tt, :])
                    sV.inc(ev)
                    v_y2[m][tt] = sV.n
                v_bev[m % 2] = sV.n
        self.v_y2 = v_y2
        self.v_bev_ffn = v_bev
        self.t_ffn_end = sT.n

    def lnb_lnc_gating(self):
        nc = self.nc
        te, ve, se, sy, ge = nc.tensor, nc.vector, nc.scalar, nc.sync, nc.gpsimd
        PS, Y1, SRC2 = self.PS, self.Y1, self.SRC2
        sT, sV, sS = self.sT, self.sV, self.sS

        v_y2 = self.v_y2
        v_bev = self.v_bev_ffn

        def gate_b(dc, tt):
            g = [(sV.h, v_y2[dc][tt])]
            if dc == 0 and tt == 0:
                g.append((sV.h, v_bev[0]))
                g.append((sV.h, v_bev[1]))
            return g
        t_stats = self.ln_stats(Y1, gate=gate_b)
        sy.wait_ge(sT.h, self.t_ffn_end)
        src_r = self.d_srcT.rearrange("(c p) n -> p c n", p=128)
        self.cSRC2.inc(sy.dma_start(out=SRC2, in_=src_r[:, :, HALO:HALO + T]), 16)
        s_rows = self.ln_rows(self.ROWA, self.ROWB, self.ROWT)
        s_rows_b = s_rows
        t_bc = self.ln_bcast(self.ROWT, self.ROWA, s_rows)
        se.wait_ge(self.cSRC2.h, self.cSRC2.n)
        s_bias = [0] * DC
        for dc in range(DC):
            sS.inc(se.add(SRC2[:, dc, :], SRC2[:, dc, :], self.B2LT[:, dc:dc + 1]))
            s_bias[dc] = sS.n
        v_p1 = self.ln_apply_pass1(Y1, t_bc)
        v_y3 = [0] * DC
        ve.wait_ge(sV.h, v_p1)
        for dc in range(DC):
            sV.inc(ve.tensor_mul(Y1[:, dc, :], Y1[:, dc, :], PS[:, 4:8, :]))
            ve.wait_ge(sV.h, sV.n)
            ve.wait_ge(sS.h, s_bias[dc])
            sV.inc(ve.scalar_tensor_tensor(SRC2[:, dc, :], Y1[:, dc, :],
                                           self.G2T[:, dc:dc + 1], SRC2[:, dc, :],
                                           ALU.mult, ALU.add))
            v_y3[dc] = sV.n

        def gate_c(dc, tt):
            g = [(sV.h, v_y3[dc])]
            if dc == 0 and tt == 0:
                g.append((sV.h, v_y3[DC - 1]))
            return g
        t_stats = self.ln_stats(SRC2, gate=gate_c)
        s_rows = self.ln_rows(self.ROW2A, self.ROW2B, self.ROW2T)
        t_bc = self.ln_bcast(self.ROW2T, self.ROW2A, s_rows)
        s_x1bf = [0] * DC
        v_x1 = [0] * DC
        for dc in range(DC):
            ve.wait_ge(sT.h, t_bc)
            sV.inc(ve.tensor_sub(Y1[:, dc, :], SRC2[:, dc, :], PS[:, 0:4, :]))
            ve.wait_ge(sV.h, sV.n)
            sV.inc(ve.tensor_mul(Y1[:, dc, :], Y1[:, dc, :], PS[:, 4:8, :]))
            v_x1[dc] = sV.n
            se.wait_ge(sV.h, v_x1[dc])
            sS.inc(se.copy(self.X1BF[:, dc, :], Y1[:, dc, :]))
            s_x1bf[dc] = sS.n
        self.v_x1 = v_x1
        self.s_x1bf = s_x1bf

        sy.wait_ge(sS.h, s_rows_b)
        self.cSEL.inc(sy.dma_start(out=self.SEL8, in_=self.d_sel8), 16)
        sV.inc(ve.tensor_copy(self.WGHI, self.WG))
        ve.wait_ge(sV.h, sV.n)
        sV.inc(ve.scalar_tensor_tensor(self.WGLO, self.WGHI, -1.0, self.WG,
                                       ALU.mult, ALU.add))
        v_lo = [0] * DC
        for dc in range(DC):
            ve.wait_ge(sS.h, s_x1bf[dc])
            sV.inc(ve.scalar_tensor_tensor(self.LOBF[:, dc, :], self.X1BF[:, dc, :],
                                           -1.0, Y1[:, dc, :], ALU.mult, ALU.add))
            v_lo[dc] = sV.n
        started = [False] * NTT
        mm = None
        for dc in range(DC):
            te.wait_ge(sV.h, v_lo[dc])
            if dc == 0:
                te.wait_ge(sV.h, self.v_x1[DC - 1])
            for (wgt, xin) in ((self.WGHI, self.X1BF), (self.WGHI, self.LOBF),
                               (self.WGLO, self.X1BF)):
                for tt in range(NTT):
                    last = (dc == DC - 1) and (wgt is self.WGLO)
                    mm = te.matmul(PS[0:8, tt, :], wgt[:, dc, :],
                                   xin[:, dc, TT * tt:TT * (tt + 1)],
                                   start=(not started[tt]), stop=last)
                    started[tt] = True
        sT.inc(mm)
        t_lg = sT.n
        se.wait_ge(sT.h, t_lg)
        sS.inc(se.activation(self.LGS, PS[0:8, 0:4, :], AF.Identity, bias=self.BG))
        s_lg = sS.n
        te.wait_ge(sS.h, s_lg)
        for j in range(16):
            sT.inc(te.matmul(PS[:, 4, 8 * j:8 * (j + 1)], self.LGS[0:8, 128 * j:128 * (j + 1)],
                             self.IDENTF[0:8, 0:8], is_transpose=True,
                             start=(j == 0), stop=(j == 15)))
        t_tr = sT.n
        ve.wait_ge(sT.h, t_tr)
        sV.inc(ve.tensor_copy(self.LT.rearrange("p a b -> p (a b)"), PS[:, 4, 0:128]))
        ve.wait_ge(sV.h, sV.n)
        for j in range(16):
            sV.inc(ve.max(self.SRTB[:, j, :], self.LT[:, j, :]))
        ve.wait_ge(sV.h, sV.n)
        m1b = self.SRTB[:, :, 0:1].broadcast_to([128, 16, 8])
        m2b = self.SRTB[:, :, 1:2].broadcast_to([128, 16, 8])
        sV.inc(ve.tensor_sub(self.EX8, self.LT, m1b))
        sV.inc(ve.tensor_tensor(self.IND, self.LT, m2b, ALU.is_ge))
        sV.inc(ve.tensor_sub(self.GS2C, self.SRTB[:, :, 1], self.SRTB[:, :, 0]))
        se.wait_ge(sV.h, sV.n)
        sS.inc(se.activation(self.EX8, self.EX8, AF.Exp))
        sS.inc(se.activation(self.GS2C, self.GS2C, AF.Exp))
        ve.wait_ge(sS.h, sS.n)
        sV.inc(ve.tensor_scalar_add(self.GS2C, self.GS2C, 1.0))
        ve.wait_ge(sV.h, sV.n)
        sV.inc(ve.reciprocal(self.GS2C, self.GS2C))
        ve.wait_ge(sV.h, sV.n)
        sV.inc(ve.tensor_mul(self.CT, self.EX8, self.IND))
        ve.wait_ge(sV.h, sV.n)
        sV.inc(ve.tensor_mul(self.CT, self.CT,
                             self.GS2C.unsqueeze(2).broadcast_to([128, 16, 8])))
        ve.wait_ge(sV.h, sV.n)
        v_ct = sV.n
        te.wait_ge(sV.h, v_ct)
        for w in range(4):
            for k in range(4):
                j = 4 * w + k
                sT.inc(te.matmul(PS[0:8, 5, 128 * k:128 * (k + 1)],
                                 self.CT[:, j, :], self.IDENTF,
                                 is_transpose=True, start=(k == 0), stop=(k == 3)))
            ve.wait_ge(sT.h, sT.n)
            sV.inc(ve.tensor_copy(self.CSB[0:8, 512 * w:512 * (w + 1)], PS[0:8, 5, :]))
            te.wait_ge(sV.h, sV.n)
        self.v_csb = sV.n
        self.t_gate_end = sT.n

    def moe(self):
        nc = self.nc
        te, ve, se, sy, ge = nc.tensor, nc.vector, nc.scalar, nc.sync, nc.gpsimd
        PS, Y1 = self.PS, self.Y1
        sT, sV, sS = self.sT, self.sV, self.sS

        se.preload_act_table(AF.Silu)
        mw1_r = self.d_mw1.rearrange("e (c p) n -> e p c n", p=128)
        ge.wait_ge(sT.h, self.t_gate_end)
        self.cW1E.inc(ge.dma_start(out=self.W1EB, in_=mw1_r[0]), 16)

        def w2_dma(e, m):
            sl = (e * DC + m) % 3
            self.cW2E[sl].inc(ge.dma_start(
                out=self.W2E[:, sl, :, :],
                in_=self.d_mw2[e, :, 128 * m:128 * (m + 1)]
                    .rearrange("(a p) n -> p a n", p=128)), 16)
        w2_dma(0, 0)
        w2_dma(0, 1)
        w2_dma(0, 2)

        v_y4 = [[0] * NTT for _ in range(DC)]
        v_bev = [0, 0]
        t_w2read = [0, 0, 0]
        v_a = [0] * NTT
        s_sg4 = [0] * NTT
        v_cbc = 0
        for e in range(E):
            if e == 0:
                te.wait_ge(sV.h, self.v_csb)
                te.wait_ge(self.cSEL.h, 16)
            else:
                te.wait_ge(sV.h, v_bev[0])
            t_cb = [0] * NTT
            for tt in range(NTT):
                sT.inc(te.matmul(PS[:, tt, :], self.SEL8[:, 128 * e:128 * (e + 1)],
                                 self.CSB[0:8, TT * tt:TT * (tt + 1)],
                                 start=True, stop=True))
                t_cb[tt] = sT.n
            for tt in range(NTT):
                ve.wait_ge(sT.h, t_cb[tt])
                sV.inc(ve.tensor_copy(self.CBC[:, TT * tt:TT * (tt + 1)], PS[:, tt, :]))
            v_cbc = sV.n
            te.wait_ge(sV.h, v_cbc)

            te.wait_ge(self.cW1E.h, 16 * (e + 1))
            if e > 0:
                te.wait_ge(sV.h, v_bev[1])
            for jj in range(DC):
                mm = None
                for dc in range(DC):
                    if e == 0 and jj == 0:
                        te.wait_ge(sS.h, self.s_x1bf[dc])
                    for tt in range(NTT):
                        if dc == 0:
                            te.wait_ge(sS.h, s_sg4[tt])
                        mm = te.matmul(PS[:, tt, :], self.W1EB[:, dc, 128 * jj:128 * (jj + 1)],
                                       self.X1BF[:, dc, TT * tt:TT * (tt + 1)],
                                       start=(dc == 0), stop=(dc == DC - 1))
                sT.inc(mm)
                t_g = sT.n
                mm = None
                for dc in range(DC):
                    for tt in range(NTT):
                        if dc == 0:
                            te.wait_ge(sV.h, v_a[tt])
                        mm = te.matmul(PS[:, 4 + tt, :],
                                       self.W1EB[:, dc, 1024 + 128 * jj:1024 + 128 * (jj + 1)],
                                       self.X1BF[:, dc, TT * tt:TT * (tt + 1)],
                                       start=(dc == 0), stop=(dc == DC - 1))
                sT.inc(mm)
                t_u = sT.n
                for tt in range(NTT):
                    kb = tt % 2
                    se.wait_ge(sT.h, t_g)
                    se.wait_ge(sV.h, v_a[tt - 2] if tt >= 2 else v_a[tt + 2])
                    sS.inc(se.activation(self.SGB[:, kb, :], PS[:, tt, :], AF.Silu))
                    s_sg4[tt] = sS.n
                    ve.wait_ge(sS.h, s_sg4[tt])
                    ve.wait_ge(sT.h, t_u)
                    sV.inc(ve.tensor_mul(self.ABF[:, jj, TT * tt:TT * (tt + 1)],
                                         self.SGB[:, kb, :], PS[:, 4 + tt, :]))
                    ve.wait_ge(sV.h, sV.n)
                    sV.inc(ve.tensor_mul(self.ABF[:, jj, TT * tt:TT * (tt + 1)],
                                         self.ABF[:, jj, TT * tt:TT * (tt + 1)],
                                         self.CBC[:, TT * tt:TT * (tt + 1)]))
                    v_a[tt] = sV.n
            t_w1_end = sT.n
            if e + 1 < E:
                ge.wait_ge(sT.h, t_w1_end)
                self.cW1E.inc(ge.dma_start(out=self.W1EB, in_=mw1_r[e + 1]), 16)

            te.wait_ge(sS.h, max(s_sg4))
            te.wait_ge(sV.h, max(v_a))
            for m in range(DC):
                bk = (m % 2) * 4
                gidx = e * DC + m
                sl = gidx % 3
                te.wait_ge(self.cW2E[sl].h, self.cW2E[sl].n)
                if m >= 2 or e > 0:
                    te.wait_ge(sV.h, v_bev[m % 2])
                mm = None
                for dffc in range(DC):
                    for tt in range(NTT):
                        mm = te.matmul(PS[:, bk + tt, :], self.W2E[:, sl, dffc, :],
                                       self.ABF[:, dffc, TT * tt:TT * (tt + 1)],
                                       start=(dffc == 0), stop=(dffc == DC - 1))
                sT.inc(mm)
                t_w2read[sl] = sT.n
                t_this = sT.n
                ngidx = gidx + 3
                if ngidx < E * DC:
                    ne, nm = divmod(ngidx, DC)
                    ge.wait_ge(sT.h, t_this)
                    w2_dma(ne, nm)
                ve.wait_ge(sT.h, t_this)
                ev = None
                for tt in range(NTT):
                    ve.wait_ge(sV.h, v_y4[m][tt])
                    ev = ve.tensor_add(Y1[:, m, TT * tt:TT * (tt + 1)],
                                       Y1[:, m, TT * tt:TT * (tt + 1)],
                                       PS[:, bk + tt, :])
                    sV.inc(ev)
                    v_y4[m][tt] = sV.n
                v_bev[m % 2] = sV.n
        self.v_y4 = v_y4
        self.v_bev_moe = v_bev

    def lnd_out(self):
        nc = self.nc
        te, ve, se, sy = nc.tensor, nc.vector, nc.scalar, nc.sync
        PS, Y1 = self.PS, self.Y1
        sT, sV, sS, sO = self.sT, self.sV, self.sS, self.sO

        self.preload_set("natural_log_exp_and_others")
        v_y4 = self.v_y4
        v_bev = self.v_bev_moe

        def gate_d(dc, tt):
            g = [(sV.h, v_y4[dc][tt])]
            if dc == 0 and tt == 0:
                g.append((sV.h, v_bev[0]))
                g.append((sV.h, v_bev[1]))
            return g
        self.ln_stats(Y1, gate=gate_d)
        s_rows = self.ln_rows(self.ROW2A, self.ROW2B, self.ROW2T)
        t_bc = self.ln_bcast(self.ROW2T, self.ROW2A, s_rows)
        outr = self.d_out.rearrange("(c p) n -> p c n", p=128)
        for dc in range(DC):
            k = dc % 2
            ve.wait_ge(sT.h, t_bc)
            if dc >= 2:
                ve.wait_ge(self.cOUT[k].h, self.cOUT[k].n)
            sV.inc(ve.tensor_sub(self.OSTG[:, k, :], Y1[:, dc, :], PS[:, 0:4, :]))
            ve.wait_ge(sV.h, sV.n)
            sV.inc(ve.tensor_mul(self.OSTG[:, k, :], self.OSTG[:, k, :], PS[:, 4:8, :]))
            sy.wait_ge(sV.h, sV.n)
            self.cOUT[k].inc(sy.dma_start(out=outr[:, dc, :], in_=self.OSTG[:, k, :]), 16)
        sy.wait_ge(self.cOUT[0].h, self.cOUT[0].n)
        sy.wait_ge(self.cOUT[1].h, self.cOUT[1].n)

_NC_CACHE = {}


def get_nc(stage="full"):
    if stage not in _NC_CACHE:
        _NC_CACHE[stage] = Builder(stage).nc
    return _NC_CACHE[stage]


def make_masks(core):
    t0 = 2048 * (core % 4)
    jj = np.arange(128)[:, None]
    ii = np.arange(256)[None, :]
    bm = np.zeros((6, 128, 256), np.float32)
    for b in range(6):
        j = 128 * b + jj
        bm[b] = ((ii <= j) & (j <= ii + 512)).astype(np.float32)
    em0 = np.zeros_like(bm)
    em1 = np.zeros_like(bm)
    for b in range(6):
        j = 128 * b + jj
        k0 = t0 - 256 + j
        k7 = t0 + 1792 - 256 + j
        em0[b] = bm[b] * ((k0 >= 0) & (k0 < S))
        em1[b] = bm[b] * ((k7 >= 0) & (k7 < S))
    bf = ml_dtypes.bfloat16
    return (bm[[0, 1, 4, 5]].astype(bf), em0.astype(bf), em1.astype(bf))


def make_inputs(core, X):
    b, t0 = core // 4, 2048 * (core % 4)
    src = np.asarray(X["L0_attn_src"][b], np.float32)
    pad = np.zeros((TH, D), np.float32)
    lo, hi = max(0, t0 - HALO), min(S, t0 + T + HALO)
    pad[lo - (t0 - HALO):hi - (t0 - HALO)] = src[lo:hi]
    srcT = np.ascontiguousarray(pad.T)
    bm, em0, em1 = make_masks(core)
    tmask = np.stack([bm[0][:, 0:128], bm[1][:, 128:256],
                      bm[2][:, 0:128], bm[3][:, 128:256]])
    vec6 = np.stack([
        np.asarray(X["L0_attn_b2"], np.float32).reshape(8, 128).T,
        np.asarray(X["L0_attn_ln1_g"], np.float32).reshape(8, 128).T,
        np.asarray(X["L0_attn_ln1_b"], np.float32).reshape(8, 128).T,
        np.asarray(X["L0_attn_ln2_g"], np.float32).reshape(8, 128).T,
        np.asarray(X["L0_attn_ln2_b"], np.float32).reshape(8, 128).T,
    ])
    return {
        "srcT": srcT,
        "w1": np.ascontiguousarray(np.asarray(X["L0_attn_W1"], np.float32)),
        "w2": np.ascontiguousarray(np.asarray(X["L0_attn_W2"], np.float32)),
        "b1T": np.ascontiguousarray(np.asarray(X["L0_attn_b1"], np.float32).reshape(32, 128).T),
        "vec6": np.ascontiguousarray(vec6),
        "bg": np.asarray(X["L0_ffn_bg"], np.float32).reshape(8, 1),
        "wgT": np.ascontiguousarray(np.asarray(X["L0_ffn_Wg"], np.float32).T),
        "mw1": np.ascontiguousarray(np.asarray(X["L0_ffn_w1"], np.float32).transpose(0, 2, 1)),
        "mw2": np.ascontiguousarray(np.asarray(X["L0_ffn_w2"], np.float32).transpose(0, 2, 1)),
        "bmask": np.ascontiguousarray(bm),
        "tmask": np.ascontiguousarray(tmask),
        "em0": np.ascontiguousarray(em0),
        "em1": np.ascontiguousarray(em1),
        "identm": np.eye(128, dtype=ml_dtypes.bfloat16),
        "sel8": np.ascontiguousarray(np.kron(np.eye(8, dtype=np.float32), np.ones((1, 128), np.float32)).astype(ml_dtypes.bfloat16)),
    }


def kernel(**inputs):
    nc = get_nc("full")
    in_maps = [make_inputs(c, inputs) for c in range(NCORES)]
    res = run_bass_kernel_spmd(nc, in_maps, core_ids=list(range(NCORES)))
    out = np.empty((2, S, D), np.float32)
    for c in range(NCORES):
        b, t0 = c // 4, 2048 * (c % 4)
        out[b, t0:t0 + T] = np.asarray(res.results[c]["outT"]).T
    return out


# version 10
# speedup vs baseline: 1.3105x; 1.0721x over previous
import numpy as np
import ml_dtypes
import concourse.bass as bass
import concourse.mybir as mybir
from concourse.bass_utils import run_bass_kernel_spmd

F32 = mybir.dt.float32
BF16 = mybir.dt.bfloat16
AF = mybir.ActivationFunctionType
ALU = mybir.AluOpType
AX = mybir.AxisListType

NCORES = 8
T = 2048
HALO = 256
TH = T + 2 * HALO
D = 1024
DC = 8
H = 16
DH = 64
NG = T // 256
S = 8192
DFF = 4096
NQ = 4
NTT = 4
TT = 512
E = 8
MDFF = 2048
LN_EPS = 1e-5
NARROW = True

SBUF_BYTES = 229376 - 64
SBUF_BASE = 16512

OFF_Y1 = SBUF_BASE
OFF_VEC = SBUF_BASE + 65536
PHASE = SBUF_BASE + 69632

OFF_SRCB = PHASE
OFF_VTOK = OFF_SRCB + 40960
OFF_BMASK = OFF_VTOK + 41600
OFF_EM0 = OFF_BMASK + 2048
OFF_EM1 = OFF_EM0 + 3072
OFF_ESB = OFF_EM1 + 3072
OFF_PVS = OFF_ESB + 9216
OFF_RDEN = OFF_PVS + 2048
OFF_PVN = OFF_RDEN + 1024
OFF_W1QA = SBUF_BYTES - 16384
OFF_STB = SBUF_BASE + 182272
OFF_STB2 = OFF_STB + 4096
OFF_TM = OFF_STB2 + 4096
OFF_SRTB = OFF_TM + 1024
OFF_EX8 = OFF_SRTB + 512
OFF_IND = OFF_EX8 + 512
OFF_GS2C = OFF_IND + 512

OFF_ROWA = PHASE
OFF_ROWB = OFF_ROWA + 8192
OFF_ROWT = OFF_ROWB + 8192
OFF_UBF = PHASE + 24576
OFF_HQ = OFF_UBF + 32768
OFF_W1QB = OFF_HQ + 32768
OFF_W2S = OFF_W1QB + 16384
OFF_SRC2 = OFF_UBF

OFF_X1BF = OFF_UBF
OFF_ABF = OFF_HQ
OFF_W1E = OFF_HQ + 32768
OFF_CSB = PHASE
OFF_CB = PHASE + 4096
OFF_SEL = PHASE + 8192
OFF_SG = PHASE + 10240
OFF_W2E = PHASE + 14336
OFF_ROW2A = OFF_W1E
OFF_ROW2B = OFF_W1E + 8192
OFF_ROW2T = OFF_W1E + 16384
OFF_LG = OFF_W1E + 24576
OFF_OSTG = OFF_X1BF

VEC = {}
_vo = OFF_VEC
def _vecslot(name, nbytes, align=32):
    global _vo
    _vo = (_vo + align - 1) // align * align
    VEC[name] = _vo
    _vo += nbytes
_vecslot("b1T", 32 * 4)
_vecslot("b2T", 8 * 4)
_vecslot("g1T", 8 * 4)
_vecslot("b1lT", 8 * 4)
_vecslot("g2T", 8 * 4)
_vecslot("b2lT", 8 * 4)
_vecslot("bg", 4)
_vecslot("negM", 4)
_vecslot("mrow", 4)
_vecslot("wg", 64 * 4)
_vecslot("wghi", 64 * 2)
_vecslot("wglo", 64 * 2)
_vecslot("ident", 128 * 2)
_vecslot("ones_bf", 2)
_vecslot("onesr_bf", 64 * 2)
_vecslot("lt", 16 * 8 * 4)
_vecslot("ct", 16 * 8 * 4)
_vecslot("srt", 8 * 4)
_vecslot("gs1", 4)
_vecslot("gs2", 4)
_vecslot("gs3", 8 * 4)
_vecslot("gs4", 8 * 4)
_vecslot("onesf", 128 * 4)
_vecslot("identf", 128 * 4)
_vecslot("eps1", 4)
_vecslot("cz", 4)
_vecslot("co", 4)
_vecslot("cn1", 4)
_vecslot("cnh", 4)
assert _vo <= PHASE, _vo
assert PHASE + 24576 + 32768 + 32768 + 16384 + 4096 <= OFF_W1QA


class Ctr:
    def __init__(self, nc, name):
        self.h = nc.alloc_semaphore(name)
        self.n = 0

    def inc(self, instr, k=1):
        instr.then_inc(self.h, k)
        self.n += k
        return self.n


class Builder:
    def __init__(self, stage="full"):
        self.stage = stage
        nc = self.nc = bass.Bass("TRN2", target_bir_lowering=False)
        assert nc.SBUF_PARTITION_SIZE_BYTES >= SBUF_BYTES
        assert nc.sbuf_base <= SBUF_BASE, nc.sbuf_base

        def dram(name, shape, kind="ExternalInput", dt=F32):
            return nc.dram_tensor(name, shape, dt, kind=kind).ap()

        self.d_srcT = dram("srcT", [D, TH])
        self.d_w1 = dram("w1", [D, DFF])
        self.d_w2 = dram("w2", [DFF, D])
        self.d_b1T = dram("b1T", [128, 32])
        self.d_vec6 = dram("vec6", [5, 128, 8])
        self.d_bg = dram("bg", [8, 1])
        self.d_wgT = dram("wgT", [D, 8])
        self.d_mw1 = dram("mw1", [E, D, MDFF])
        self.d_mw2 = dram("mw2", [E, D, D])
        self.d_bmask = dram("bmask", [4, 128, 256], dt=BF16)
        self.d_em0 = dram("em0", [6, 128, 256], dt=BF16)
        self.d_em1 = dram("em1", [6, 128, 256], dt=BF16)
        self.d_ident = dram("identm", [128, 128], dt=BF16)
        self.d_sel8 = dram("sel8", [8, 1024], dt=BF16)
        self.d_tmask = dram("tmask", [4, 128, 128], dt=BF16)
        self.d_out = dram("outT", [D, T], kind="ExternalOutput")

        sb = lambda name, shape, dt, off: nc.alloc_sbuf_tensor_at(name, shape, dt, offset=off).ap()
        self.Y1 = sb("y1", [128, DC, T], F32, OFF_Y1)
        self.SRCB = sb("srcb", [128, DC, TH], BF16, OFF_SRCB)
        self.VTOK = sb("vtok", [128, 20, H, 65], BF16, OFF_VTOK)
        self.BMASK = sb("bmask_s", [128, 4, 256], BF16, OFF_BMASK)
        self.EM0 = sb("em0_s", [128, 6, 256], BF16, OFF_EM0)
        self.EM1 = sb("em1_s", [128, 6, 256], BF16, OFF_EM1)
        self.ESB = sb("esb", [128, 3, 1536], BF16, OFF_ESB)
        self.PVS = sb("pvs", [65, 2, 256], F32, OFF_PVS)
        self.RDEN = sb("rden", [128, 2, 256], BF16, OFF_RDEN)
        self.PVN = sb("pvn", [64, 2, 256], BF16, OFF_PVN)
        self.W1QA = sb("w1qa", [128, DC, 1024], BF16, OFF_W1QA)
        self.W1QB = sb("w1qb", [128, DC, 1024], BF16, OFF_W1QB)
        self.W2S = sb("w2s", [128, 3, DC, 128], BF16, OFF_W2S)
        self.ROWA = sb("rowa", [1, T], F32, OFF_ROWA)
        self.ROWB = sb("rowb", [1, T], F32, OFF_ROWB)
        self.ROWT = sb("rowt", [1, T], F32, OFF_ROWT)
        self.UBF = sb("ubf", [128, DC, T], BF16, OFF_UBF)
        self.HQ = sb("hq", [128, DC, T], BF16, OFF_HQ)
        self.SRC2 = sb("src2", [128, DC, T], F32, OFF_SRC2)
        self.X1BF = sb("x1bf", [128, DC, T], BF16, OFF_X1BF)
        self.LOBF = sb("lobf", [128, DC, T], BF16, OFF_ABF)
        self.ABF = sb("abf", [128, DC, T], BF16, OFF_ABF)
        self.W1EB = sb("w1e", [128, DC, MDFF], BF16, OFF_W1E)
        self.CSB = sb("csb", [8, T], BF16, OFF_CSB)
        self.SEL8 = sb("sel8_s", [8, 1024], BF16, OFF_SEL)
        self.TMASK = sb("tmask_s", [128, 4, 128], BF16, OFF_TM)
        self.SRTB = sb("srtb_s", [128, 16, 8], F32, OFF_SRTB)
        self.EX8 = sb("ex8_s", [128, 16, 8], F32, OFF_EX8)
        self.IND = sb("ind_s", [128, 16, 8], F32, OFF_IND)
        self.GS2C = sb("gs2c_s", [128, 16], F32, OFF_GS2C)
        self.CBC = sb("cb", [128, T], BF16, OFF_CB)
        self.SGB = sb("sgb", [128, 4, TT], BF16, OFF_SG)
        self.W2E = sb("w2e", [128, 3, DC, 128], BF16, OFF_W2E)
        self.ROW2A = sb("row2a", [1, T], F32, OFF_ROW2A)
        self.ROW2B = sb("row2b", [1, T], F32, OFF_ROW2B)
        self.ROW2T = sb("row2t", [1, T], F32, OFF_ROW2T)
        self.LGS = sb("lg", [8, T], F32, OFF_LG)
        self.OSTG = sb("ostg", [128, 2, T], F32, OFF_OSTG)

        self.B1T = sb("b1t_s", [128, 32], F32, VEC["b1T"])
        self.B2T = sb("b2t_s", [128, 8], F32, VEC["b2T"])
        self.G1T = sb("g1t_s", [128, 8], F32, VEC["g1T"])
        self.B1LT = sb("b1lt_s", [128, 8], F32, VEC["b1lT"])
        self.G2T = sb("g2t_s", [128, 8], F32, VEC["g2T"])
        self.B2LT = sb("b2lt_s", [128, 8], F32, VEC["b2lT"])
        self.BG = sb("bg_s", [8, 1], F32, VEC["bg"])
        self.NEGM = sb("negm_s", [128, 1], F32, VEC["negM"])
        self.MROW = sb("mrow_s", [128, 1], F32, VEC["mrow"])
        self.WG = sb("wg_s", [128, DC, 8], F32, VEC["wg"])
        self.WGHI = sb("wghi_s", [128, DC, 8], BF16, VEC["wghi"])
        self.WGLO = sb("wglo_s", [128, DC, 8], BF16, VEC["wglo"])
        self.IDENT = sb("ident_s", [128, 128], BF16, VEC["ident"])
        self.ONES = sb("ones_s", [128, 1], BF16, VEC["ones_bf"])
        self.ONESR = sb("onesr_s", [128, 64], BF16, VEC["onesr_bf"])
        self.LT = sb("lt_s", [128, 16, 8], F32, VEC["lt"])
        self.CT = sb("ct_s", [128, 16, 8], F32, VEC["ct"])
        self.SRT = sb("srt_s", [128, 8], F32, VEC["srt"])
        self.GS1 = sb("gs1_s", [128, 1], F32, VEC["gs1"])
        self.GS2 = sb("gs2_s", [128, 1], F32, VEC["gs2"])
        self.GS3 = sb("gs3_s", [128, 8], F32, VEC["gs3"])
        self.GS4 = sb("gs4_s", [128, 8], F32, VEC["gs4"])
        self.ONESF = sb("onesf_s", [1, 128], F32, VEC["onesf"])
        self.IDENTF = sb("identf_s", [128, 128], F32, VEC["identf"])
        self.STB = sb("stb_s", [128, 4, 512], BF16, OFF_STB)
        self.STB2 = sb("stb2_s", [128, 4, 512], BF16, OFF_STB2)
        self.EPS1 = sb("eps1_s", [1, 1], F32, VEC["eps1"])
        self.CZ = sb("cz_s", [128, 1], F32, VEC["cz"])
        self.CO = sb("co_s", [128, 1], F32, VEC["co"])
        nc.const_aps.register(F32, 0.0, self.CZ)
        nc.const_aps.register(F32, 1.0, self.CO)
        self.CN1 = sb("cn1_s", [128, 1], F32, VEC["cn1"])
        self.CNH = sb("cnh_s", [128, 1], F32, VEC["cnh"])
        nc.const_aps.register(F32, -1.0, self.CN1)
        nc.const_aps.register(F32, -0.5, self.CNH)

        self.PS = nc.alloc_psum_tensor("ps", [128, 8, 512], F32).ap()
        self.PSB = self.PS.bitcast(BF16)

        self.sT = Ctr(nc, "c_st")
        self.sV = Ctr(nc, "c_sv")
        self.sS = Ctr(nc, "c_ss")
        self.sP = Ctr(nc, "c_sp")
        self.sO = Ctr(nc, "c_so")
        self.cY1 = Ctr(nc, "c_y1")
        self.cCST = Ctr(nc, "c_cst")
        self.cSRCB = Ctr(nc, "c_srcb")
        self.cW1A = Ctr(nc, "c_w1a")
        self.cW1B = Ctr(nc, "c_w1b")
        self.cW2S = [Ctr(nc, "c_w2s0"), Ctr(nc, "c_w2s1"), Ctr(nc, "c_w2s2")]
        self.cSRC2 = Ctr(nc, "c_src2")
        self.cW1E = Ctr(nc, "c_w1e")
        self.cSEL = Ctr(nc, "c_sel")
        self.cOUT = [Ctr(nc, "c_out0"), Ctr(nc, "c_out1")]
        self.cW2E = [Ctr(nc, "c_w2e0"), Ctr(nc, "c_w2e1"), Ctr(nc, "c_w2e2")]

        self.build()

    def barrier(self):
        self.nc.all_engine_barrier()

    def preload_set(self, name):
        from concourse.hw_specs import get_activation_tables
        names = list(get_activation_tables(self.nc.m.arch).keys())
        idx = names.index(name)
        se = self.nc.scalar
        se.add_instruction(mybir.InstLoadActFuncSet(
            name=self.nc.get_next_instruction_name(), act_func_set_id=idx,
            ins=[], outs=[]))

    def build(self):
        nc = self.nc
        self.prologue()
        self.attn()
        if self.stage == "attn":
            self.emit_out(self.Y1)
            return
        self.lna_ffn()
        if self.stage == "ffn":
            self.emit_out(self.Y1)
            return
        self.lnb_lnc_gating()
        self.moe()
        self.lnd_out()

    def emit_out(self, plane):
        nc, sy = self.nc, self.nc.sync
        self.barrier()
        outr = self.d_out.rearrange("(c p) n -> p c n", p=128)
        for dc in range(DC):
            self.sO.inc(sy.dma_start(out=outr[:, dc, :], in_=plane[:, dc, :]), 16)
        sy.wait_ge(self.sO.h, self.sO.n)

    def prologue(self):
        nc = self.nc
        te, ve, se, sy, ge = nc.tensor, nc.vector, nc.scalar, nc.sync, nc.gpsimd
        src_r = self.d_srcT.rearrange("(c p) n -> p c n", p=128)
        self.cY1.inc(sy.dma_start(out=self.Y1, in_=src_r[:, :, HALO:HALO + T]), 16)
        self.cCST.inc(sy.dma_start(out=self.BMASK, in_=self.d_bmask.rearrange("c p n -> p c n")), 16)
        self.cCST.inc(sy.dma_start(out=self.EM0, in_=self.d_em0.rearrange("c p n -> p c n")), 16)
        self.cCST.inc(sy.dma_start(out=self.EM1, in_=self.d_em1.rearrange("c p n -> p c n")), 16)
        self.cCST.inc(sy.dma_start(out=self.IDENT, in_=self.d_ident), 16)
        self.cCST.inc(sy.dma_start(out=self.B1T, in_=self.d_b1T), 16)
        for i, t in enumerate([self.B2T, self.G1T, self.B1LT, self.G2T, self.B2LT]):
            self.cCST.inc(sy.dma_start(out=t, in_=self.d_vec6[i]), 16)
        self.cCST.inc(sy.dma_start(out=self.BG, in_=self.d_bg), 16)
        self.cCST.inc(sy.dma_start(out=self.TMASK, in_=self.d_tmask.rearrange("c p n -> p c n")), 16)
        self.cCST.inc(sy.dma_start(out=self.WG, in_=self.d_wgT.rearrange("(c p) e -> p c e", p=128)), 16)
        self.cSRCB.inc(ge.dma_start(out=self.SRCB, in_=src_r), 16)
        self.w1_r = self.d_w1.rearrange("(c p) (q n) -> q p c n", p=128, n=1024)
        self.cW1A.inc(ge.dma_start(out=self.W1QA, in_=self.w1_r[0]), 16)

        ve.memset(self.ONES, 1.0)
        ve.memset(self.ONESR[64:65, :], 1.0)
        ve.memset(self.ONESF, 1.0)
        ve.memset(self.EPS1, float(LN_EPS))
        ve.memzero(self.CZ)
        ve.memset(self.CO, 1.0)
        ve.memset(self.CN1, -1.0)
        ve.memset(self.CNH, -0.5)
        ve.wait_ge(self.cCST.h, self.cCST.n)
        ve.tensor_copy(self.IDENTF, self.IDENT)
        self.sV.inc(ve.memset(self.VTOK[:, :, :, 64:65], 1.0))
        self.v_ones = self.sV.n

    def attn(self):
        nc = self.nc
        te, ve, se, ge = nc.tensor, nc.vector, nc.scalar, nc.gpsimd
        PS, SRCB, ESB, VTOK, PVS, RDEN, Y1 = (self.PS, self.SRCB, self.ESB, self.VTOK,
                                              self.PVS, self.RDEN, self.Y1)
        sT, sV, sS, sP = self.sT, self.sV, self.sS, self.sP

        self.preload_set("natural_log_exp_and_others")

        te.wait_ge(self.cSRCB.h, self.cSRCB.n)
        te.wait_ge(self.cCST.h, self.cCST.n)
        ve.wait_ge(self.cCST.h, self.cCST.n)
        sT.inc(te.matmul(PS[:, 0, 0:320], SRCB[0:64, 0, 256:384],
                         SRCB[0:64, 0, 0:2560:8], start=True, stop=True))
        ve.wait_ge(sT.h, sT.n)
        sV.inc(ve.tensor_reduce(self.MROW, PS[:, 0, 0:320], AX.X, ALU.max))
        te.wait_ge(sV.h, sV.n)
        sT.inc(te.matmul(PS[0:1, 1, 0:128], self.MROW, self.IDENTF,
                         is_transpose=True, start=True, stop=True))
        ve.wait_ge(sT.h, sT.n)
        sV.inc(ve.tensor_reduce(self.MROW[0:1, :], PS[0:1, 1, 0:128], AX.X, ALU.max))
        te.wait_ge(sV.h, sV.n)
        sT.inc(te.matmul(PS[:, 1, 0:1], self.ONESF, self.MROW[0:1, :],
                         start=True, stop=True))
        ve.wait_ge(sT.h, sT.n)
        sV.inc(ve.tensor_scalar(self.NEGM, PS[:, 1, 0:1], -1.0, -8.0, ALU.mult, ALU.add))
        v_negm = sV.n
        se.wait_ge(sV.h, v_negm)
        te.wait_ge(sV.h, v_negm)

        NI = NG * H
        t_qk = [0] * NI
        t_pv = [0] * NI
        t_bc = [0] * NI
        t_sh = [0] * NI
        s_exp = [0] * NI
        v_cp = [0] * NI
        v_msk = [0] * NI
        s_rec = [0] * NI
        v_mul = [0] * NI
        v_acc = [0] * NI
        built = set()

        def chunk_cols(gg, b):
            if not NARROW or gg in (0, NG - 1):
                return 0, 256
            if b == 0:
                return 0, 128
            if b == 5:
                return 128, 256
            return 0, 256

        ve.wait_ge(self.cY1.h, self.cY1.n)
        ve.wait_ge(sV.h, self.v_ones)

        def build_vhat(kcs):
            for kc in kcs:
                if kc in built:
                    continue
                built.add(kc)
                for wave in range(2):
                    for k in range(4):
                        dc = 4 * wave + k
                        sT.inc(te.matmul(self.PSB[:, 0, 128 * k:128 * (k + 1)],
                                         SRCB[:, dc, 128 * kc:128 * (kc + 1)],
                                         self.IDENT, is_transpose=True,
                                         start=(k == 0), stop=(k == 3)))
                    ve.wait_ge(sT.h, sT.n)
                    src_ap = self.PSB[:, 0, 0:512] \
                        .rearrange("p (a b) -> p a b", a=4) \
                        .rearrange("p a (c d) -> p (a c) d", c=2)
                    sV.inc(ve.tensor_copy(VTOK[:, kc, 8 * wave:8 * wave + 8, 0:64], src_ap))
                    te.wait_ge(sV.h, sV.n)

        def emit_qk(i):
            gg, h = divmod(i, H)
            dc, r0 = h // 2, 64 * (h % 2)
            par = i % 2
            if i >= 2:
                te.wait_ge(sS.h, s_exp[i - 2])
            mm = None
            for b in range(6):
                kc = 2 * gg + b
                c0, c1 = chunk_cols(gg, b)
                mm = te.matmul(PS[:, 3 * par + b // 2, 256 * (b % 2) + c0:256 * (b % 2) + c1],
                               SRCB[r0:r0 + 64, dc, 128 * kc:128 * kc + 128],
                               SRCB[r0:r0 + 64, dc, 256 + 256 * gg + c0:256 + 256 * gg + c1],
                               start=(b % 2 == 0), stop=(b % 2 == 1))
            sT.inc(mm)
            t_qk[i] = sT.n

        def emit_exp(i):
            par3 = i % 3
            se.wait_ge(sT.h, t_qk[i])
            if i >= 3:
                se.wait_ge(sT.h, t_pv[i - 3])
            sS.inc(se.activation(ESB[:, par3, :], PS[:, 3 * (i % 2):3 * (i % 2) + 3, :],
                                 AF.Exp, bias=self.NEGM))
            s_exp[i] = sS.n

        def emit_mask(i):
            gg = i // H
            par3 = i % 3
            ve.wait_ge(sS.h, s_exp[i])
            mm = None
            if gg == 0:
                for (ec, mc) in ((0, 0), (2, 2), (4, 4)):
                    mm = ve.tensor_mul(ESB[:, par3, 256 * ec:256 * ec + 512],
                                       ESB[:, par3, 256 * ec:256 * ec + 512],
                                       self.EM0[:, mc:mc + 2, :])
            elif gg == NG - 1:
                for (ec, mc) in ((0, 0), (2, 2), (4, 4)):
                    mm = ve.tensor_mul(ESB[:, par3, 256 * ec:256 * ec + 512],
                                       ESB[:, par3, 256 * ec:256 * ec + 512],
                                       self.EM1[:, mc:mc + 2, :])
            elif NARROW:
                tri = bass.AP(ESB.tensor, ESB.offset + par3 * 1536,
                              [[4608, 128], [1024, 2], [384, 2], [1, 128]])
                tm = self.TMASK.rearrange("p (a c) x -> p a c x", a=2)
                mm = ve.tensor_mul(tri, tri, tm)
            else:
                for (ec, mc) in ((0, 0), (4, 2)):
                    mm = ve.tensor_mul(ESB[:, par3, 256 * ec:256 * ec + 512],
                                       ESB[:, par3, 256 * ec:256 * ec + 512],
                                       self.BMASK[:, mc:mc + 2, :])
            sV.inc(mm)
            v_msk[i] = sV.n

        def emit_pv(i):
            gg, h = divmod(i, H)
            par = i % 2
            par3 = i % 3
            te.wait_ge(sV.h, v_msk[i])
            if i >= 2:
                te.wait_ge(sV.h, v_acc[i - 2])
            mm = None
            bs = range(6) if (not NARROW or gg in (0, NG - 1)) else (1, 0, 2, 3, 4, 5)
            for bi, b in enumerate(bs):
                kc = 2 * gg + b
                c0, c1 = chunk_cols(gg, b)
                mm = te.matmul(PS[0:65, 6 + par, c0:c1],
                               VTOK[:, kc, h, :],
                               ESB[:, par3, 256 * b + c0:256 * b + c1],
                               start=(bi == 0), stop=(bi == 5))
            sT.inc(mm)
            t_pv[i] = sT.n

        def emit_copy(i):
            par = i % 2
            se.wait_ge(sT.h, t_pv[i])
            sS.inc(se.copy(PVS[:, par, :], PS[0:65, 6 + par, 0:256]))
            v_cp[i] = sS.n

        def emit_recip(i):
            par = i % 2
            ve.wait_ge(sS.h, v_cp[i])
            if i >= 2:
                ve.wait_ge(sT.h, t_bc[i - 2])
            with nc.allow_low_precision(reason="softmax denom reciprocal feeds a bf16 rank-1 broadcast matmul"):
                sV.inc(ve.reciprocal(RDEN[64:65, par, :], PVS[64:65, par, :]))
            s_rec[i] = sV.n

        def emit_bcast(i):
            par = i % 2
            te.wait_ge(sV.h, s_rec[i])
            sT.inc(te.matmul(PS[0:64, 6 + par, 256:512], self.ONESR[64:65, :],
                             RDEN[64:65, par, :], start=True, stop=True))
            t_bc[i] = sT.n

        def emit_mult(i):
            par = i % 2
            ve.wait_ge(sT.h, t_bc[i])
            sV.inc(ve.tensor_mul(self.PVN[:, par, :], PVS[0:64, par, :],
                                 PS[0:64, 6 + par, 256:512]))
            v_mul[i] = sV.n

        def emit_shift(i):
            par = i % 2
            te.wait_ge(sV.h, v_mul[i])
            sT.inc(te.matmul(PS[64:128, 6 + par, 256:512], self.IDENT[0:64, 0:64],
                             self.PVN[:, par, :], start=True, stop=True))
            t_sh[i] = sT.n

        def emit_add(i):
            gg, h = divmod(i, H)
            dc, r0 = h // 2, 64 * (h % 2)
            par = i % 2
            y = Y1[r0:r0 + 64, dc, 256 * gg:256 * gg + 256]
            if h % 2 == 0:
                ve.wait_ge(sV.h, v_mul[i])
                sV.inc(ve.tensor_add(y, y, self.PVN[:, par, :]))
            else:
                ve.wait_ge(sT.h, t_sh[i])
                sV.inc(ve.tensor_add(y, y, PS[64:128, 6 + par, 256:512]))
            v_acc[i] = sV.n

        for i in range(NI + 4):
            j4 = i - 4
            if 0 <= j4 < NI:
                emit_mult(j4)
                if j4 % 2 == 1:
                    emit_shift(j4)
                emit_add(j4)
            if i < NI:
                if i % H == 0:
                    gg = i // H
                    kcs = list(range(6)) if gg == 0 else [2 * gg + 4, 2 * gg + 5]
                    if i >= 2:
                        te.wait_ge(sS.h, s_exp[i - 1])
                    build_vhat(kcs)
                emit_qk(i)
            if 0 <= i - 3 < NI:
                emit_copy(i - 3)
            if i < NI:
                emit_exp(i)
            if 0 <= i - 1 < NI:
                emit_mask(i - 1)
            if 0 <= i - 2 < NI:
                emit_pv(i - 2)
            if 0 <= i - 3 < NI:
                emit_recip(i - 3)
                emit_bcast(i - 3)

    def ln_stats(self, plane, gate=None):
        nc = self.nc
        te, ve, se = nc.tensor, nc.vector, nc.scalar
        sT, sV, sS = self.sT, self.sV, self.sS
        last_mm = [0, 0, 0, 0]
        for dc in range(DC):
            for tt in range(NTT):
                k = (dc * NTT + tt) % 4
                tile = plane[:, dc, TT * tt:TT * (tt + 1)]
                if gate is not None:
                    for (h, n) in gate(dc, tt):
                        ve.wait_ge(h, n)
                        se.wait_ge(h, n)
                        te.wait_ge(h, n)
                ve.wait_ge(sT.h, last_mm[k])
                se.wait_ge(sT.h, last_mm[k])
                sV.inc(ve.tensor_copy(self.STB[:, k, :], tile))
                sS.inc(se.activation(self.STB2[:, k, :], tile, AF.Square))
                te.wait_ge(sV.h, sV.n)
                te.wait_ge(sS.h, sS.n)
                te.matmul(self.PS[0:1, tt, :], self.ONES, self.STB[:, k, :],
                          start=(dc == 0), stop=(dc == DC - 1))
                sT.inc(te.matmul(self.PS[0:1, 4 + tt, :], self.ONES, self.STB2[:, k, :],
                                 start=(dc == 0), stop=(dc == DC - 1)))
                last_mm[k] = sT.n
        return sT.n

    def ln_rows(self, rowA, rowB, rowT):
        nc = self.nc
        ve, se = nc.vector, nc.scalar
        sV, sS = self.sV, self.sS
        sumv = self.PS[0:1, 0:4, :]
        ssv = self.PS[0:1, 4:8, :]
        ve.wait_ge(self.sT.h, self.sT.n)
        sV.inc(ve.tensor_scalar_mul(rowT, sumv, 1.0 / D))
        ve.wait_ge(sV.h, sV.n)
        sV.inc(ve.tensor_mul(rowB, rowT, rowT))
        ve.wait_ge(sV.h, sV.n)
        sV.inc(ve.scalar_tensor_tensor(rowA, ssv, 1.0 / D, rowB, ALU.mult, ALU.subtract))
        se.wait_ge(sV.h, sV.n)
        sS.inc(se.activation(rowB, rowA, AF.Ln, bias=self.EPS1))
        se.wait_ge(sS.h, sS.n)
        sS.inc(se.activation(rowA, rowB, AF.Exp, scale=-0.5))
        return sS.n

    def ln_bcast(self, rowT, rowA, s_rows):
        te = self.nc.tensor
        te.wait_ge(self.sS.h, s_rows)
        for tt in range(NTT):
            te.matmul(self.PS[:, tt, :], self.ONESF, rowT[0:1, TT * tt:TT * (tt + 1)],
                      start=True, stop=True)
            self.sT.inc(te.matmul(self.PS[:, 4 + tt, :], self.ONESF,
                                  rowA[0:1, TT * tt:TT * (tt + 1)], start=True, stop=True))
        return self.sT.n

    def ln_apply_pass1(self, plane, t_bc):
        ve, sV = self.nc.vector, self.sV
        for dc in range(DC):
            ve.wait_ge(self.sT.h, t_bc)
            sV.inc(ve.tensor_sub(plane[:, dc, :], plane[:, dc, :], self.PS[:, 0:4, :]))
        return sV.n

    def lna_ffn(self):
        nc = self.nc
        te, ve, se, sy, ge = nc.tensor, nc.vector, nc.scalar, nc.sync, nc.gpsimd
        PS, Y1, UBF, HQ = self.PS, self.Y1, self.UBF, self.HQ
        sT, sV, sS = self.sT, self.sV, self.sS

        self.barrier()
        t_stats_a = self.ln_stats(Y1)
        s_rows = self.ln_rows(self.ROWA, self.ROWB, self.ROWT)
        t_bc = self.ln_bcast(self.ROWT, self.ROWA, s_rows)
        v_p1 = self.ln_apply_pass1(Y1, t_bc)
        v_apply = [0] * DC
        s_cast = [0] * DC
        ve.wait_ge(sV.h, v_p1)
        for dc in range(DC):
            sV.inc(ve.tensor_mul(Y1[:, dc, :], Y1[:, dc, :], PS[:, 4:8, :]))
            t1v = sV.n
            se.wait_ge(sV.h, t1v)
            sS.inc(se.activation(UBF[:, dc, :], Y1[:, dc, :], AF.Identity,
                                 scale=self.G1T[:, dc:dc + 1], bias=self.B1LT[:, dc:dc + 1]))
            s_cast[dc] = sS.n
            ve.wait_ge(sS.h, s_cast[dc])
            sV.inc(ve.tensor_scalar(Y1[:, dc, :], Y1[:, dc, :],
                                    self.G1T[:, dc:dc + 1], self.B1LT[:, dc:dc + 1],
                                    ALU.mult, ALU.add))
            v_apply[dc] = sV.n
        v_p2 = v_apply[DC - 1]

        w1buf = [self.W1QA, self.W1QB]
        w1sem = [self.cW1A, self.cW1B]
        v_y2 = [[0] * NTT for _ in range(DC)]
        s_hq = [0] * DC
        t_a = {}
        t_b = {}
        v_bev = [0, 0]
        t_w2read = [0, 0, 0]
        for q in range(NQ):
            wb = w1buf[q % 2]
            wsem = w1sem[q % 2]
            wcnt = 16 * (q // 2 + 1)
            if q + 1 < NQ:
                if q >= 1:
                    ge.wait_ge(sT.h, self.t_a_end)
                w1sem[(q + 1) % 2].inc(ge.dma_start(out=w1buf[(q + 1) % 2],
                                                    in_=self.w1_r[q + 1]), 16)
            if q == 0:
                for m0 in range(3):
                    self.cW2S[m0].inc(ge.dma_start(
                        out=self.W2S[:, m0, :, :],
                        in_=self.d_w2[1024 * q:1024 * (q + 1), 128 * m0:128 * (m0 + 1)]
                            .rearrange("(a p) n -> p a n", p=128)), 16)
            te.wait_ge(wsem.h, wcnt)
            if q > 0:
                te.wait_ge(sV.h, v_bev[0])
                te.wait_ge(sV.h, v_bev[1])
            for hc in range(DC):
                bk = (hc % 2) * 4
                if q == 0 and hc == 0:
                    te.wait_ge(sV.h, v_p1)
                if q == 0 and hc == 1:
                    te.wait_ge(sV.h, v_p2)
                if hc >= 2:
                    te.wait_ge(sS.h, s_hq[hc - 2])
                mm = None
                for dc in range(DC):
                    if q == 0 and hc == 0:
                        te.wait_ge(sS.h, s_cast[dc])
                    for tt in range(NTT):
                        mm = te.matmul(PS[:, bk + tt, :], wb[:, dc, 128 * hc:128 * (hc + 1)],
                                       UBF[:, dc, TT * tt:TT * (tt + 1)],
                                       start=(dc == 0), stop=(dc == DC - 1))
                sT.inc(mm)
                t_a[hc] = sT.n
                se.wait_ge(sT.h, t_a[hc])
                a = None
                for tt in range(NTT):
                    a = se.activation(HQ[:, hc, TT * tt:TT * (tt + 1)], PS[:, bk + tt, :],
                                      AF.Relu, bias=self.B1T[:, 8 * q + hc:8 * q + hc + 1])
                sS.inc(a)
                s_hq[hc] = sS.n
            self.t_a_end = t_a[DC - 1]
            for m in range(DC):
                bk = (m % 2) * 4
                gidx = q * DC + m
                sl = gidx % 3
                te.wait_ge(self.cW2S[sl].h, self.cW2S[sl].n)
                te.wait_ge(sS.h, s_hq[DC - 1])
                if m >= 2:
                    te.wait_ge(sV.h, v_bev[m % 2])
                mm = None
                for hc in range(DC):
                    for tt in range(NTT):
                        mm = te.matmul(PS[:, bk + tt, :], self.W2S[:, sl, hc, :],
                                       HQ[:, hc, TT * tt:TT * (tt + 1)],
                                       start=(hc == 0), stop=(hc == DC - 1))
                sT.inc(mm)
                t_b[m] = sT.n
                t_w2read[sl] = sT.n
                ngidx = gidx + 3
                if ngidx < NQ * DC:
                    nq, nm = divmod(ngidx, DC)
                    ge.wait_ge(sT.h, t_b[m])
                    self.cW2S[sl].inc(ge.dma_start(
                        out=self.W2S[:, sl, :, :],
                        in_=self.d_w2[1024 * nq:1024 * (nq + 1), 128 * nm:128 * (nm + 1)]
                            .rearrange("(a p) n -> p a n", p=128)), 16)
                ve.wait_ge(sT.h, t_b[m])
                ev = None
                for tt in range(NTT):
                    ve.wait_ge(sV.h, v_y2[m][tt])
                    if q == 0:
                        ev = ve.scalar_tensor_tensor(Y1[:, m, TT * tt:TT * (tt + 1)],
                                                     PS[:, bk + tt, :],
                                                     self.B2T[:, m:m + 1],
                                                     Y1[:, m, TT * tt:TT * (tt + 1)],
                                                     ALU.add, ALU.add)
                    else:
                        ev = ve.tensor_add(Y1[:, m, TT * tt:TT * (tt + 1)],
                                           Y1[:, m, TT * tt:TT * (tt + 1)],
                                           PS[:, bk + tt, :])
                    sV.inc(ev)
                    v_y2[m][tt] = sV.n
                v_bev[m % 2] = sV.n
        self.v_y2 = v_y2
        self.v_bev_ffn = v_bev
        self.t_ffn_end = sT.n

    def lnb_lnc_gating(self):
        nc = self.nc
        te, ve, se, sy, ge = nc.tensor, nc.vector, nc.scalar, nc.sync, nc.gpsimd
        PS, Y1, SRC2 = self.PS, self.Y1, self.SRC2
        sT, sV, sS = self.sT, self.sV, self.sS

        v_y2 = self.v_y2
        v_bev = self.v_bev_ffn

        def gate_b(dc, tt):
            g = [(sV.h, v_y2[dc][tt])]
            if dc == 0 and tt == 0:
                g.append((sV.h, v_bev[0]))
                g.append((sV.h, v_bev[1]))
            return g
        t_stats = self.ln_stats(Y1, gate=gate_b)
        sy.wait_ge(sT.h, self.t_ffn_end)
        src_r = self.d_srcT.rearrange("(c p) n -> p c n", p=128)
        self.cSRC2.inc(sy.dma_start(out=SRC2, in_=src_r[:, :, HALO:HALO + T]), 16)
        s_rows = self.ln_rows(self.ROWA, self.ROWB, self.ROWT)
        s_rows_b = s_rows
        t_bc = self.ln_bcast(self.ROWT, self.ROWA, s_rows)
        se.wait_ge(self.cSRC2.h, self.cSRC2.n)
        s_bias = [0] * DC
        for dc in range(DC):
            sS.inc(se.add(SRC2[:, dc, :], SRC2[:, dc, :], self.B2LT[:, dc:dc + 1]))
            s_bias[dc] = sS.n
        v_p1 = self.ln_apply_pass1(Y1, t_bc)
        v_y3 = [0] * DC
        ve.wait_ge(sV.h, v_p1)
        for dc in range(DC):
            sV.inc(ve.tensor_mul(Y1[:, dc, :], Y1[:, dc, :], PS[:, 4:8, :]))
            ve.wait_ge(sV.h, sV.n)
            ve.wait_ge(sS.h, s_bias[dc])
            sV.inc(ve.scalar_tensor_tensor(SRC2[:, dc, :], Y1[:, dc, :],
                                           self.G2T[:, dc:dc + 1], SRC2[:, dc, :],
                                           ALU.mult, ALU.add))
            v_y3[dc] = sV.n

        def gate_c(dc, tt):
            g = [(sV.h, v_y3[dc])]
            if dc == 0 and tt == 0:
                g.append((sV.h, v_y3[DC - 1]))
            return g
        t_stats = self.ln_stats(SRC2, gate=gate_c)
        s_rows = self.ln_rows(self.ROW2A, self.ROW2B, self.ROW2T)
        t_bc = self.ln_bcast(self.ROW2T, self.ROW2A, s_rows)
        s_x1bf = [0] * DC
        v_x1 = [0] * DC
        for dc in range(DC):
            ve.wait_ge(sT.h, t_bc)
            sV.inc(ve.tensor_sub(Y1[:, dc, :], SRC2[:, dc, :], PS[:, 0:4, :]))
            ve.wait_ge(sV.h, sV.n)
            sV.inc(ve.tensor_mul(Y1[:, dc, :], Y1[:, dc, :], PS[:, 4:8, :]))
            v_x1[dc] = sV.n
            se.wait_ge(sV.h, v_x1[dc])
            sS.inc(se.copy(self.X1BF[:, dc, :], Y1[:, dc, :]))
            s_x1bf[dc] = sS.n
        self.v_x1 = v_x1
        self.s_x1bf = s_x1bf

        sy.wait_ge(sS.h, s_rows_b)
        self.cSEL.inc(sy.dma_start(out=self.SEL8, in_=self.d_sel8), 16)
        sV.inc(ve.tensor_copy(self.WGHI, self.WG))
        ve.wait_ge(sV.h, sV.n)
        sV.inc(ve.scalar_tensor_tensor(self.WGLO, self.WGHI, -1.0, self.WG,
                                       ALU.mult, ALU.add))
        v_lo = [0] * DC
        for dc in range(DC):
            ve.wait_ge(sS.h, s_x1bf[dc])
            sV.inc(ve.scalar_tensor_tensor(self.LOBF[:, dc, :], self.X1BF[:, dc, :],
                                           -1.0, Y1[:, dc, :], ALU.mult, ALU.add))
            v_lo[dc] = sV.n
        started = [False] * NTT
        mm = None
        for dc in range(DC):
            te.wait_ge(sV.h, v_lo[dc])
            if dc == 0:
                te.wait_ge(sV.h, self.v_x1[DC - 1])
            for (wgt, xin) in ((self.WGHI, self.X1BF), (self.WGHI, self.LOBF),
                               (self.WGLO, self.X1BF)):
                for tt in range(NTT):
                    last = (dc == DC - 1) and (wgt is self.WGLO)
                    mm = te.matmul(PS[0:8, tt, :], wgt[:, dc, :],
                                   xin[:, dc, TT * tt:TT * (tt + 1)],
                                   start=(not started[tt]), stop=last)
                    started[tt] = True
        sT.inc(mm)
        t_lg = sT.n
        se.wait_ge(sT.h, t_lg)
        sS.inc(se.activation(self.LGS, PS[0:8, 0:4, :], AF.Identity, bias=self.BG))
        s_lg = sS.n
        te.wait_ge(sS.h, s_lg)
        for j in range(16):
            sT.inc(te.matmul(PS[:, 4, 8 * j:8 * (j + 1)], self.LGS[0:8, 128 * j:128 * (j + 1)],
                             self.IDENTF[0:8, 0:8], is_transpose=True,
                             start=(j == 0), stop=(j == 15)))
        t_tr = sT.n
        ve.wait_ge(sT.h, t_tr)
        sV.inc(ve.tensor_copy(self.LT.rearrange("p a b -> p (a b)"), PS[:, 4, 0:128]))
        ve.wait_ge(sV.h, sV.n)
        for j in range(16):
            sV.inc(ve.max(self.SRTB[:, j, :], self.LT[:, j, :]))
        ve.wait_ge(sV.h, sV.n)
        m1b = self.SRTB[:, :, 0:1].broadcast_to([128, 16, 8])
        m2b = self.SRTB[:, :, 1:2].broadcast_to([128, 16, 8])
        sV.inc(ve.tensor_sub(self.EX8, self.LT, m1b))
        sV.inc(ve.tensor_tensor(self.IND, self.LT, m2b, ALU.is_ge))
        sV.inc(ve.tensor_sub(self.GS2C, self.SRTB[:, :, 1], self.SRTB[:, :, 0]))
        se.wait_ge(sV.h, sV.n)
        sS.inc(se.activation(self.EX8, self.EX8, AF.Exp))
        sS.inc(se.activation(self.GS2C, self.GS2C, AF.Exp))
        ve.wait_ge(sS.h, sS.n)
        sV.inc(ve.tensor_scalar_add(self.GS2C, self.GS2C, 1.0))
        ve.wait_ge(sV.h, sV.n)
        sV.inc(ve.reciprocal(self.GS2C, self.GS2C))
        ve.wait_ge(sV.h, sV.n)
        sV.inc(ve.tensor_mul(self.CT, self.EX8, self.IND))
        ve.wait_ge(sV.h, sV.n)
        sV.inc(ve.tensor_mul(self.CT, self.CT,
                             self.GS2C.unsqueeze(2).broadcast_to([128, 16, 8])))
        ve.wait_ge(sV.h, sV.n)
        v_ct = sV.n
        te.wait_ge(sV.h, v_ct)
        for w in range(4):
            for k in range(4):
                j = 4 * w + k
                sT.inc(te.matmul(PS[0:8, 5, 128 * k:128 * (k + 1)],
                                 self.CT[:, j, :], self.IDENTF,
                                 is_transpose=True, start=(k == 0), stop=(k == 3)))
            ve.wait_ge(sT.h, sT.n)
            sV.inc(ve.tensor_copy(self.CSB[0:8, 512 * w:512 * (w + 1)], PS[0:8, 5, :]))
            te.wait_ge(sV.h, sV.n)
        self.v_csb = sV.n
        self.t_gate_end = sT.n

    def moe(self):
        nc = self.nc
        te, ve, se, sy, ge = nc.tensor, nc.vector, nc.scalar, nc.sync, nc.gpsimd
        PS, Y1 = self.PS, self.Y1
        sT, sV, sS = self.sT, self.sV, self.sS

        se.preload_act_table(AF.Silu)
        mw1_r = self.d_mw1.rearrange("e (c p) n -> e p c n", p=128)
        ge.wait_ge(sT.h, self.t_gate_end)
        self.cW1E.inc(ge.dma_start(out=self.W1EB, in_=mw1_r[0]), 16)

        def w2_dma(e, m):
            sl = (e * DC + m) % 3
            self.cW2E[sl].inc(ge.dma_start(
                out=self.W2E[:, sl, :, :],
                in_=self.d_mw2[e, :, 128 * m:128 * (m + 1)]
                    .rearrange("(a p) n -> p a n", p=128)), 16)
        w2_dma(0, 0)
        w2_dma(0, 1)
        w2_dma(0, 2)

        v_y4 = [[0] * NTT for _ in range(DC)]
        v_bev = [0, 0]
        t_w2read = [0, 0, 0]
        v_a = [0] * NTT
        s_sg4 = [0] * NTT
        v_cbc = 0
        for e in range(E):
            if e == 0:
                te.wait_ge(sV.h, self.v_csb)
                te.wait_ge(self.cSEL.h, 16)
            else:
                te.wait_ge(sV.h, v_bev[0])
                te.wait_ge(sV.h, max(v_a))
                te.wait_ge(sS.h, max(s_sg4))
            t_cb = [0] * NTT
            for tt in range(NTT):
                sT.inc(te.matmul(PS[:, tt, :], self.SEL8[:, 128 * e:128 * (e + 1)],
                                 self.CSB[0:8, TT * tt:TT * (tt + 1)],
                                 start=True, stop=True))
                t_cb[tt] = sT.n
            for tt in range(NTT):
                ve.wait_ge(sT.h, t_cb[tt])
                sV.inc(ve.tensor_copy(self.CBC[:, TT * tt:TT * (tt + 1)], PS[:, tt, :]))
            v_cbc = sV.n
            te.wait_ge(sV.h, v_cbc)

            te.wait_ge(self.cW1E.h, 16 * (e + 1))
            for jj in range(DC):
                for half in range(2):
                    b0 = 4 * half
                    toff = 1024 * half
                    mm = None
                    for dc in range(DC):
                        if e == 0 and jj == 0:
                            te.wait_ge(sS.h, self.s_x1bf[dc])
                        for tt in range(2):
                            if dc == 0:
                                te.wait_ge(sS.h, s_sg4[2 * half + tt])
                            mm = te.matmul(PS[:, b0 + tt, :],
                                           self.W1EB[:, dc, 128 * jj:128 * (jj + 1)],
                                           self.X1BF[:, dc, toff + TT * tt:toff + TT * (tt + 1)],
                                           start=(dc == 0), stop=(dc == DC - 1))
                    sT.inc(mm)
                    t_g = sT.n
                    mm = None
                    for dc in range(DC):
                        for tt in range(2):
                            if dc == 0:
                                te.wait_ge(sV.h, v_a[2 * half + tt])
                            mm = te.matmul(PS[:, b0 + 2 + tt, :],
                                           self.W1EB[:, dc, 1024 + 128 * jj:1024 + 128 * (jj + 1)],
                                           self.X1BF[:, dc, toff + TT * tt:toff + TT * (tt + 1)],
                                           start=(dc == 0), stop=(dc == DC - 1))
                    sT.inc(mm)
                    t_u = sT.n
                    for tt in range(2):
                        kb = 2 * half + tt
                        se.wait_ge(sT.h, t_g)
                        se.wait_ge(sV.h, v_a[kb])
                        sS.inc(se.activation(self.SGB[:, kb, :], PS[:, b0 + tt, :], AF.Silu))
                        s_sg4[kb] = sS.n
                        ve.wait_ge(sS.h, s_sg4[kb])
                        ve.wait_ge(sT.h, t_u)
                        sV.inc(ve.tensor_mul(
                            self.ABF[:, jj, toff + TT * tt:toff + TT * (tt + 1)],
                            self.SGB[:, kb, :], PS[:, b0 + 2 + tt, :]))
                        ve.wait_ge(sV.h, sV.n)
                        sV.inc(ve.tensor_mul(
                            self.ABF[:, jj, toff + TT * tt:toff + TT * (tt + 1)],
                            self.ABF[:, jj, toff + TT * tt:toff + TT * (tt + 1)],
                            self.CBC[:, toff + TT * tt:toff + TT * (tt + 1)]))
                        v_a[kb] = sV.n
            t_w1_end = sT.n
            if e + 1 < E:
                ge.wait_ge(sT.h, t_w1_end)
                self.cW1E.inc(ge.dma_start(out=self.W1EB, in_=mw1_r[e + 1]), 16)

            te.wait_ge(sS.h, max(s_sg4))
            te.wait_ge(sV.h, max(v_a))
            for m in range(DC):
                bk = (m % 2) * 4
                gidx = e * DC + m
                sl = gidx % 3
                te.wait_ge(self.cW2E[sl].h, self.cW2E[sl].n)
                if m >= 2 or e > 0:
                    te.wait_ge(sV.h, v_bev[m % 2])
                mm = None
                for dffc in range(DC):
                    for tt in range(NTT):
                        mm = te.matmul(PS[:, bk + tt, :], self.W2E[:, sl, dffc, :],
                                       self.ABF[:, dffc, TT * tt:TT * (tt + 1)],
                                       start=(dffc == 0), stop=(dffc == DC - 1))
                sT.inc(mm)
                t_w2read[sl] = sT.n
                t_this = sT.n
                ngidx = gidx + 3
                if ngidx < E * DC:
                    ne, nm = divmod(ngidx, DC)
                    ge.wait_ge(sT.h, t_this)
                    w2_dma(ne, nm)
                ve.wait_ge(sT.h, t_this)
                ev = None
                for tt in range(NTT):
                    ve.wait_ge(sV.h, v_y4[m][tt])
                    ev = ve.tensor_add(Y1[:, m, TT * tt:TT * (tt + 1)],
                                       Y1[:, m, TT * tt:TT * (tt + 1)],
                                       PS[:, bk + tt, :])
                    sV.inc(ev)
                    v_y4[m][tt] = sV.n
                v_bev[m % 2] = sV.n
        self.v_y4 = v_y4
        self.v_bev_moe = v_bev

    def lnd_out(self):
        nc = self.nc
        te, ve, se, sy = nc.tensor, nc.vector, nc.scalar, nc.sync
        PS, Y1 = self.PS, self.Y1
        sT, sV, sS, sO = self.sT, self.sV, self.sS, self.sO

        self.preload_set("natural_log_exp_and_others")
        v_y4 = self.v_y4
        v_bev = self.v_bev_moe

        def gate_d(dc, tt):
            g = [(sV.h, v_y4[dc][tt])]
            if dc == 0 and tt == 0:
                g.append((sV.h, v_bev[0]))
                g.append((sV.h, v_bev[1]))
            return g
        self.ln_stats(Y1, gate=gate_d)
        s_rows = self.ln_rows(self.ROW2A, self.ROW2B, self.ROW2T)
        t_bc = self.ln_bcast(self.ROW2T, self.ROW2A, s_rows)
        outr = self.d_out.rearrange("(c p) n -> p c n", p=128)
        for dc in range(DC):
            k = dc % 2
            ve.wait_ge(sT.h, t_bc)
            if dc >= 2:
                ve.wait_ge(self.cOUT[k].h, self.cOUT[k].n)
            sV.inc(ve.tensor_sub(self.OSTG[:, k, :], Y1[:, dc, :], PS[:, 0:4, :]))
            ve.wait_ge(sV.h, sV.n)
            sV.inc(ve.tensor_mul(self.OSTG[:, k, :], self.OSTG[:, k, :], PS[:, 4:8, :]))
            sy.wait_ge(sV.h, sV.n)
            self.cOUT[k].inc(sy.dma_start(out=outr[:, dc, :], in_=self.OSTG[:, k, :]), 16)
        sy.wait_ge(self.cOUT[0].h, self.cOUT[0].n)
        sy.wait_ge(self.cOUT[1].h, self.cOUT[1].n)

_NC_CACHE = {}


def get_nc(stage="full"):
    if stage not in _NC_CACHE:
        _NC_CACHE[stage] = Builder(stage).nc
    return _NC_CACHE[stage]


def make_masks(core):
    t0 = 2048 * (core % 4)
    jj = np.arange(128)[:, None]
    ii = np.arange(256)[None, :]
    bm = np.zeros((6, 128, 256), np.float32)
    for b in range(6):
        j = 128 * b + jj
        bm[b] = ((ii <= j) & (j <= ii + 512)).astype(np.float32)
    em0 = np.zeros_like(bm)
    em1 = np.zeros_like(bm)
    for b in range(6):
        j = 128 * b + jj
        k0 = t0 - 256 + j
        k7 = t0 + 1792 - 256 + j
        em0[b] = bm[b] * ((k0 >= 0) & (k0 < S))
        em1[b] = bm[b] * ((k7 >= 0) & (k7 < S))
    bf = ml_dtypes.bfloat16
    return (bm[[0, 1, 4, 5]].astype(bf), em0.astype(bf), em1.astype(bf))


def make_inputs(core, X):
    b, t0 = core // 4, 2048 * (core % 4)
    src = np.asarray(X["L0_attn_src"][b], np.float32)
    pad = np.zeros((TH, D), np.float32)
    lo, hi = max(0, t0 - HALO), min(S, t0 + T + HALO)
    pad[lo - (t0 - HALO):hi - (t0 - HALO)] = src[lo:hi]
    srcT = np.ascontiguousarray(pad.T)
    bm, em0, em1 = make_masks(core)
    tmask = np.stack([bm[0][:, 0:128], bm[1][:, 128:256],
                      bm[2][:, 0:128], bm[3][:, 128:256]])
    vec6 = np.stack([
        np.asarray(X["L0_attn_b2"], np.float32).reshape(8, 128).T,
        np.asarray(X["L0_attn_ln1_g"], np.float32).reshape(8, 128).T,
        np.asarray(X["L0_attn_ln1_b"], np.float32).reshape(8, 128).T,
        np.asarray(X["L0_attn_ln2_g"], np.float32).reshape(8, 128).T,
        np.asarray(X["L0_attn_ln2_b"], np.float32).reshape(8, 128).T,
    ])
    return {
        "srcT": srcT,
        "w1": np.ascontiguousarray(np.asarray(X["L0_attn_W1"], np.float32)),
        "w2": np.ascontiguousarray(np.asarray(X["L0_attn_W2"], np.float32)),
        "b1T": np.ascontiguousarray(np.asarray(X["L0_attn_b1"], np.float32).reshape(32, 128).T),
        "vec6": np.ascontiguousarray(vec6),
        "bg": np.asarray(X["L0_ffn_bg"], np.float32).reshape(8, 1),
        "wgT": np.ascontiguousarray(np.asarray(X["L0_ffn_Wg"], np.float32).T),
        "mw1": np.ascontiguousarray(np.asarray(X["L0_ffn_w1"], np.float32).transpose(0, 2, 1)),
        "mw2": np.ascontiguousarray(np.asarray(X["L0_ffn_w2"], np.float32).transpose(0, 2, 1)),
        "bmask": np.ascontiguousarray(bm),
        "tmask": np.ascontiguousarray(tmask),
        "em0": np.ascontiguousarray(em0),
        "em1": np.ascontiguousarray(em1),
        "identm": np.eye(128, dtype=ml_dtypes.bfloat16),
        "sel8": np.ascontiguousarray(np.kron(np.eye(8, dtype=np.float32), np.ones((1, 128), np.float32)).astype(ml_dtypes.bfloat16)),
    }


def kernel(**inputs):
    nc = get_nc("full")
    in_maps = [make_inputs(c, inputs) for c in range(NCORES)]
    res = run_bass_kernel_spmd(nc, in_maps, core_ids=list(range(NCORES)))
    out = np.empty((2, S, D), np.float32)
    for c in range(NCORES):
        b, t0 = c // 4, 2048 * (c % 4)
        out[b, t0:t0 + T] = np.asarray(res.results[c]["outT"]).T
    return out
